# version 01
# baseline (speedup 1.0000x reference)
import numpy as np
import ml_dtypes
from contextlib import ExitStack
import concourse.bass as bass
import concourse.mybir as mybir
from concourse.bass_utils import run_bass_kernel_spmd

F32 = mybir.dt.float32
BF16 = mybir.dt.bfloat16
I32 = mybir.dt.int32
AF = mybir.ActivationFunctionType
ALU = mybir.AluOpType
AX = mybir.AxisListType
BF = ml_dtypes.bfloat16

T = 4096
TT = 512
NT = T // TT
NCORE = 8


def _tile4(m):
    K, F = m.shape
    assert K % 128 == 0 and F % 128 == 0, (K, F)
    return np.ascontiguousarray(
        m.reshape(K // 128, 128, F // 128, 128).transpose(2, 0, 1, 3))


def _col128(v):
    return np.ascontiguousarray(v.reshape(-1, 128).T)


def prep_host(inp):
    h = {}
    f32 = np.float32
    h["Wz"] = _tile4(inp["L0_attn_in_proj_w"].T.astype(f32))
    h["Wdtin"] = _tile4(inp["L0_attn_dt_in_proj_w"].T.astype(f32))
    h["Wdtp"] = _tile4(inp["L0_attn_dt_proj_w"].T.astype(f32))
    h["dtb"] = _col128(inp["L0_attn_dt_proj_b"].astype(f32))
    h["convw"] = np.ascontiguousarray(
        inp["L0_attn_conv_w"].astype(f32).reshape(32, 128, 4).transpose(1, 0, 2).reshape(128, 128))
    h["convb"] = _col128(inp["L0_attn_conv_b"].astype(f32))
    h["Alog"] = np.ascontiguousarray(
        inp["L0_attn_A_log"].astype(f32).reshape(32, 128, 16).transpose(1, 0, 2).reshape(128, 512))
    h["Dv"] = _col128(inp["L0_attn_D"].astype(f32))
    h["Wout0"] = _tile4(inp["L0_attn_out_proj_w"].T.astype(f32))
    h["W1"] = _tile4(inp["L0_ffn_w1"].T.astype(f32))
    h["W2"] = _tile4(inp["L0_ffn_w2"].T.astype(f32))
    h["W3"] = _tile4(inp["L0_ffn_w3"].T.astype(f32))
    h["Wqkv"] = _tile4(inp["L1_attn_w_qkv"].T.astype(f32))
    h["Wo1"] = _tile4(inp["L1_attn_w_out"].T.astype(f32))
    h["Wamp"] = _tile4(inp["L1_ffn_amp_w"].astype(f32))
    h["Wph"] = _tile4(inp["L1_ffn_ph_w"].astype(f32))
    h["ampb"] = _col128(inp["L1_ffn_amp_b"].astype(f32))
    h["phb"] = _col128(inp["L1_ffn_ph_b"].astype(f32))
    for l in range(2):
        h[f"Went{l}"] = _tile4(inp["L1_ffn_ent_w"][l].astype(f32))
        h[f"Wentp{l}"] = _tile4(inp["L1_ffn_ent_pw"][l].astype(f32))
        h[f"entb{l}"] = _col128(inp["L1_ffn_ent_b"][l].astype(f32))
        h[f"entpb{l}"] = _col128(inp["L1_ffn_ent_pb"][l].astype(f32))
        for p in range(2):
            h[f"Wpath{l}{p}"] = _tile4(inp["L1_ffn_path_w"][l, p].astype(f32))
        h[f"pathb{l}"] = np.ascontiguousarray(np.concatenate(
            [_col128(inp["L1_ffn_path_b"][l, p].astype(f32)) for p in range(2)], axis=1))
        h[f"phs{l}"] = np.ascontiguousarray(np.concatenate(
            [_col128(inp["L1_ffn_phase"][l, p].astype(f32)) for p in range(2)], axis=1))
        h[f"ln3g{l}"] = _col128(inp["L1_ffn_ln3_g"][l].astype(f32))
        h[f"ln3b{l}"] = _col128(inp["L1_ffn_ln3_b"][l].astype(f32))
        h[f"ln4g{l}"] = _col128(inp["L1_ffn_ln4_g"][l].astype(f32))
        h[f"ln4b{l}"] = _col128(inp["L1_ffn_ln4_b"][l].astype(f32))
    h["identb"] = np.ascontiguousarray(np.eye(128).astype(BF))
    kk = np.arange(1024)[:, None]
    qq = np.arange(512)[None, :]
    h["maskg"] = np.ascontiguousarray(
        np.where((kk >= qq) & (kk <= qq + 512), 0.0, -1e9).astype(BF))
    h["mask0"] = np.ascontiguousarray(
        np.where(kk[:512] <= qq, 0.0, -1e9).astype(BF))
    return h


_S = {}
for _n, _r in [("z_s", 4096), ("xs_s", 1024), ("b_s", 1024), ("c_s", 4096),
               ("dt_s", 4096), ("xact_s", 4096), ("yssm_s", 4096), ("yg_s", 4096),
               ("att0_s", 2048), ("x1_s", 2048), ("hid_s", 5632), ("ff0_s", 2048),
               ("x2_s", 2048), ("qkv_s", 6144), ("ao_s", 2048), ("att1_s", 2048),
               ("x3_s", 2048), ("amp_s", 2048), ("h0_s", 2048),
               ("e0_s", 8192), ("e1_s", 8192), ("s30_s", 2048), ("s31_s", 2048),
               ("hA0_s", 2048), ("hA1_s", 2048), ("s40_s", 2048), ("s41_s", 2048),
               ("hB0_s", 2048), ("hB1_s", 2048)]:
    _S[_n] = (_r, T, BF16, BF)
_S["w3q_s"] = (2048, 5632, BF16, BF)
_S["outT"] = (2048, T, F32, np.float32)
SCRATCH = _S



class Ctx:
    def __init__(self, nc, es, ext_in, ext_out, host_specs):
        self.nc = nc
        self.es = es
        self.blk = None
        self.ext_in = ext_in
        self.ext_out = ext_out
        self.host_specs = host_specs
        self.sem = {}
        self.c = {}
        for s in ["q0", "q2", "pe", "act", "dve", "pool", "pf", "wf", "st", "x0", "x1"]:
            self.sem[s] = es.enter_context(nc.semaphore(s))
            self.c[s] = 0
        self.dram = {}
        self.streams = {}
        self._uid = 0
        self.cc = None
        self.ps = None
        self.keep = {}

    def uid(self):
        self._uid += 1
        return self._uid

    def ap(self, name):
        if name in self.dram:
            return self.dram[name]
        if name in SCRATCH:
            r, c, dt, _ = SCRATCH[name]
            kind = ("ExternalInput" if name in self.ext_in else
                    "ExternalOutput" if name in self.ext_out else "Internal")
            t = self.nc.dram_tensor(name, [r, c], dt, kind=kind)
        else:
            shape, dt = self.host_specs[name]
            t = self.nc.dram_tensor(name, list(shape), dt, kind="ExternalInput")
        self.dram[name] = t.ap()
        return self.dram[name]

    def sb(self, es, tag, p, f, dt, side="left"):
        return es.enter_context(
            self.nc.sbuf_tensor(f"{tag}{self.uid()}", [p, f], dt, side=side))

    def psb(self, tag, p, f, dt):
        if tag not in self.keep:
            self.keep[tag] = self.sb(self.es, tag, p, f, dt, side="right")
        return self.keep[tag]

    def inc(self, ins, s, n=1):
        ins.then_inc(self.sem[s], n)
        self.c[s] += n
        return self.c[s]

    def stream(self, name, nslots):
        if name not in self.streams:
            self.streams[name] = Stream(self, name, nslots)
        st = self.streams[name]
        assert st.n == nslots, (name, nslots, st.n)
        return st


class Stream:

    def __init__(self, ctx, name, nslots):
        self.n = nslots
        self.sems = [ctx.es.enter_context(ctx.nc.semaphore(f"{name}{i}"))
                     for i in range(nslots)]
        self.cnt = [0] * nslots

    def put(self, e, out, in_, slot):
        d = e.dma_start(out=out, in_=in_)
        d.then_inc(self.sems[slot], 16)
        self.cnt[slot] += 16
        return (slot, self.cnt[slot])

    def wait(self, e, tok):
        e.wait_ge(self.sems[tok[0]], tok[1])

    def base(self):
        return list(self.cnt)


CONSTS = [0.0, 1.0, 0.5, -0.5, np.pi / 2, 1e-5, -1e9, 1.0 / 2048.0, np.pi,
          1.0 / (2 * np.pi), 2 * np.pi, 0.088388347648318447, 1e-8,
          1.0 / 5632.0]


def ccol(ctx, v):
    return ctx.cc[:, CONSTS.index(v):CONSTS.index(v) + 1]


def crow(ctx, v):
    return ctx.cc[0:1, CONSTS.index(v):CONSTS.index(v) + 1]


def slab_loader(ctx, wname, nk, nm, wbuf, lookahead=2, ntt=NT):
    W = ctx.ap(wname)
    wt = ctx.stream("wt", 2)
    bw = wt.base()
    base_pe = ctx.c["pe"]
    blk = ctx.blk

    @blk.gpsimd
    def _(e):
        for m in range(nm):
            if m >= lookahead:
                e.wait_ge(ctx.sem["pe"], base_pe + (m - lookahead + 1) * ntt)
            wt.put(e, wbuf[:, (m % 2) * nk * 128:(m % 2 + 1) * nk * 128]
                   .rearrange("p (k c) -> p k c", k=nk),
                   W[m].rearrange("k p c -> p k c"), m % 2)
    return lambda m: (m % 2, bw[m % 2] + 16 * (m // 2 + 1))


def load_cols(ctx, e, name, dst, q="q0"):
    d = e.dma_start(out=dst[:], in_=ctx.ap(name)[:, :])
    return ctx.inc(d, q, 16)


def mm_section(ctx, nk, nm, wbuf, rhs, w_ready, rhs_wait, bank_free, nbank=4,
               lookahead=2, rhs_stride=T, ntt=NT, c0=0):
    blk = ctx.blk
    base_pe = ctx.c["pe"]

    @blk.tensor
    def _(e):
        if rhs_wait is not None:
            e.wait_ge(ctx.sem[rhs_wait[0]], rhs_wait[1])
        for m in range(nm):
            tok = w_ready(m)
            e.wait_ge(ctx.streams["wt"].sems[tok[0]], tok[1])
            for t in range(ntt):
                j = m * ntt + t
                if j >= nbank:
                    sn, cv = bank_free(j - nbank)
                    e.wait_ge(ctx.sem[sn], cv)
                ps = ctx.ps[j % nbank]
                for k in range(nk):
                    ins = e.matmul(
                        out=ps[:],
                        lhsT=wbuf[:, (m % lookahead) * nk * 128 + k * 128:
                                  (m % lookahead) * nk * 128 + (k + 1) * 128],
                        rhs=rhs[:, k * rhs_stride + c0 + t * TT:
                                k * rhs_stride + c0 + t * TT + TT],
                        start=(k == 0), stop=(k == nk - 1))
                ctx.inc(ins, "pe")
    return base_pe


def store_section(ctx, njobs, ready_sem, ready_of, src_of, dst_of, nslots=4,
                  eng=None):
    blk = ctx.blk
    st = ctx.stream("st", 4)
    dec = getattr(blk, eng or "sync")

    @dec
    def _(e):
        for j in range(njobs):
            e.wait_ge(ctx.sem[ready_sem], ready_of(j))
            st.put(e, dst_of(j), src_of(j), j % nslots)
        for sl in range(nslots):
            e.wait_ge(st.sems[sl], st.cnt[sl])


def st_tok(ctx, base, j, nslots=4):
    return (j % nslots, base[j % nslots] + 16 * (j // nslots + 1))



def p_init(ctx):
    with ctx.nc.Block("init") as blk:
        ctx.blk = blk

        @blk.vector
        def _(e):
            for i, v in enumerate(CONSTS):
                e.memset(ctx.cc[:, i:i + 1], float(v))


def generic_linear(ctx, name, wname, nk, nm, rhs, rhs_wait, dst_of, epi=None,
                   nbank=4, ntt=NT, rhs_stride=T, c0=0, epi_eng="vector"):
    with ctx.nc.Block(name) as blk:
        ctx.blk = blk
        with ExitStack() as les:
            wbuf = ctx.sb(les, "wb", 128, 2 * nk * 128, BF16)
            ob = ctx.sb(les, "ob", 128, 4 * TT, BF16)
            w_ready = slab_loader(ctx, wname, nk, nm, wbuf, ntt=ntt)
            base_dve = ctx.c["dve"]
            base_pe = mm_section(ctx, nk, nm, wbuf, rhs, w_ready, rhs_wait,
                                 lambda j: ("dve", base_dve + j + 1), nbank=nbank,
                                 rhs_stride=rhs_stride, ntt=ntt, c0=c0)
            base_st = ctx.stream("st", 4).base()
            dec = getattr(blk, epi_eng)

            @dec
            def _(e):
                for j in range(nm * ntt):
                    e.wait_ge(ctx.sem["pe"], base_pe + j + 1)
                    if j >= 4:
                        ctx.streams["st"].wait(e, st_tok(ctx, base_st, j - 4))
                    buf = ob[:, (j % 4) * TT:(j % 4 + 1) * TT]
                    if epi is None:
                        ins = e.tensor_copy(out=buf, in_=ctx.ps[j % nbank][:])
                    else:
                        ins = epi(e, j // ntt, j % ntt, ctx.ps[j % nbank], buf)
                    ctx.inc(ins, "dve")

            store_section(ctx, nm * ntt, "dve", lambda j: base_dve + j + 1,
                          lambda j: ob[:, (j % 4) * TT:(j % 4 + 1) * TT],
                          lambda j: dst_of(j // ntt, j % ntt))


def rhs_load_block(ctx, name, src, nk, rhs, cast=False, c0=0, ncols=T):
    with ctx.nc.Block(name) as blk:
        ctx.blk = blk
        src_ap = ctx.ap(src)
        qn = "q2" if cast else "q0"
        eng = blk.gpsimd if cast else blk.sync

        @eng
        def _(e):
            for k in range(nk):
                d = e.dma_start(out=rhs[:, k * ncols:(k + 1) * ncols],
                                in_=src_ap[128 * k:128 * (k + 1), c0:c0 + ncols])
                ctx.inc(d, qn, 16)
    return (qn, ctx.c[qn])


def p_l0in(ctx):
    with ExitStack() as les:
        rhs = ctx.sb(les, "xres", 128, 16 * T, BF16)
        rw = rhs_load_block(ctx, "ldx", "xT", 16, rhs, cast=True)

        dsts = []
        for m in range(80):
            if m < 32:
                dsts.append(("z_s", m))
            elif m < 40:
                dsts.append(("xs_s", m - 32))
            elif m < 48:
                dsts.append(("b_s", m - 40))
            else:
                dsts.append(("c_s", m - 48))

        def dst_of(m, t):
            nm_, mm = dsts[m]
            return ctx.ap(nm_)[128 * mm:128 * mm + 128, t * TT:(t + 1) * TT]

        generic_linear(ctx, "inproj", "Wz", 16, 80, rhs, rw, dst_of)

        dtm = ctx.psb("dtmid", 128, T, BF16)
        with ctx.nc.Block("dtmid") as blk:
            ctx.blk = blk
            with ExitStack() as les2:
                wbuf = ctx.sb(les2, "wb", 128, 2 * 16 * 128, BF16)
                w_ready = slab_loader(ctx, "Wdtin", 16, 1, wbuf)
                base_dve = ctx.c["dve"]
                base_pe = mm_section(ctx, 16, 1, wbuf, rhs, w_ready, rw,
                                     lambda j: ("dve", base_dve + j + 1))

                @blk.vector
                def _(e):
                    for j in range(NT):
                        e.wait_ge(ctx.sem["pe"], base_pe + j + 1)
                        ins = e.tensor_copy(out=dtm[:, j * TT:(j + 1) * TT],
                                            in_=ctx.ps[j % 4][:])
                        ctx.inc(ins, "dve")

    dtm = ctx.keep["dtmid"]
    with ctx.nc.Block("dt") as blk:
        ctx.blk = blk
        with ExitStack() as les:
            wbuf = ctx.sb(les, "wb", 128, 2 * 128, BF16)
            bia = ctx.sb(les, "bia", 128, 32, F32)
            t1 = ctx.sb(les, "t1", 128, 4 * TT, F32)
            t2 = ctx.sb(les, "t2", 128, 4 * TT, F32)
            ob = ctx.sb(les, "ob", 128, 4 * TT, BF16)

            @blk.sync
            def _(e):
                load_cols(ctx, e, "dtb", bia)
            bia_n = ctx.c["q0"]

            w_ready = slab_loader(ctx, "Wdtp", 1, 32, wbuf)
            base_act = ctx.c["act"]
            base_pe = mm_section(ctx, 1, 32, wbuf, dtm, w_ready, ("q0", bia_n),
                                 lambda j: ("act", base_act + 4 * (j + 1)))
            base_dve = ctx.c["dve"]
            base_st = ctx.stream("st", 4).base()

            @blk.scalar
            def _(e):
                e.wait_ge(ctx.sem["q0"], bia_n)
                for j in range(32 * NT):
                    m = j // NT
                    e.wait_ge(ctx.sem["pe"], base_pe + j + 1)
                    if j >= 4:
                        e.wait_ge(ctx.sem["dve"], base_dve + j - 3)
                    ps = ctx.ps[j % 4]
                    ta = t1[:, (j % 4) * TT:(j % 4 + 1) * TT]
                    tb = t2[:, (j % 4) * TT:(j % 4 + 1) * TT]
                    b = bia[:, m:m + 1]
                    ins = e.activation(out=ta, in_=ps[:], func=AF.Abs, bias=b,
                                       scale=ccol(ctx, 1.0))
                    e.wait_ge(ctx.sem["act"], ctx.inc(ins, "act"))
                    ins = e.activation(out=ta, in_=ta, func=AF.Exp, scale=-1.0)
                    e.wait_ge(ctx.sem["act"], ctx.inc(ins, "act"))
                    ins = e.activation(out=ta, in_=ta, func=AF.Ln, bias=ccol(ctx, 1.0),
                                       scale=ccol(ctx, 1.0))
                    ctx.inc(ins, "act")
                    ins = e.activation(out=tb, in_=ps[:], func=AF.Relu, bias=b,
                                       scale=ccol(ctx, 1.0))
                    ctx.inc(ins, "act")

            @blk.vector
            def _(e):
                for j in range(32 * NT):
                    e.wait_ge(ctx.sem["act"], base_act + 4 * (j + 1))
                    if j >= 4:
                        ctx.streams["st"].wait(e, st_tok(ctx, base_st, j - 4))
                    o = ob[:, (j % 4) * TT:(j % 4 + 1) * TT]
                    ins = e.tensor_add(out=o, in0=t1[:, (j % 4) * TT:(j % 4 + 1) * TT],
                                       in1=t2[:, (j % 4) * TT:(j % 4 + 1) * TT])
                    ctx.inc(ins, "dve")

            store_section(ctx, 32 * NT, "dve", lambda j: base_dve + j + 1,
                          lambda j: ob[:, (j % 4) * TT:(j % 4 + 1) * TT],
                          lambda j: ctx.ap("dt_s")[128 * (j // NT):128 * (j // NT) + 128,
                                                   (j % NT) * TT:(j % NT + 1) * TT])


def p_conv(ctx):
    with ctx.nc.Block("conv") as blk:
        ctx.blk = blk
        with ExitStack() as les:
            xf = ctx.sb(les, "xf", 128, 2 * (T + 3), BF16)
            acc = ctx.sb(les, "acc", 128, 2 * T, F32)
            ob = ctx.sb(les, "ob", 128, 2 * T, BF16)
            wcv = ctx.sb(les, "wcv", 128, 128, F32)
            bcv = ctx.sb(les, "bcv", 128, 32, F32)
            xs = ctx.ap("xs_s")
            ld = ctx.stream("ld", 4)
            base_ld = ld.base()
            base_q0 = ctx.c["q0"]
            base_dve = ctx.c["dve"]
            base_act = ctx.c["act"]
            base_st = ctx.stream("st", 4).base()

            @blk.sync
            def _(e):
                load_cols(ctx, e, "convw", wcv)
                load_cols(ctx, e, "convb", bcv)
                for k in range(32):
                    if k >= 2:
                        e.wait_ge(ctx.sem["dve"], base_dve + 1 + 4 * (k - 1))
                    for g in range(2):
                        src = (xs[32 * k + 16 * g:32 * k + 16 * g + 16, :]
                               .unsqueeze(0).broadcast_to([4, 16, T]))
                        ld.put(e, xf[64 * g:64 * (g + 1),
                                     (k % 2) * (T + 3) + 3:(k % 2 + 1) * (T + 3)],
                               src, k % 2)

            @blk.vector
            def _(e):
                e.memset(xf[:, 0:3], 0.0)
                ins = e.memset(xf[:, T + 3:T + 6], 0.0)
                e.wait_ge(ctx.sem["dve"], ctx.inc(ins, "dve"))
                base_dve2 = ctx.c["dve"]
                for k in range(32):
                    e.wait_ge(ctx.sem["q0"], base_q0 + 32)
                    e.wait_ge(ld.sems[k % 2], base_ld[k % 2] + 32 * (k // 2 + 1))
                    if k >= 2:
                        e.wait_ge(ctx.sem["act"], base_act + k - 1)
                    x0 = (k % 2) * (T + 3)
                    a = acc[:, (k % 2) * T:(k % 2 + 1) * T]
                    ins = e.tensor_scalar(out=a, in0=xf[:, x0:x0 + T],
                                          scalar1=wcv[:, 4 * k:4 * k + 1],
                                          scalar2=None, op0=ALU.mult)
                    e.wait_ge(ctx.sem["dve"], ctx.inc(ins, "dve"))
                    for j in range(1, 4):
                        ins = e.scalar_tensor_tensor(
                            out=a, in0=xf[:, x0 + j:x0 + j + T],
                            scalar=wcv[:, 4 * k + j:4 * k + j + 1], in1=a,
                            op0=ALU.mult, op1=ALU.add)
                        n = ctx.inc(ins, "dve")
                        if j < 3:
                            e.wait_ge(ctx.sem["dve"], n)

            stq = ctx.stream("st", 4)

            @blk.scalar
            def _(e):
                for k in range(32):
                    e.wait_ge(ctx.sem["dve"], base_dve + 1 + 4 * (k + 1))
                    if k >= 2:
                        stq.wait(e, st_tok(ctx, base_st, k - 2, 2))
                    ins = e.activation(out=ob[:, (k % 2) * T:(k % 2 + 1) * T],
                                       in_=acc[:, (k % 2) * T:(k % 2 + 1) * T],
                                       func=AF.Silu, bias=bcv[:, k:k + 1],
                                       scale=ccol(ctx, 1.0))
                    n = ctx.inc(ins, "act")
                    e.wait_ge(ctx.sem["act"], n)
                    stq.put(e, ctx.ap("xact_s")[128 * k:128 * (k + 1), :],
                            ob[:, (k % 2) * T:(k % 2 + 1) * T], k % 2)
                for sl in range(2):
                    e.wait_ge(stq.sems[sl], stq.cnt[sl])


def _yacc(ctx, e, i, pr, yk, st2, y_tok, base):
    k, n = i // 16, i % 16
    x0 = (i % 2) * T
    e.wait_ge(ctx.sem["pf"], base["pf"] + i + 1)
    ykv = yk[:, (k % 2) * T:(k % 2 + 1) * T]
    if n == 0:
        if k >= 2:
            st2.wait(e, y_tok(k - 2))
        ins = e.tensor_copy(out=ykv, in_=pr[:, x0:x0 + T])
    else:
        e.wait_ge(ctx.sem["x0"], base["x0"] + i)
        ins = e.tensor_tensor(out=ykv, in0=ykv, in1=pr[:, x0:x0 + T], op=ALU.add)
    nx = ctx.inc(ins, "x0")
    if n == 15:
        e.wait_ge(ctx.sem["x0"], nx)
        st2.put(e, ctx.ap("yssm_s")[128 * k:128 * (k + 1), :], ykv, k % 2)


def p_scan(ctx):
    with ctx.nc.Block("scan") as blk:
        ctx.blk = blk
        with ExitStack() as les:
            dtk = ctx.sb(les, "dtk", 128, 2 * T, BF16)
            xak = ctx.sb(les, "xak", 128, 2 * T, BF16)
            dxk = ctx.sb(les, "dxk", 128, 2 * T, BF16)
            bnb = ctx.sb(les, "bnb", 128, 2 * T, BF16)
            cnb = ctx.sb(les, "cnb", 128, 2 * T, BF16)
            ab = ctx.sb(les, "ab", 128, 2 * T, F32)
            ub = ctx.sb(les, "ub", 128, 2 * T, BF16)
            hb = ctx.sb(les, "hb", 128, 2 * T, BF16)
            pr = ctx.sb(les, "pr", 128, 2 * T, BF16)
            yk = ctx.sb(les, "yk", 128, 2 * T, F32)
            asb = ctx.sb(les, "asb", 128, 512, F32)
            bs = ctx.ap("b_s")
            cs = ctx.ap("c_s")

            @blk.sync
            def _(e):
                d = e.dma_start(out=asb[:], in_=ctx.ap("Alog")[:, :])
                ctx.inc(d, "q0", 16)
            alog_n = ctx.c["q0"]

            @blk.scalar
            def _(e):
                e.wait_ge(ctx.sem["q0"], alog_n)
                ins = e.activation(out=asb[:], in_=asb[:], func=AF.Exp)
                ctx.inc(ins, "act")
            aexp_n = ctx.c["act"]

            @blk.vector
            def _(e):
                e.wait_ge(ctx.sem["act"], aexp_n)
                ins = e.tensor_scalar_mul(out=asb[:], in0=asb[:], scalar1=-1.0)
                ctx.inc(ins, "dve")

            base = {sn: ctx.c[sn] for sn in
                    ["q0", "q2", "act", "dve", "pool", "pf", "wf", "x0", "x1"]}
            sd = ctx.stream("d", 2)
            sb_ = ctx.stream("b", 2)
            scb = ctx.stream("cb", 2)
            st2 = ctx.stream("st2", 2)
            bd, bb, bcb, bst2 = sd.base(), sb_.base(), scb.base(), st2.base()
            d_tok = lambda k: (k % 2, bd[k % 2] + 32 * (k // 2 + 1))
            b_tok = lambda i: (i % 2, bb[i % 2] + 32 * (i // 2 + 1))
            c_tok = lambda i: (i % 2, bcb[i % 2] + 16 * (i // 2 + 1))
            y_tok = lambda k: (k % 2, bst2[k % 2] + 16 * (k // 2 + 1))

            @blk.sync
            def _(e):
                for k in range(32):
                    if k >= 2:
                        e.wait_ge(ctx.sem["act"], base["act"] + (k - 1) * 16)
                        e.wait_ge(ctx.sem["x1"], base["x1"] + k - 1)
                    sd.put(e, dtk[:, (k % 2) * T:(k % 2 + 1) * T],
                           ctx.ap("dt_s")[128 * k:128 * (k + 1), :], k % 2)
                    sd.put(e, xak[:, (k % 2) * T:(k % 2 + 1) * T],
                           ctx.ap("xact_s")[128 * k:128 * (k + 1), :], k % 2)
                    for n in range(16):
                        i = 16 * k + n
                        x0 = (i % 2) * T
                        if i >= 2:
                            e.wait_ge(ctx.sem["pool"], base["pool"] + i - 1)
                        r0 = 32 * k + n
                        sb_.put(e, bnb[0:64, x0:x0 + T],
                                bs[r0:r0 + 1, :].partition_broadcast(64), i % 2)
                        sb_.put(e, bnb[64:128, x0:x0 + T],
                                bs[r0 + 16:r0 + 17, :].partition_broadcast(64), i % 2)

            @blk.scalar
            def _(e):
                for k in range(32):
                    for n in range(16):
                        i = 16 * k + n
                        x0 = (i % 2) * T
                        if i >= 2:
                            e.wait_ge(ctx.sem["pf"], base["pf"] + i - 1)
                        src = (cs[128 * k:128 * (k + 1), :]
                               .rearrange("(g s) n -> g s n", s=16)[:, n, :]
                               .unsqueeze(1).broadcast_to([8, 16, T]))
                        scb.put(e, cnb[:, x0:x0 + T], src, i % 2)
                        if n == 0:
                            sd.wait(e, d_tok(k))
                        e.wait_ge(ctx.sem["dve"],
                                  base["dve"] + (i - 1 if i >= 2 else 0))
                        ins = e.activation(out=ab[:, x0:x0 + T],
                                           in_=dtk[:, (k % 2) * T:(k % 2 + 1) * T],
                                           func=AF.Exp, scale=asb[:, i:i + 1])
                        ctx.inc(ins, "act")

            @blk.gpsimd
            def _(e):
                for k in range(32):
                    sd.wait(e, d_tok(k))
                    ins = e.tensor_tensor(out=dxk[:, (k % 2) * T:(k % 2 + 1) * T],
                                          in0=dtk[:, (k % 2) * T:(k % 2 + 1) * T],
                                          in1=xak[:, (k % 2) * T:(k % 2 + 1) * T],
                                          op=ALU.mult)
                    ctx.inc(ins, "x1")
                    e.wait_ge(ctx.sem["x1"], base["x1"] + k + 1)
                    for n in range(16):
                        i = 16 * k + n
                        x0 = (i % 2) * T
                        sb_.wait(e, b_tok(i))
                        if i >= 2:
                            e.wait_ge(ctx.sem["dve"], base["dve"] + i - 1)
                        ins = e.tensor_tensor(out=ub[:, x0:x0 + T],
                                              in0=dxk[:, (k % 2) * T:(k % 2 + 1) * T],
                                              in1=bnb[:, x0:x0 + T], op=ALU.mult)
                        ctx.inc(ins, "pool")
                        if i >= 1:
                            _yacc(ctx, e, i - 1, pr, yk, st2, y_tok, base)
                _yacc(ctx, e, 511, pr, yk, st2, y_tok, base)
                for sl in range(2):
                    e.wait_ge(st2.sems[sl], st2.cnt[sl])

            @blk.vector
            def _(e):
                for k in range(32):
                    for n in range(16):
                        i = 16 * k + n
                        x0 = (i % 2) * T
                        e.wait_ge(ctx.sem["act"], base["act"] + i + 1)
                        e.wait_ge(ctx.sem["pool"], base["pool"] + i + 1)
                        ins = e.tensor_tensor_scan(out=hb[:, x0:x0 + T],
                                                   data0=ab[:, x0:x0 + T],
                                                   data1=ub[:, x0:x0 + T],
                                                   initial=0.0, op0=ALU.mult,
                                                   op1=ALU.add)
                        n_scan = ctx.inc(ins, "dve")
                        scb.wait(e, c_tok(i))
                        if i >= 2:
                            e.wait_ge(ctx.sem["x0"], base["x0"] + i - 1)
                        e.wait_ge(ctx.sem["dve"], n_scan)
                        ins = e.tensor_tensor(out=pr[:, x0:x0 + T], in0=hb[:, x0:x0 + T],
                                              in1=cnb[:, x0:x0 + T], op=ALU.mult)
                        ctx.inc(ins, "pf")


def p_gate(ctx):
    with ctx.nc.Block("gate") as blk:
        ctx.blk = blk
        with ExitStack() as les:
            ytk = ctx.sb(les, "ytk", 128, 2 * T, BF16)
            xak = ctx.sb(les, "xak", 128, 2 * T, BF16)
            zk = ctx.sb(les, "zk", 128, 2 * T, BF16)
            szk = ctx.sb(les, "szk", 128, 2 * T, BF16)
            t1 = ctx.sb(les, "t1", 128, 2 * T, F32)
            ob = ctx.sb(les, "ob", 128, 2 * T, BF16)
            dvc = ctx.sb(les, "dvc", 128, 32, F32)
            ld = ctx.stream("ld", 4)
            base_ld = ld.base()
            ld_tok = lambda k: (k % 2, base_ld[k % 2] + 48 * (k // 2 + 1))
            base_q0 = ctx.c["q0"]
            base_act = ctx.c["act"]
            base_dve = ctx.c["dve"]
            base_pool = ctx.c["pool"]
            base_st = ctx.stream("st", 4).base()

            @blk.sync
            def _(e):
                load_cols(ctx, e, "Dv", dvc)
                for k in range(32):
                    if k >= 2:
                        e.wait_ge(ctx.sem["pool"], base_pool + k - 1)
                    for nm_, dst in [("yssm_s", ytk), ("xact_s", xak), ("z_s", zk)]:
                        ld.put(e, dst[:, (k % 2) * T:(k % 2 + 1) * T],
                               ctx.ap(nm_)[128 * k:128 * (k + 1), :], k % 2)

            @blk.scalar
            def _(e):
                e.wait_ge(ctx.sem["q0"], base_q0 + 16)
                for k in range(32):
                    ld.wait(e, ld_tok(k))
                    if k >= 2:
                        e.wait_ge(ctx.sem["pool"], base_pool + k - 1)
                    ins = e.activation(out=szk[:, (k % 2) * T:(k % 2 + 1) * T],
                                       in_=zk[:, (k % 2) * T:(k % 2 + 1) * T],
                                       func=AF.Silu)
                    ctx.inc(ins, "act")

            @blk.vector
            def _(e):
                e.wait_ge(ctx.sem["q0"], base_q0 + 16)
                for k in range(32):
                    ld.wait(e, ld_tok(k))
                    if k >= 2:
                        e.wait_ge(ctx.sem["pool"], base_pool + k - 1)
                    ins = e.scalar_tensor_tensor(
                        out=t1[:, (k % 2) * T:(k % 2 + 1) * T],
                        in0=xak[:, (k % 2) * T:(k % 2 + 1) * T],
                        scalar=dvc[:, k:k + 1],
                        in1=ytk[:, (k % 2) * T:(k % 2 + 1) * T],
                        op0=ALU.mult, op1=ALU.add)
                    ctx.inc(ins, "dve")

            stq = ctx.stream("st2", 2)
            base_st2 = stq.base()

            @blk.gpsimd
            def _(e):
                for k in range(32):
                    e.wait_ge(ctx.sem["act"], base_act + k + 1)
                    e.wait_ge(ctx.sem["dve"], base_dve + k + 1)
                    if k >= 2:
                        stq.wait(e, st_tok(ctx, base_st2, k - 2, 2))
                    ins = e.tensor_tensor(out=ob[:, (k % 2) * T:(k % 2 + 1) * T],
                                          in0=t1[:, (k % 2) * T:(k % 2 + 1) * T],
                                          in1=szk[:, (k % 2) * T:(k % 2 + 1) * T],
                                          op=ALU.mult)
                    n = ctx.inc(ins, "pool")
                    e.wait_ge(ctx.sem["pool"], n)
                    stq.put(e, ctx.ap("yg_s")[128 * k:128 * (k + 1), :],
                            ob[:, (k % 2) * T:(k % 2 + 1) * T], k % 2)
                for sl in range(2):
                    e.wait_ge(stq.sems[sl], stq.cnt[sl])


def p_outproj(ctx):
    with ExitStack() as les:
        rhs = ctx.sb(les, "ygres", 128, 32 * 2048, BF16)
        for half in range(2):
            rw = rhs_load_block(ctx, f"ldyg{half}", "yg_s", 32, rhs,
                                c0=half * 2048, ncols=2048)

            def dst_of(m, t, _h=half):
                return ctx.ap("att0_s")[128 * m:128 * m + 128,
                                        _h * 2048 + t * TT:_h * 2048 + (t + 1) * TT]

            generic_linear(ctx, f"outproj{half}", "Wout0", 32, 16, rhs, rw,
                           dst_of, ntt=4, rhs_stride=2048)


def p_ln(ctx, name, srcA, srcB, dst, srcA_f32=False, gname=None, bname=None,
         out_f32=False):
    with ExitStack() as ges:
        gcols = ctx.sb(ges, "gc", 128, 32, F32) if gname else None
        if gname:
            with ctx.nc.Block(name + "g") as blk:
                ctx.blk = blk

                @blk.sync
                def _(e):
                    load_cols(ctx, e, gname, gcols[:, 0:16])
                    load_cols(ctx, e, bname, gcols[:, 16:32])
                    e.wait_ge(ctx.sem["q0"], ctx.c["q0"])
        for t in range(NT):
            _ln_tile(ctx, f"{name}t{t}", srcA, srcB, dst, t, srcA_f32, gcols,
                     out_f32)


def _ln_tile(ctx, name, srcA, srcB, dst, t, srcA_f32, gcols, out_f32):
    nk = 16
    with ctx.nc.Block(name) as blk:
        ctx.blk = blk
        with ExitStack() as les:
            at = ctx.sb(les, "at", 128, nk * TT, F32 if srcA_f32 else BF16)
            bt = ctx.sb(les, "bt", 128, nk * TT, BF16)
            s1 = ctx.sb(les, "s1", 128, nk * TT, BF16)
            sq = ctx.sb(les, "sq", 128, nk * TT, BF16)
            rows = ctx.sb(les, "rw", 1, 5 * TT, F32)
            onec = ctx.sb(les, "on", 128, 1, BF16)
            onef = ctx.sb(les, "onf", 1, 128, F32)
            rstdb = ctx.sb(les, "rsb", 128, TT, F32)
            mbb = ctx.sb(les, "mbb", 128, TT, F32)
            tm = ctx.sb(les, "tm", 128, 2 * TT, F32)
            obx = ctx.sb(les, "obx", 128, 2 * TT, F32 if out_f32 else BF16)
            apA, apB = ctx.ap(srcA), ctx.ap(srcB)
            base = {sn: ctx.c[sn] for sn in
                    ["q0", "pe", "act", "dve", "pool", "pf", "x0", "x1"]}
            base["st"] = ctx.stream("st", 4).base()

            @blk.sync
            def _(e):
                for k in range(nk):
                    d = e.dma_start(out=at[:, k * TT:(k + 1) * TT],
                                    in_=apA[128 * k:128 * (k + 1), t * TT:(t + 1) * TT])
                    ctx.inc(d, "q0", 16)
                    d = e.dma_start(out=bt[:, k * TT:(k + 1) * TT],
                                    in_=apB[128 * k:128 * (k + 1), t * TT:(t + 1) * TT])
                    ctx.inc(d, "q0", 16)

            @blk.gpsimd
            def _(e):
                e.wait_ge(ctx.sem["q0"], base["q0"] + 32 * nk)
                for k in range(nk):
                    ins = e.tensor_tensor(out=s1[:, k * TT:(k + 1) * TT],
                                          in0=at[:, k * TT:(k + 1) * TT],
                                          in1=bt[:, k * TT:(k + 1) * TT], op=ALU.add)
                    ctx.inc(ins, "pool")
                for k in range(nk):
                    e.wait_ge(ctx.sem["dve"], base["dve"] + 7 + k + 1)
                    if k >= 2:
                        ctx.streams["st"].wait(e, st_tok(ctx, base["st"], k - 2, 2))
                    ov = obx[:, (k % 2) * TT:(k % 2 + 1) * TT]
                    ins = e.tensor_tensor(out=ov,
                                          in0=tm[:, (k % 2) * TT:(k % 2 + 1) * TT],
                                          in1=mbb[:], op=ALU.subtract)
                    if gcols is not None:
                        e.wait_ge(ctx.sem["x1"], ctx.inc(ins, "x1"))
                        ins = e.tensor_scalar(out=ov, in0=ov, scalar1=gcols[:, k:k + 1],
                                              scalar2=gcols[:, 16 + k:17 + k],
                                              op0=ALU.mult, op1=ALU.add)
                    ctx.inc(ins, "pool")

            @blk.scalar
            def _(e):
                for k in range(nk):
                    e.wait_ge(ctx.sem["pool"], base["pool"] + k + 1)
                    ins = e.activation(out=sq[:, k * TT:(k + 1) * TT],
                                       in_=s1[:, k * TT:(k + 1) * TT], func=AF.Square)
                    ctx.inc(ins, "act")
                e.wait_ge(ctx.sem["dve"], base["dve"] + 4)
                var = rows[0:1, 2 * TT:3 * TT]
                rstd = rows[0:1, 3 * TT:4 * TT]
                ins = e.activation(out=rstd, in_=var, func=AF.Ln,
                                   bias=crow(ctx, 1e-5), scale=crow(ctx, 1.0))
                e.wait_ge(ctx.sem["act"], ctx.inc(ins, "act"))
                ins = e.activation(out=rstd, in_=rstd, func=AF.Exp, scale=-0.5)
                ctx.inc(ins, "x0")

            @blk.vector
            def _(e):
                e.memset(onec[:], 1.0)
                ins = e.memset(onef[:], 1.0)
                ctx.inc(ins, "pf")
                e.wait_ge(ctx.sem["pe"], base["pe"] + nk)
                mu = rows[0:1, 0:TT]
                ms = rows[0:1, TT:2 * TT]
                var = rows[0:1, 2 * TT:3 * TT]
                rstd = rows[0:1, 3 * TT:4 * TT]
                mb = rows[0:1, 4 * TT:5 * TT]
                ins = e.tensor_scalar_mul(out=mu, in0=ctx.ps[0][0:1, :], scalar1=1.0 / 2048.0)
                e.wait_ge(ctx.sem["dve"], ctx.inc(ins, "dve"))
                ins = e.tensor_scalar_mul(out=ms, in0=ctx.ps[1][0:1, :], scalar1=1.0 / 2048.0)
                e.wait_ge(ctx.sem["dve"], ctx.inc(ins, "dve"))
                ins = e.tensor_tensor(out=var, in0=mu, in1=mu, op=ALU.mult)
                e.wait_ge(ctx.sem["dve"], ctx.inc(ins, "dve"))
                ins = e.tensor_sub(out=var, in0=ms, in1=var)
                ctx.inc(ins, "dve")
                e.wait_ge(ctx.sem["x0"], base["x0"] + 1)
                ins = e.tensor_tensor(out=mb, in0=mu, in1=rstd, op=ALU.mult)
                ctx.inc(ins, "dve")
                e.wait_ge(ctx.sem["pe"], base["pe"] + nk + 2)
                ins = e.tensor_copy(out=rstdb[:], in_=ctx.ps[2][:])
                ctx.inc(ins, "dve")
                ins = e.tensor_copy(out=mbb[:], in_=ctx.ps[3][:])
                e.wait_ge(ctx.sem["dve"], ctx.inc(ins, "dve"))
                for k in range(nk):
                    if k >= 2:
                        e.wait_ge(ctx.sem["pool"], base["pool"] + nk + k - 1)
                    ins = e.tensor_tensor(out=tm[:, (k % 2) * TT:(k % 2 + 1) * TT],
                                          in0=s1[:, k * TT:(k + 1) * TT],
                                          in1=rstdb[:], op=ALU.mult)
                    ctx.inc(ins, "dve")

            @blk.tensor
            def _(e):
                e.wait_ge(ctx.sem["pf"], base["pf"] + 1)
                for k in range(nk):
                    e.wait_ge(ctx.sem["pool"], base["pool"] + k + 1)
                    e.matmul(out=ctx.ps[0][0:1, :], lhsT=onec[:, 0:1],
                             rhs=s1[:, k * TT:(k + 1) * TT],
                             start=(k == 0), stop=(k == nk - 1))
                    e.wait_ge(ctx.sem["act"], base["act"] + k + 1)
                    ins = e.matmul(out=ctx.ps[1][0:1, :], lhsT=onec[:, 0:1],
                                   rhs=sq[:, k * TT:(k + 1) * TT],
                                   start=(k == 0), stop=(k == nk - 1))
                    ctx.inc(ins, "pe")
                e.wait_ge(ctx.sem["x0"], base["x0"] + 1)
                ins = e.matmul(out=ctx.ps[2][:], lhsT=onef[:, :],
                               rhs=rows[0:1, 3 * TT:4 * TT], start=True, stop=True)
                ctx.inc(ins, "pe")
                e.wait_ge(ctx.sem["dve"], base["dve"] + 5)
                ins = e.matmul(out=ctx.ps[3][:], lhsT=onef[:, :],
                               rhs=rows[0:1, 4 * TT:5 * TT], start=True, stop=True)
                ctx.inc(ins, "pe")

            store_section(ctx, nk, "pool", lambda k: base["pool"] + nk + k + 1,
                          lambda k: obx[:, (k % 2) * TT:(k % 2 + 1) * TT],
                          lambda k: ctx.ap(dst)[128 * k:128 * (k + 1),
                                                t * TT:(t + 1) * TT], nslots=2)


def p_ln0a(ctx):
    p_ln(ctx, "ln0a", "xT", "att0_s", "x1_s", srcA_f32=True)



def _gamma_thresh(ctx, name, wname, nslab, nk, thr, col):
    with ctx.nc.Block(name) as blk:
        ctx.blk = blk
        with ExitStack() as les:
            wbuf = ctx.sb(les, "gw", 128, 2 * nk * 128, F32)
            asum = ctx.sb(les, "as", 128, 2, F32)
            red = ctx.sb(les, "rd", 128, 2, F32)
            onef = ctx.sb(les, "onf", 128, 1, F32)
            onr = ctx.sb(les, "onr", 1, 128, F32)
            tot = ctx.sb(les, "tot", 1, 2, F32)
            W = ctx.ap(wname)
            ld = ctx.stream("ld", 4)
            base_ld = ld.base()
            base_dve = ctx.c["dve"]

            @blk.sync
            def _(e):
                for m in range(nslab):
                    if m >= 2:
                        e.wait_ge(ctx.sem["dve"], base_dve + 3 + 2 * (m - 1))
                    ld.put(e, wbuf[:, (m % 2) * nk * 128:(m % 2 + 1) * nk * 128]
                           .rearrange("p (k c) -> p k c", k=nk),
                           W[m].rearrange("k p c -> p k c"), m % 2)

            @blk.vector
            def _(e):
                ins = e.memset(asum[:, 0:1], 0.0)
                e.wait_ge(ctx.sem["dve"], ctx.inc(ins, "dve"))
                ins = e.memset(onef[:], 1.0)
                e.wait_ge(ctx.sem["dve"], ctx.inc(ins, "dve"))
                ins = e.memset(onr[:], 1.0)
                e.wait_ge(ctx.sem["dve"], ctx.inc(ins, "dve"))
                for m in range(nslab):
                    e.wait_ge(ld.sems[m % 2], base_ld[m % 2] + 16 * (m // 2 + 1))
                    ins = e.tensor_reduce(out=red[:, 0:1],
                                          in_=wbuf[:, (m % 2) * nk * 128:(m % 2 + 1) * nk * 128],
                                          axis=AX.X, op=ALU.add, apply_absolute_value=True)
                    e.wait_ge(ctx.sem["dve"], ctx.inc(ins, "dve"))
                    ins = e.tensor_add(out=asum[:, 0:1], in0=asum[:, 0:1], in1=red[:, 0:1])
                    e.wait_ge(ctx.sem["dve"], ctx.inc(ins, "dve"))
                ctx.inc(e.engine_nop(), "x0")

            @blk.tensor
            def _(e):
                e.wait_ge(ctx.sem["x0"], ctx.c["x0"])
                ins = e.matmul(out=ctx.ps[0][0:1, 0:1], lhsT=asum[:, 0:1],
                               rhs=onef[:], start=True, stop=True)
                ctx.inc(ins, "pe")

            @blk.vector
            def _(e):
                e.wait_ge(ctx.sem["pe"], ctx.c["pe"])
                sc = 0.5 / (nslab * nk * 128 * 128)
                ins = e.tensor_scalar(out=tot[0:1, 0:1], in0=ctx.ps[0][0:1, 0:1],
                                      scalar1=sc, scalar2=0.5e-8,
                                      op0=ALU.mult, op1=ALU.add)
                e.wait_ge(ctx.sem["dve"], ctx.inc(ins, "dve"))
                ins = e.tensor_scalar_mul(out=tot[0:1, 1:2], in0=tot[0:1, 0:1],
                                          scalar1=-1.0)
                e.wait_ge(ctx.sem["dve"], ctx.inc(ins, "dve"))
                ctx.inc(e.engine_nop(), "x1")

            @blk.tensor
            def _(e):
                e.wait_ge(ctx.sem["x1"], ctx.c["x1"])
                ins = e.matmul(out=ctx.ps[1][:, 0:2], lhsT=onr[:],
                               rhs=tot[0:1, 0:2], start=True, stop=True)
                ctx.inc(ins, "pe")

            @blk.vector
            def _(e):
                e.wait_ge(ctx.sem["pe"], ctx.c["pe"])
                ins = e.tensor_copy(out=thr[:, col:col + 2], in_=ctx.ps[1][:, 0:2])
                ctx.inc(ins, "dve")


def _quant_ops(ctx, e, wsrc, tmp1, tmp2, qdst, thr, col):
    e.tensor_single_scalar(out=tmp1, in_=wsrc, scalar=thr[:, col:col + 1], op=ALU.is_gt)
    ins = e.tensor_single_scalar(out=tmp2, in_=wsrc, scalar=thr[:, col + 1:col + 2],
                                 op=ALU.is_lt)
    e.wait_ge(ctx.sem["dve"], ctx.inc(ins, "dve"))
    ins = e.tensor_sub(out=qdst, in0=tmp1, in1=tmp2)
    return ins


def p_ffnh(ctx):
    thr = ctx.psb("thr", 128, 8, F32)
    _gamma_thresh(ctx, "gam1", "W1", 44, 16, thr, 0)
    _gamma_thresh(ctx, "gam2", "W2", 44, 16, thr, 2)
    nk, nm = 16, 44
    with ExitStack() as les0:
        rhs = ctx.sb(les0, "x1res", 128, 16 * T, BF16)
        rw = rhs_load_block(ctx, "ldx1", "x1_s", 16, rhs)
        with ctx.nc.Block("ffnh") as blk:
            ctx.blk = blk
            with ExitStack() as les:
                wf1 = ctx.sb(les, "wf1", 128, 2 * nk * 128, F32)
                wf2 = ctx.sb(les, "wf2", 128, 2 * nk * 128, F32)
                tq1 = ctx.sb(les, "tq1", 128, nk * 128, F32)
                tq2 = ctx.sb(les, "tq2", 128, nk * 128, F32)
                wq1 = ctx.sb(les, "wq1", 128, 2 * nk * 128, BF16)
                wq2 = ctx.sb(les, "wq2", 128, 2 * nk * 128, BF16)
                sg = ctx.sb(les, "sg", 128, 2 * TT, BF16)
                ob = ctx.sb(les, "ob", 128, 4 * TT, BF16)
                ld = ctx.stream("ld", 4)
                base_ld = ld.base()
                base_pe = ctx.c["pe"]
                base_dve = ctx.c["dve"]
                base_act = ctx.c["act"]
                base_pf = ctx.c["pf"]
                base_st = ctx.stream("st", 4).base()
                stq = ctx.stream("st", 4)

                @blk.sync
                def _(e):
                    for m in range(nm):
                        if m >= 2:
                            e.wait_ge(ctx.sem["dve"], base_dve + 6 * (m - 1))
                        ld.put(e, wf1[:, (m % 2) * nk * 128:(m % 2 + 1) * nk * 128]
                               .rearrange("p (k c) -> p k c", k=nk),
                               ctx.ap("W1")[m].rearrange("k p c -> p k c"), m % 2)
                        ld.put(e, wf2[:, (m % 2) * nk * 128:(m % 2 + 1) * nk * 128]
                               .rearrange("p (k c) -> p k c", k=nk),
                               ctx.ap("W2")[m].rearrange("k p c -> p k c"), 2 + m % 2)

                def quantize(e, m):
                    e.wait_ge(ld.sems[m % 2], base_ld[m % 2] + 16 * (m // 2 + 1))
                    e.wait_ge(ld.sems[2 + m % 2], base_ld[2 + m % 2] + 16 * (m // 2 + 1))
                    if m >= 2:
                        e.wait_ge(ctx.sem["pe"], base_pe + (m - 1) * NT)
                    o1 = (m % 2) * nk * 128
                    ins = _quant_ops(ctx, e, wf1[:, o1:o1 + nk * 128], tq1[:],
                                     tq2[:], wq1[:, o1:o1 + nk * 128], thr, 0)
                    e.wait_ge(ctx.sem["dve"], ctx.inc(ins, "dve"))
                    ins = _quant_ops(ctx, e, wf2[:, o1:o1 + nk * 128], tq1[:],
                                     tq2[:], wq2[:, o1:o1 + nk * 128], thr, 2)
                    e.wait_ge(ctx.sem["dve"], ctx.inc(ins, "dve"))
                    ctx.inc(e.engine_nop(), "dve")
                    ctx.inc(e.engine_nop(), "dve")

                @blk.tensor
                def _(e):
                    e.wait_ge(ctx.sem[rw[0]], rw[1])
                    for m in range(nm):
                        e.wait_ge(ctx.sem["dve"], base_dve + 6 * (m + 1))
                        for t in range(NT):
                            j = m * NT + t
                            if j >= 2:
                                e.wait_ge(ctx.sem["pf"], base_pf + j - 1)
                            o1 = (m % 2) * nk * 128
                            b0 = ctx.ps[(j % 2) * 2]
                            b1 = ctx.ps[(j % 2) * 2 + 1]
                            for k in range(nk):
                                e.matmul(out=b0[:], lhsT=wq1[:, o1 + k * 128:o1 + (k + 1) * 128],
                                         rhs=rhs[:, k * T + t * TT:k * T + t * TT + TT],
                                         start=(k == 0), stop=(k == nk - 1))
                            for k in range(nk):
                                ins = e.matmul(out=b1[:], lhsT=wq2[:, o1 + k * 128:o1 + (k + 1) * 128],
                                               rhs=rhs[:, k * T + t * TT:k * T + t * TT + TT],
                                               start=(k == 0), stop=(k == nk - 1))
                            ctx.inc(ins, "pe")

                def dst_hid(j):
                    return ctx.ap("hid_s")[128 * (j // NT):128 * (j // NT) + 128,
                                           (j % NT) * TT:(j % NT + 1) * TT]

                @blk.scalar
                def _(e):
                    for j in range(nm * NT):
                        e.wait_ge(ctx.sem["pe"], base_pe + j + 1)
                        if j >= 2:
                            e.wait_ge(ctx.sem["pf"], base_pf + j - 1)
                        ins = e.activation(out=sg[:, (j % 2) * TT:(j % 2 + 1) * TT],
                                           in_=ctx.ps[(j % 2) * 2][:], func=AF.Sigmoid)
                        ctx.inc(ins, "act")
                        if j >= 1:
                            e.wait_ge(ctx.sem["pf"], base_pf + j)
                            stq.put(e, dst_hid(j - 1),
                                    ob[:, ((j - 1) % 4) * TT:((j - 1) % 4 + 1) * TT],
                                    (j - 1) % 4)
                    e.wait_ge(ctx.sem["pf"], base_pf + nm * NT)
                    stq.put(e, dst_hid(nm * NT - 1),
                            ob[:, ((nm * NT - 1) % 4) * TT:((nm * NT - 1) % 4 + 1) * TT],
                            (nm * NT - 1) % 4)
                    for sl in range(4):
                        e.wait_ge(stq.sems[sl], stq.cnt[sl])

                @blk.vector
                def _(e):
                    quantize(e, 0)
                    quantize(e, 1)
                    for j in range(nm * NT):
                        e.wait_ge(ctx.sem["act"], base_act + j + 1)
                        if j >= 4:
                            stq.wait(e, st_tok(ctx, base_st, j - 4))
                        ins = e.tensor_tensor(out=ob[:, (j % 4) * TT:(j % 4 + 1) * TT],
                                              in0=sg[:, (j % 2) * TT:(j % 2 + 1) * TT],
                                              in1=ctx.ps[(j % 2) * 2 + 1][:], op=ALU.mult)
                        ctx.inc(ins, "pf")
                        if j % NT == NT - 1 and j // NT + 2 < nm:
                            quantize(e, j // NT + 2)


def p_ffn3(ctx):
    thr = ctx.psb("thr", 128, 8, F32)
    _gamma_thresh(ctx, "gam3", "W3", 16, 44, thr, 4)
    with ctx.nc.Block("q3") as blk:
        ctx.blk = blk
        with ExitStack() as les:
            wf = ctx.sb(les, "wf", 128, 2 * 44 * 128, F32)
            tq1 = ctx.sb(les, "tq1", 128, 44 * 128, F32)
            tq2 = ctx.sb(les, "tq2", 128, 44 * 128, F32)
            qb = ctx.sb(les, "qb", 128, 2 * 44 * 128, BF16)
            ld = ctx.stream("ld", 4)
            base_ld = ld.base()
            base_dve = ctx.c["dve"]
            stq = ctx.stream("st", 4)
            base_st = stq.base()

            @blk.sync
            def _(e):
                for m in range(16):
                    if m >= 2:
                        e.wait_ge(ctx.sem["dve"], base_dve + 3 * (m - 1))
                    ld.put(e, wf[:, (m % 2) * 44 * 128:(m % 2 + 1) * 44 * 128]
                           .rearrange("p (k c) -> p k c", k=44),
                           ctx.ap("W3")[m].rearrange("k p c -> p k c"), m % 2)

            @blk.vector
            def _(e):
                for m in range(16):
                    e.wait_ge(ld.sems[m % 2], base_ld[m % 2] + 16 * (m // 2 + 1))
                    if m >= 2:
                        stq.wait(e, st_tok(ctx, base_st, m - 2, 2))
                    o = (m % 2) * 44 * 128
                    ins = _quant_ops(ctx, e, wf[:, o:o + 44 * 128], tq1[:], tq2[:],
                                     qb[:, o:o + 44 * 128], thr, 4)
                    e.wait_ge(ctx.sem["dve"], ctx.inc(ins, "dve"))
                    ctx.inc(e.engine_nop(), "dve")

            store_section(ctx, 16, "dve", lambda m: base_dve + 3 * (m + 1),
                          lambda m: qb[:, (m % 2) * 44 * 128:(m % 2 + 1) * 44 * 128],
                          lambda m: ctx.ap("w3q_s")[128 * m:128 * (m + 1), :],
                          nslots=2, eng="scalar")

    with ExitStack() as les0:
        rhs = ctx.sb(les0, "hidres", 128, 44 * 1024, BF16)
        for tb in range(4):
            rw = rhs_load_block(ctx, f"ldh{tb}", "hid_s", 44, rhs,
                                c0=tb * 1024, ncols=1024)
            with ctx.nc.Block(f"ffn3_{tb}") as blk:
                ctx.blk = blk
                with ExitStack() as les:
                    wq = ctx.sb(les, "wq", 128, 2 * 44 * 128, BF16)
                    ob = ctx.sb(les, "ob", 128, 4 * TT, BF16)
                    ld = ctx.stream("ld", 4)
                    base_ld = ld.base()
                    base_pe = ctx.c["pe"]
                    base_dve = ctx.c["dve"]
                    stq = ctx.stream("st", 4)
                    base_st = stq.base()

                    @blk.sync
                    def _(e):
                        for m in range(16):
                            if m >= 2:
                                e.wait_ge(ctx.sem["pe"], base_pe + 2 * (m - 1))
                            ld.put(e, wq[:, (m % 2) * 44 * 128:(m % 2 + 1) * 44 * 128],
                                   ctx.ap("w3q_s")[128 * m:128 * (m + 1), :], m % 2)

                    @blk.tensor
                    def _(e):
                        e.wait_ge(ctx.sem[rw[0]], rw[1])
                        for m in range(16):
                            e.wait_ge(ld.sems[m % 2], base_ld[m % 2] + 16 * (m // 2 + 1))
                            for t in range(2):
                                j = 2 * m + t
                                if j >= 4:
                                    e.wait_ge(ctx.sem["dve"], base_dve + j - 3)
                                o = (m % 2) * 44 * 128
                                for k in range(44):
                                    ins = e.matmul(
                                        out=ctx.ps[j % 4][:],
                                        lhsT=wq[:, o + k * 128:o + (k + 1) * 128],
                                        rhs=rhs[:, k * 1024 + t * TT:k * 1024 + t * TT + TT],
                                        start=(k == 0), stop=(k == 43))
                                ctx.inc(ins, "pe")

                    @blk.vector
                    def _(e):
                        for j in range(32):
                            e.wait_ge(ctx.sem["pe"], base_pe + j + 1)
                            if j >= 4:
                                stq.wait(e, st_tok(ctx, base_st, j - 4))
                            ins = e.tensor_copy(out=ob[:, (j % 4) * TT:(j % 4 + 1) * TT],
                                                in_=ctx.ps[j % 4][:])
                            ctx.inc(ins, "dve")

                    store_section(ctx, 32, "dve", lambda j: base_dve + j + 1,
                                  lambda j: ob[:, (j % 4) * TT:(j % 4 + 1) * TT],
                                  lambda j: ctx.ap("ff0_s")[
                                      128 * (j // 2):128 * (j // 2) + 128,
                                      tb * 1024 + (j % 2) * TT:tb * 1024 + (j % 2 + 1) * TT],
                                  eng="scalar")


def p_ln0b(ctx):
    p_ln(ctx, "ln0b", "x1_s", "ff0_s", "x2_s")



def p_qkv(ctx):
    with ExitStack() as les:
        rhs = ctx.sb(les, "x2res", 128, 16 * T, BF16)
        rw = rhs_load_block(ctx, "ldx2", "x2_s", 16, rhs)
        generic_linear(ctx, "qkv", "Wqkv", 16, 48, rhs, rw,
                       lambda m, t: ctx.ap("qkv_s")[128 * m:128 * m + 128,
                                                    t * TT:(t + 1) * TT])


def _attn_blk(ctx, h, ci, cst):
    nw = 4 if ci == 0 else 8
    qkv = ctx.ap("qkv_s")
    mk = cst["mk0"] if ci == 0 else cst["mkg"]
    with ctx.nc.Block(f"at{h}_{ci}") as blk:
        ctx.blk = blk
        with ExitStack() as les:
            qt = ctx.sb(les, "qt", 128, TT, BF16)
            kw = ctx.sb(les, "kw", 128, nw * 128, BF16)
            vw = ctx.sb(les, "vw", 128, nw * 128, BF16)
            vtm = ctx.sb(les, "vtm", 128, nw * 128, BF16)
            pre2 = ctx.sb(les, "pre2", 128, nw * TT, F32)
            tmp = ctx.sb(les, "tmp", 128, 2 * TT, F32)
            au = ctx.sb(les, "au", 128, nw * TT, BF16)
            mxb = ctx.sb(les, "mxb", 128, 4, BF16)
            cred = ctx.sb(les, "cred", 128, 2, BF16)
            mrow = ctx.sb(les, "mrow", 1, TT, BF16)
            srow = ctx.sb(les, "srow", 1, TT, F32)
            rrow = ctx.sb(les, "rrow", 1, TT, F32)
            mxs = ctx.sb(les, "mxs", 128, TT, F32)
            rbs = ctx.sb(les, "rbs", 128, TT, F32)
            ob = ctx.sb(les, "ob", 128, TT, BF16)
            ld = ctx.stream("ld", 4)
            base = {sn: ctx.c[sn] for sn in
                    ["pe", "act", "dve", "pool", "pf", "x0", "x1", "st"]}
            kv0 = (ci - 1) * TT if ci > 0 else 0
            toks = {}
            nhalf = nw // 4
            PE_VT = 1
            PE_R1 = 2
            PE_MAX = lambda qb_: 2 + nhalf * (qb_ + 1)
            PE_MROW = 2 + 4 * nhalf + 1
            PE_MAXB = PE_MROW + 1
            PE_R2 = PE_MAXB + (1 if nw == 8 else 0)
            PE_SUM = PE_R2 + 1
            PE_RECB = PE_SUM + 1
            PE_AV = PE_RECB + 1
            DVE_VTM = 1
            DVE_MXS = DVE_VTM + 8 + 1

            @blk.sync
            def _(e):
                toks[0] = ld.put(e, qt[:], qkv[128 * h:128 * (h + 1),
                                              ci * TT:(ci + 1) * TT], 0)
                toks[1] = ld.put(e, kw[:], qkv[2048 + 128 * h:2048 + 128 * (h + 1),
                                               kv0:kv0 + nw * 128], 1)
                toks[2] = ld.put(e, vw[:], qkv[4096 + 128 * h:4096 + 128 * (h + 1),
                                               kv0:kv0 + nw * 128], 2)
                e.wait_ge(ctx.sem["x1"], base["x1"] + 3)
                d = e.dma_start(out=ctx.ap("ao_s")[128 * h:128 * (h + 1),
                                                   ci * TT:(ci + 1) * TT], in_=ob[:])
                d.then_inc(ctx.sem["st"], 16)
                ctx.c["st"] += 16
                e.wait_ge(ctx.sem["st"], ctx.c["st"])

            @blk.tensor
            def _(e):
                ld.wait(e, toks[2])
                for j in range(nw):
                    ins = e.matmul(out=ctx.ps[4 + j // 4][:, (j % 4) * 128:(j % 4 + 1) * 128],
                                   lhsT=vw[:, j * 128:(j + 1) * 128], rhs=cst["idb"][:],
                                   start=(j % 4 == 0), stop=(j % 4 == 3 or j == nw - 1))
                ctx.inc(ins, "pe")
                ld.wait(e, toks[0])
                ld.wait(e, toks[1])
                e.wait_ge(ctx.sem["dve"], base["dve"] + DVE_VTM)
                for j in range(min(nw, 4)):
                    ins = e.matmul(out=ctx.ps[j][:], lhsT=kw[:, j * 128:(j + 1) * 128],
                                   rhs=qt[:], start=True, stop=True)
                ctx.inc(ins, "pe")
                for qb_ in range(4):
                    bks = (4, 5) if qb_ % 2 == 0 else (6, 7)
                    if qb_ >= 2:
                        e.wait_ge(ctx.sem["x0"], base["x0"] + qb_ - 1)
                    for half in range(nhalf):
                        ins = e.matmul(out=ctx.ps[bks[half]][:],
                                       lhsT=qt[:, qb_ * 128:(qb_ + 1) * 128],
                                       rhs=kw[:, half * 512:half * 512 + min(512, nw * 128)],
                                       start=True, stop=True)
                        ctx.inc(ins, "pe")
                e.wait_ge(ctx.sem["x0"], base["x0"] + 4)
                for qb_ in range(4):
                    ins = e.matmul(out=ctx.ps[6][0:1, qb_ * 128:(qb_ + 1) * 128],
                                   lhsT=mxb[:, qb_:qb_ + 1], rhs=cst["idb"][:],
                                   start=(qb_ == 0), stop=(qb_ == 3))
                ctx.inc(ins, "pe")
                e.wait_ge(ctx.sem["x1"], base["x1"] + 1)
                ins = e.matmul(out=ctx.ps[7][:], lhsT=cst["onrb"][:], rhs=mrow[:],
                               start=True, stop=True)
                ctx.inc(ins, "pe")
                if nw == 8:
                    e.wait_ge(ctx.sem["pf"], base["pf"] + 4)
                    for j in range(4, 8):
                        ins = e.matmul(out=ctx.ps[j - 4][:],
                                       lhsT=kw[:, j * 128:(j + 1) * 128],
                                       rhs=qt[:], start=True, stop=True)
                    ctx.inc(ins, "pe")
                for j in range(nw):
                    e.wait_ge(ctx.sem["act"], base["act"] + j + 1)
                    ins = e.matmul(out=ctx.ps[6][0:1, 0:TT], lhsT=cst["onecb"][:],
                                   rhs=au[:, j * TT:(j + 1) * TT],
                                   start=(j == 0), stop=(j == nw - 1))
                ctx.inc(ins, "pe")
                e.wait_ge(ctx.sem["x1"], base["x1"] + 2)
                ins = e.matmul(out=ctx.ps[4][:], lhsT=cst["onrf"][:], rhs=rrow[:],
                               start=True, stop=True)
                ctx.inc(ins, "pe")
                for j in range(nw):
                    ins = e.matmul(out=ctx.ps[5][:], lhsT=vtm[:, j * 128:(j + 1) * 128],
                                   rhs=au[:, j * TT:(j + 1) * TT],
                                   start=(j == 0), stop=(j == nw - 1))
                ctx.inc(ins, "pe")

            @blk.vector
            def _(e):
                e.wait_ge(ctx.sem["pe"], base["pe"] + PE_VT)
                if nw == 4:
                    ins = e.tensor_copy(out=vtm[:], in_=ctx.ps[4][:])
                else:
                    e.tensor_copy(out=vtm[:, 0:512], in_=ctx.ps[4][:])
                    ins = e.tensor_copy(out=vtm[:, 512:1024], in_=ctx.ps[5][:])
                e.wait_ge(ctx.sem["dve"], ctx.inc(ins, "dve"))
                for qb_ in range(4):
                    bks = (4, 5) if qb_ % 2 == 0 else (6, 7)
                    e.wait_ge(ctx.sem["pe"], base["pe"] + PE_MAX(qb_))
                    if nhalf == 2:
                        ins = e.tensor_reduce(out=cred[:, 0:1], in_=ctx.ps[bks[0]][:],
                                              axis=AX.X, op=ALU.max)
                        e.wait_ge(ctx.sem["dve"], ctx.inc(ins, "dve"))
                        ins = e.tensor_reduce(out=cred[:, 1:2], in_=ctx.ps[bks[1]][:],
                                              axis=AX.X, op=ALU.max)
                        e.wait_ge(ctx.sem["dve"], ctx.inc(ins, "dve"))
                        ins = e.tensor_max(out=mxb[:, qb_:qb_ + 1],
                                           in0=cred[:, 0:1], in1=cred[:, 1:2])
                    else:
                        ins = e.tensor_reduce(out=mxb[:, qb_:qb_ + 1],
                                              in_=ctx.ps[bks[0]][:],
                                              axis=AX.X, op=ALU.max)
                        e.wait_ge(ctx.sem["dve"], ctx.inc(ins, "dve"))
                        ctx.inc(e.engine_nop(), "dve")
                        ins = e.engine_nop()
                    e.wait_ge(ctx.sem["x0"], ctx.inc(ins, "x0"))
                e.wait_ge(ctx.sem["pe"], base["pe"] + PE_MROW)
                ins = e.tensor_copy(out=mrow[:], in_=ctx.ps[6][0:1, :])
                e.wait_ge(ctx.sem["x1"], ctx.inc(ins, "x1"))
                e.wait_ge(ctx.sem["pe"], base["pe"] + PE_MAXB)
                ins = e.tensor_copy(out=mxs[:], in_=ctx.ps[7][:])
                e.wait_ge(ctx.sem["dve"], ctx.inc(ins, "dve"))
                for j in range(nw):
                    e.wait_ge(ctx.sem["pool"], base["pool"] + j + 1)
                    e.wait_ge(ctx.sem["pe"],
                              base["pe"] + (PE_R1 if j < 4 else PE_R2))
                    if j >= 2:
                        e.wait_ge(ctx.sem["act"], base["act"] + j - 1)
                    ins = e.scalar_tensor_tensor(
                        out=tmp[:, (j % 2) * TT:(j % 2 + 1) * TT],
                        in0=ctx.ps[j % 4][:], scalar=ccol(ctx, 0.088388347648318447),
                        in1=pre2[:, j * TT:(j + 1) * TT],
                        op0=ALU.mult, op1=ALU.add)
                    ctx.inc(ins, "pf")
                e.wait_ge(ctx.sem["pe"], base["pe"] + PE_SUM)
                ins = e.tensor_copy(out=srow[:], in_=ctx.ps[6][0:1, 0:TT])
                e.wait_ge(ctx.sem["dve"], ctx.inc(ins, "dve"))
                ins = e.reciprocal(out=rrow[:], in_=srow[:])
                e.wait_ge(ctx.sem["x1"], ctx.inc(ins, "x1"))
                e.wait_ge(ctx.sem["pe"], base["pe"] + PE_RECB)
                ins = e.tensor_copy(out=rbs[:], in_=ctx.ps[4][:])
                e.wait_ge(ctx.sem["dve"], ctx.inc(ins, "dve"))
                e.wait_ge(ctx.sem["pe"], base["pe"] + PE_AV)
                ins = e.tensor_tensor(out=ob[:], in0=ctx.ps[5][:], in1=rbs[:],
                                      op=ALU.mult)
                ctx.inc(ins, "x1")

            @blk.gpsimd
            def _(e):
                e.wait_ge(ctx.sem["dve"], base["dve"] + DVE_MXS)
                for j in range(nw):
                    ins = e.tensor_tensor(out=pre2[:, j * TT:(j + 1) * TT],
                                          in0=mk[:, j * TT:(j + 1) * TT],
                                          in1=mxs[:], op=ALU.subtract)
                    ctx.inc(ins, "pool")

            @blk.scalar
            def _(e):
                for j in range(nw):
                    e.wait_ge(ctx.sem["pf"], base["pf"] + j + 1)
                    ins = e.activation(out=au[:, j * TT:(j + 1) * TT],
                                       in_=tmp[:, (j % 2) * TT:(j % 2 + 1) * TT],
                                       func=AF.Exp)
                    ctx.inc(ins, "act")


def p_attn(ctx):
    cst = {
        "idb": ctx.psb("idb", 128, 128, BF16),
        "mkg": ctx.psb("mkg", 128, 8 * TT, BF16),
        "mk0": ctx.psb("mk0", 128, 4 * TT, BF16),
        "onrb": ctx.psb("onrb", 1, 128, BF16),
        "onrf": ctx.psb("onrf", 1, 128, F32),
        "onecb": ctx.psb("onecb", 128, 1, BF16),
    }
    with ctx.nc.Block("atpro") as blk:
        ctx.blk = blk

        @blk.sync
        def _(e):
            d = e.dma_start(out=cst["idb"][:], in_=ctx.ap("identb")[:, :])
            ctx.inc(d, "q0", 16)
            for j in range(8):
                d = e.dma_start(out=cst["mkg"][:, j * TT:(j + 1) * TT],
                                in_=ctx.ap("maskg")[128 * j:128 * (j + 1), :])
                ctx.inc(d, "q0", 16)
            for j in range(4):
                d = e.dma_start(out=cst["mk0"][:, j * TT:(j + 1) * TT],
                                in_=ctx.ap("mask0")[128 * j:128 * (j + 1), :])
                ctx.inc(d, "q0", 16)
            e.wait_ge(ctx.sem["q0"], ctx.c["q0"])

        @blk.vector
        def _(e):
            e.memset(cst["onrb"][:], 1.0)
            e.memset(cst["onrf"][:], 1.0)
            e.memset(cst["onecb"][:], 1.0)
    for ci in range(8):
        for h in range(16):
            _attn_blk(ctx, h, ci, cst)


def p_wout(ctx):
    with ExitStack() as les:
        rhs = ctx.sb(les, "aores", 128, 16 * T, BF16)
        rw = rhs_load_block(ctx, "ldao", "ao_s", 16, rhs)
        generic_linear(ctx, "wout", "Wo1", 16, 16, rhs, rw,
                       lambda m, t: ctx.ap("att1_s")[128 * m:128 * m + 128,
                                                     t * TT:(t + 1) * TT])


def p_ln1a(ctx):
    p_ln(ctx, "ln1a", "x2_s", "att1_s", "x3_s")



def p_l1fa(ctx):
    modc = ctx.psb("modc", 128, 64, F32)
    ampbc = ctx.psb("ampbc", 128, 16, F32)
    phbc = ctx.psb("phbc", 128, 16, F32)
    pbc = ctx.psb("pbc", 128, 64, F32)
    with ctx.nc.Block("mod") as blk:
        ctx.blk = blk
        with ExitStack() as les:
            ph = ctx.sb(les, "phx", 128, 64, F32)
            t1 = ctx.sb(les, "mt1", 128, 64, F32)
            ti = ctx.sb(les, "mti", 128, 64, I32)
            sr = ctx.sb(les, "msr", 128, 64, F32)

            @blk.sync
            def _(e):
                load_cols(ctx, e, "phs0", ph[:, 0:32])
                load_cols(ctx, e, "phs1", ph[:, 32:64])
                load_cols(ctx, e, "ampb", ampbc)
                load_cols(ctx, e, "phb", phbc)
                load_cols(ctx, e, "pathb0", pbc[:, 0:32])
                load_cols(ctx, e, "pathb1", pbc[:, 32:64])
            nq = ctx.c["q0"]

            @blk.vector
            def _(e):
                e.wait_ge(ctx.sem["q0"], nq)
                ins = e.tensor_scalar_mul(out=t1[:], in0=ph[:],
                                          scalar1=float(1.0 / (2 * np.pi)))
                e.wait_ge(ctx.sem["dve"], ctx.inc(ins, "dve"))
                ins = e.tensor_scalar_add(out=t1[:], in0=t1[:], scalar1=12582912.0)
                e.wait_ge(ctx.sem["dve"], ctx.inc(ins, "dve"))
                ins = e.tensor_scalar_add(out=t1[:], in0=t1[:], scalar1=-12582912.0)
                e.wait_ge(ctx.sem["dve"], ctx.inc(ins, "dve"))
                ins = e.tensor_scalar_mul(out=t1[:], in0=t1[:],
                                          scalar1=float(-2.0 * np.pi))
                e.wait_ge(ctx.sem["dve"], ctx.inc(ins, "dve"))
                ins = e.tensor_add(out=t1[:], in0=t1[:], in1=ph[:])
                ctx.inc(ins, "x0")

            @blk.scalar
            def _(e):
                e.wait_ge(ctx.sem["x0"], ctx.c["x0"])
                ins = e.activation(out=sr[:], in_=t1[:], func=AF.Abs)
                e.wait_ge(ctx.sem["act"], ctx.inc(ins, "act"))
                ins = e.activation(out=sr[:], in_=sr[:], func=AF.Sin, scale=-1.0,
                                   bias=ccol(ctx, np.pi / 2))
                e.wait_ge(ctx.sem["act"], ctx.inc(ins, "act"))
                ins = e.activation(out=t1[:], in_=t1[:], func=AF.Sin)
                ctx.inc(ins, "act")
            nact_mod = ctx.c["act"]

            @blk.vector
            def _(e):
                e.wait_ge(ctx.sem["act"], nact_mod)
                ins = e.tensor_add(out=modc[:], in0=t1[:], in1=sr[:])
                ctx.inc(ins, "dve")

    with ExitStack() as les:
        rhs = ctx.sb(les, "x3res", 128, 16 * T, BF16)
        rw = rhs_load_block(ctx, "ldx3", "x3_s", 16, rhs)

        def epi_amp(e, m, t, ps, buf):
            return e.activation(out=buf, in_=ps[:], func=AF.Sigmoid,
                                bias=ampbc[:, m:m + 1], scale=ccol(ctx, 1.0))

        generic_linear(ctx, "amp", "Wamp", 16, 16, rhs, rw,
                       lambda m, t: ctx.ap("amp_s")[128 * m:128 * m + 128,
                                                    t * TT:(t + 1) * TT],
                       epi=epi_amp, epi_eng="scalar")

        with ctx.nc.Block("phc") as blk:
            ctx.blk = blk
            with ExitStack() as les2:
                wbuf = ctx.sb(les2, "wb", 128, 2 * 16 * 128, BF16)
                th = ctx.sb(les2, "th", 128, 2 * TT, F32)
                sn = ctx.sb(les2, "sn", 128, 2 * TT, F32)
                ab = ctx.sb(les2, "abq", 128, 2 * TT, F32)
                cs_ = ctx.sb(les2, "csq", 128, 2 * TT, F32)
                amt = ctx.sb(les2, "amt", 128, 4 * TT, BF16)
                sm = ctx.sb(les2, "sm", 128, 2 * TT, F32)
                ob = ctx.sb(les2, "ob", 128, 4 * TT, BF16)
                ld = ctx.stream("ld", 4)
                base_ld = ld.base()
                w_ready = slab_loader(ctx, "Wph", 16, 16, wbuf)
                base_act = ctx.c["act"]
                base_dve = ctx.c["dve"]
                base_pe = mm_section(ctx, 16, 16, wbuf, rhs, w_ready, rw,
                                     lambda j: ("act", base_act + 4 * j + 1))
                stq = ctx.stream("st", 4)
                base_st = stq.base()
                base_x0 = ctx.c["x0"]

                @blk.sync
                def _(e):
                    for j in range(16 * NT):
                        if j >= 4:
                            e.wait_ge(ctx.sem["dve"], base_dve + 3 * (j - 3))
                        ld.put(e, amt[:, (j % 4) * TT:(j % 4 + 1) * TT],
                               ctx.ap("amp_s")[128 * (j // NT):128 * (j // NT) + 128,
                                               (j % NT) * TT:(j % NT + 1) * TT],
                               j % 4)
                        if j >= 2:
                            e.wait_ge(ctx.sem["dve"], base_dve + 3 * (j - 1))
                            stq.put(e, ctx.ap("h0_s")[
                                128 * ((j - 2) // NT):128 * ((j - 2) // NT) + 128,
                                ((j - 2) % NT) * TT:((j - 2) % NT + 1) * TT],
                                ob[:, ((j - 2) % 4) * TT:((j - 2) % 4 + 1) * TT],
                                (j - 2) % 4)
                    for j in (16 * NT - 2, 16 * NT - 1):
                        e.wait_ge(ctx.sem["dve"], base_dve + 3 * (j + 1))
                        stq.put(e, ctx.ap("h0_s")[
                            128 * (j // NT):128 * (j // NT) + 128,
                            (j % NT) * TT:(j % NT + 1) * TT],
                            ob[:, (j % 4) * TT:(j % 4 + 1) * TT], j % 4)
                    for sl in range(4):
                        e.wait_ge(stq.sems[sl], stq.cnt[sl])

                @blk.scalar
                def _(e):
                    for j in range(16 * NT):
                        m = j // NT
                        e.wait_ge(ctx.sem["pe"], base_pe + j + 1)
                        if j >= 2:
                            e.wait_ge(ctx.sem["dve"], base_dve + 3 * (j - 1))
                        tv = th[:, (j % 2) * TT:(j % 2 + 1) * TT]
                        ins = e.activation(out=tv, in_=ctx.ps[j % 4][:], func=AF.Tanh,
                                           bias=phbc[:, m:m + 1], scale=ccol(ctx, 1.0))
                        e.wait_ge(ctx.sem["act"], ctx.inc(ins, "act"))
                        ins = e.activation(out=ab[:, (j % 2) * TT:(j % 2 + 1) * TT],
                                           in_=tv, func=AF.Abs)
                        e.wait_ge(ctx.sem["act"], ctx.inc(ins, "act"))
                        ins = e.activation(out=sn[:, (j % 2) * TT:(j % 2 + 1) * TT],
                                           in_=tv, func=AF.Sin, scale=float(np.pi))
                        ctx.inc(ins, "act")
                        ins = e.activation(out=cs_[:, (j % 2) * TT:(j % 2 + 1) * TT],
                                           in_=ab[:, (j % 2) * TT:(j % 2 + 1) * TT],
                                           func=AF.Sin, scale=float(-np.pi),
                                           bias=ccol(ctx, np.pi / 2))
                        ctx.inc(ins, "act")

                @blk.vector
                def _(e):
                    for j in range(16 * NT):
                        e.wait_ge(ctx.sem["act"], base_act + 4 * (j + 1))
                        if j >= 2:
                            e.wait_ge(ctx.sem["dve"], base_dve + 3 * (j - 1))
                        ins = e.tensor_add(out=sm[:, (j % 2) * TT:(j % 2 + 1) * TT],
                                           in0=sn[:, (j % 2) * TT:(j % 2 + 1) * TT],
                                           in1=cs_[:, (j % 2) * TT:(j % 2 + 1) * TT])
                        e.wait_ge(ctx.sem["dve"], ctx.inc(ins, "dve"))
                        ctx.inc(e.engine_nop(), "dve")
                        ld.wait(e, (j % 4, base_ld[j % 4] + 16 * (j // 4 + 1)))
                        if j >= 4:
                            stq.wait(e, st_tok(ctx, base_st, j - 4))
                        ins = e.tensor_tensor(out=ob[:, (j % 4) * TT:(j % 4 + 1) * TT],
                                              in0=sm[:, (j % 2) * TT:(j % 2 + 1) * TT],
                                              in1=amt[:, (j % 4) * TT:(j % 4 + 1) * TT],
                                              op=ALU.mult)
                        ctx.inc(ins, "dve")


def p_ent(ctx, l):
    name = f"ent{l}"
    hsrc = "h0_s" if l == 0 else "hB0_s"
    ebc = ctx.psb(f"entbc{l}", 128, 64, F32)
    with ctx.nc.Block(name + "c") as blk:
        ctx.blk = blk

        @blk.sync
        def _(e):
            load_cols(ctx, e, f"entb{l}", ebc)
            e.wait_ge(ctx.sem["q0"], ctx.c["q0"])
    with ExitStack() as les:
        rhs = ctx.sb(les, "hres", 128, 16 * T, BF16)
        rw = rhs_load_block(ctx, "ldh" + name, hsrc, 16, rhs)

        def epi(e, m, t, ps, buf):
            return e.tensor_scalar_add(out=buf, in0=ps[:], scalar1=ebc[:, m:m + 1])

        generic_linear(ctx, name, f"Went{l}", 16, 64, rhs, rw,
                       lambda m, t: ctx.ap(f"e{l}_s")[128 * m:128 * m + 128,
                                                      t * TT:(t + 1) * TT], epi=epi)


def p_entp(ctx, l):
    ebc = ctx.psb(f"entpbc{l}", 128, 16, F32)
    with ctx.nc.Block(f"entp{l}c") as blk:
        ctx.blk = blk

        @blk.sync
        def _(e):
            load_cols(ctx, e, f"entpb{l}", ebc)
            e.wait_ge(ctx.sem["q0"], ctx.c["q0"])
    with ExitStack() as les:
        rhs = ctx.sb(les, "eres", 128, 64 * 1024, BF16)
        for tb in range(4):
            rw = rhs_load_block(ctx, f"lde{l}_{tb}", f"e{l}_s", 64, rhs,
                                c0=tb * 1024, ncols=1024)

            def epi(e, m, t, ps, buf):
                return e.tensor_scalar_add(out=buf, in0=ps[:], scalar1=ebc[:, m:m + 1])

            def dst_of(m, t, _tb=tb):
                return ctx.ap(f"s3{l}_s")[128 * m:128 * m + 128,
                                          _tb * 1024 + t * TT:_tb * 1024 + (t + 1) * TT]

            generic_linear(ctx, f"entp{l}_{tb}", f"Wentp{l}", 64, 16, rhs, rw,
                           dst_of, epi=epi, ntt=2, rhs_stride=1024)


def p_path(ctx, l):
    hsrc = f"hA{l}_s"
    modc = ctx.keep["modc"]
    pbc = ctx.keep["pbc"]
    nk, nm = 16, 16
    with ExitStack() as les0:
        rhs = ctx.sb(les0, "hres2", 128, 16 * T, BF16)
        rw = rhs_load_block(ctx, f"ldhA{l}", hsrc, 16, rhs)
        with ctx.nc.Block(f"path{l}") as blk:
            ctx.blk = blk
            with ExitStack() as les:
                wb0 = ctx.sb(les, "wb0", 128, 2 * nk * 128, BF16)
                wb1 = ctx.sb(les, "wb1", 128, 2 * nk * 128, BF16)
                e0t = ctx.sb(les, "e0t", 128, 2 * TT, F32)
                e1t = ctx.sb(les, "e1t", 128, 2 * TT, F32)
                ob = ctx.sb(les, "ob", 128, 4 * TT, BF16)
                wt = ctx.stream("wt", 2)
                wt2 = ctx.stream("wt2", 2)
                bw = wt.base()
                bw2 = wt2.base()
                base_pe = ctx.c["pe"]
                base_dve = ctx.c["dve"]
                stq = ctx.stream("st", 4)
                base_st = stq.base()
                W0 = ctx.ap(f"Wpath{l}0")
                W1 = ctx.ap(f"Wpath{l}1")

                @blk.gpsimd
                def _(e):
                    for m in range(nm):
                        if m >= 2:
                            e.wait_ge(ctx.sem["pe"], base_pe + (m - 1) * NT)
                        wt.put(e, wb0[:, (m % 2) * nk * 128:(m % 2 + 1) * nk * 128]
                               .rearrange("p (k c) -> p k c", k=nk),
                               W0[m].rearrange("k p c -> p k c"), m % 2)
                        wt2.put(e, wb1[:, (m % 2) * nk * 128:(m % 2 + 1) * nk * 128]
                                .rearrange("p (k c) -> p k c", k=nk),
                                W1[m].rearrange("k p c -> p k c"), m % 2)

                @blk.tensor
                def _(e):
                    e.wait_ge(ctx.sem[rw[0]], rw[1])
                    for m in range(nm):
                        e.wait_ge(wt.sems[m % 2], bw[m % 2] + 16 * (m // 2 + 1))
                        e.wait_ge(wt2.sems[m % 2], bw2[m % 2] + 16 * (m // 2 + 1))
                        for t in range(NT):
                            j = m * NT + t
                            if j >= 2:
                                e.wait_ge(ctx.sem["dve"], base_dve + 4 * (j - 1))
                            o1 = (m % 2) * nk * 128
                            b0 = ctx.ps[(j % 2) * 2]
                            b1 = ctx.ps[(j % 2) * 2 + 1]
                            for k in range(nk):
                                e.matmul(out=b0[:], lhsT=wb0[:, o1 + k * 128:o1 + (k + 1) * 128],
                                         rhs=rhs[:, k * T + t * TT:k * T + t * TT + TT],
                                         start=(k == 0), stop=(k == nk - 1))
                            for k in range(nk):
                                ins = e.matmul(out=b1[:], lhsT=wb1[:, o1 + k * 128:o1 + (k + 1) * 128],
                                               rhs=rhs[:, k * T + t * TT:k * T + t * TT + TT],
                                               start=(k == 0), stop=(k == nk - 1))
                            ctx.inc(ins, "pe")

                @blk.sync
                def _(e):
                    for j in range(nm * NT):
                        e.wait_ge(ctx.sem["dve"], base_dve + 4 * (j + 1))
                        stq.put(e, ctx.ap(f"s4{l}_s")[
                            128 * (j // NT):128 * (j // NT) + 128,
                            (j % NT) * TT:(j % NT + 1) * TT],
                            ob[:, (j % 4) * TT:(j % 4 + 1) * TT], j % 4)
                    for sl in range(4):
                        e.wait_ge(stq.sems[sl], stq.cnt[sl])

                @blk.vector
                def _(e):
                    for j in range(nm * NT):
                        m = j // NT
                        e.wait_ge(ctx.sem["pe"], base_pe + j + 1)
                        ins = e.tensor_scalar(out=e0t[:, (j % 2) * TT:(j % 2 + 1) * TT],
                                              in0=ctx.ps[(j % 2) * 2][:],
                                              scalar1=pbc[:, l * 32 + m:l * 32 + m + 1],
                                              scalar2=modc[:, l * 32 + m:l * 32 + m + 1],
                                              op0=ALU.add, op1=ALU.mult)
                        e.wait_ge(ctx.sem["dve"], ctx.inc(ins, "dve"))
                        ins = e.tensor_scalar(out=e1t[:, (j % 2) * TT:(j % 2 + 1) * TT],
                                              in0=ctx.ps[(j % 2) * 2 + 1][:],
                                              scalar1=pbc[:, l * 32 + 16 + m:l * 32 + 17 + m],
                                              scalar2=modc[:, l * 32 + 16 + m:l * 32 + 17 + m],
                                              op0=ALU.add, op1=ALU.mult)
                        e.wait_ge(ctx.sem["dve"], ctx.inc(ins, "dve"))
                        ins = e.tensor_add(out=e0t[:, (j % 2) * TT:(j % 2 + 1) * TT],
                                           in0=e0t[:, (j % 2) * TT:(j % 2 + 1) * TT],
                                           in1=e1t[:, (j % 2) * TT:(j % 2 + 1) * TT])
                        e.wait_ge(ctx.sem["dve"], ctx.inc(ins, "dve"))
                        if j >= 4:
                            stq.wait(e, st_tok(ctx, base_st, j - 4))
                        ins = e.tensor_copy(out=ob[:, (j % 4) * TT:(j % 4 + 1) * TT],
                                            in_=e0t[:, (j % 2) * TT:(j % 2 + 1) * TT])
                        ctx.inc(ins, "dve")


def p_ent0(ctx):
    p_ent(ctx, 0)


def p_entp0(ctx):
    p_entp(ctx, 0)


def p_ln30(ctx):
    p_ln(ctx, "ln30", "h0_s", "s30_s", "hA0_s", gname="ln3g0", bname="ln3b0")


def p_path0(ctx):
    p_path(ctx, 0)


def p_ln40(ctx):
    p_ln(ctx, "ln40", "hA0_s", "s40_s", "hB0_s", gname="ln4g0", bname="ln4b0")


def p_ent1(ctx):
    p_ent(ctx, 1)


def p_entp1(ctx):
    p_entp(ctx, 1)


def p_ln31(ctx):
    p_ln(ctx, "ln31", "hB0_s", "s31_s", "hA1_s", gname="ln3g1", bname="ln3b1")


def p_path1(ctx):
    p_path(ctx, 1)


def p_ln41(ctx):
    p_ln(ctx, "ln41", "hA1_s", "s41_s", "hB1_s", gname="ln4g1", bname="ln4b1")


def p_lnout(ctx):
    p_ln(ctx, "lnout", "x3_s", "hB1_s", "outT", out_f32=True)


PHASES = {}


def _reg(name, fn, uses, defs):
    PHASES[name] = (fn, uses, defs)


_reg("init", p_init, [], [])
_reg("l0in", p_l0in, ["xT", "Wz", "Wdtin", "Wdtp", "dtb"],
     ["z_s", "xs_s", "b_s", "c_s", "dt_s"])
_reg("conv", p_conv, ["xs_s", "convw", "convb"], ["xact_s"])
_reg("scan", p_scan, ["dt_s", "xact_s", "b_s", "c_s", "Alog"], ["yssm_s"])
_reg("gate", p_gate, ["yssm_s", "xact_s", "z_s", "Dv"], ["yg_s"])
_reg("outproj", p_outproj, ["yg_s", "Wout0"], ["att0_s"])
_reg("ln0a", p_ln0a, ["xT", "att0_s"], ["x1_s"])
_reg("ffnh", p_ffnh, ["x1_s", "W1", "W2"], ["hid_s"])
_reg("ffn3", p_ffn3, ["hid_s", "W3"], ["ff0_s", "w3q_s"])
_reg("ln0b", p_ln0b, ["x1_s", "ff0_s"], ["x2_s"])
_reg("qkv", p_qkv, ["x2_s", "Wqkv"], ["qkv_s"])
_reg("attn", p_attn, ["qkv_s", "identb", "maskg", "mask0"], ["ao_s"])
_reg("wout", p_wout, ["ao_s", "Wo1"], ["att1_s"])
_reg("ln1a", p_ln1a, ["x2_s", "att1_s"], ["x3_s"])
_reg("l1fa", p_l1fa, ["x3_s", "Wamp", "Wph", "ampb", "phb", "phs0", "phs1",
                      "pathb0", "pathb1"], ["amp_s", "h0_s"])
_reg("ent0", p_ent0, ["h0_s", "Went0", "entb0"], ["e0_s"])
_reg("entp0", p_entp0, ["e0_s", "Wentp0", "entpb0"], ["s30_s"])
_reg("ln30", p_ln30, ["h0_s", "s30_s", "ln3g0", "ln3b0"], ["hA0_s"])
_reg("path0", p_path0, ["hA0_s", "Wpath00", "Wpath01"], ["s40_s"])
_reg("ln40", p_ln40, ["hA0_s", "s40_s", "ln4g0", "ln4b0"], ["hB0_s"])
_reg("ent1", p_ent1, ["hB0_s", "Went1", "entb1"], ["e1_s"])
_reg("entp1", p_entp1, ["e1_s", "Wentp1", "entpb1"], ["s31_s"])
_reg("ln31", p_ln31, ["hB0_s", "s31_s", "ln3g1", "ln3b1"], ["hA1_s"])
_reg("path1", p_path1, ["hA1_s", "Wpath10", "Wpath11"], ["s41_s"])
_reg("ln41", p_ln41, ["hA1_s", "s41_s", "ln4g1", "ln4b1"], ["hB1_s"])
_reg("lnout", p_lnout, ["x3_s", "hB1_s"], ["outT"])

ORDER = ["init", "l0in", "conv", "scan", "gate", "outproj", "ln0a",
         "ffnh", "ffn3", "ln0b", "qkv", "attn", "wout", "ln1a",
         "l1fa", "ent0", "entp0", "ln30", "path0", "ln40",
         "ent1", "entp1", "ln31", "path1", "ln41", "lnout"]
GROUPS = [list(ORDER)]


def build_group(phases, host_specs, keep_all=False):
    nc = bass.Bass("TRN2", target_bir_lowering=False, debug=False)
    used, defined = set(), set()
    for p in phases:
        _, u, d = PHASES[p]
        used.update(n for n in u if n not in defined)
        defined.update(d)
    later_used = set()
    idx = max(ORDER.index(p) for p in phases)
    for g_later in ORDER[idx + 1:]:
        later_used.update(PHASES[g_later][1])
    ext_in = {n for n in used if n in SCRATCH}
    ext_out = {n for n in defined
               if (n == "outT" or n in later_used or keep_all)}
    with ExitStack() as es:
        ctx = Ctx(nc, es, ext_in, ext_out, host_specs)
        ctx.cc = es.enter_context(
            nc.sbuf_tensor("ccpool", [128, len(CONSTS)], F32, side="right"))
        ctx.ps = [es.enter_context(nc.psum_tensor(f"psb{i}", [128, 512], F32))
                  for i in range(8)]
        for p in phases:
            PHASES[p][0](ctx)
    return nc, [n for n in ctx.dram if n not in SCRATCH or n in ext_in], \
        sorted(n for n in ext_out if n in ctx.dram)


def host_specs_of(host):
    hs = {k: (v.shape, BF16 if v.dtype == BF else F32) for k, v in host.items()}
    hs["xT"] = ((2048, T), F32)
    return hs


def kernel(**inputs):
    host = prep_host(inputs)
    hs = host_specs_of(host)
    xTs = [np.ascontiguousarray(inputs["L0_attn_hidden_states"][i].T.astype(np.float32))
           for i in range(NCORE)]
    vals = [dict() for _ in range(NCORE)]
    for g in GROUPS:
        nc, ext_in, ext_out = build_group(g, hs)
        in_maps = []
        for i in range(NCORE):
            m = {}
            for n in ext_in:
                m[n] = xTs[i] if n == "xT" else host.get(n, vals[i].get(n))
            in_maps.append(m)
        res = run_bass_kernel_spmd(nc, in_maps, core_ids=list(range(NCORE)))
        for i in range(NCORE):
            for n in ext_out:
                vals[i][n] = np.asarray(res.results[i][n])
    out = np.stack([np.ascontiguousarray(vals[i]["outT"].T).astype(np.float32)
                    for i in range(NCORE)])
    return out


# version 02
# speedup vs baseline: 1.2248x; 1.2248x over previous
import numpy as np
import ml_dtypes
from contextlib import ExitStack
import concourse.bass as bass
import concourse.mybir as mybir
from concourse.bass_utils import run_bass_kernel_spmd

F32 = mybir.dt.float32
BF16 = mybir.dt.bfloat16
I32 = mybir.dt.int32
AF = mybir.ActivationFunctionType
ALU = mybir.AluOpType
AX = mybir.AxisListType
BF = ml_dtypes.bfloat16

T = 4096
TT = 512
NT = T // TT
NCORE = 8


def _tile4(m):
    K, F = m.shape
    assert K % 128 == 0 and F % 128 == 0, (K, F)
    return np.ascontiguousarray(
        m.reshape(K // 128, 128, F // 128, 128).transpose(2, 0, 1, 3))


def _col128(v):
    return np.ascontiguousarray(v.reshape(-1, 128).T)


def prep_host(inp):
    h = {}
    f32 = np.float32
    h["Wz"] = _tile4(inp["L0_attn_in_proj_w"].T.astype(f32))
    h["Wdtin"] = _tile4(inp["L0_attn_dt_in_proj_w"].T.astype(f32))
    h["Wdtp"] = _tile4(inp["L0_attn_dt_proj_w"].T.astype(f32))
    h["dtb"] = _col128(inp["L0_attn_dt_proj_b"].astype(f32))
    h["convw"] = np.ascontiguousarray(
        inp["L0_attn_conv_w"].astype(f32).reshape(32, 128, 4).transpose(1, 0, 2).reshape(128, 128))
    h["convb"] = _col128(inp["L0_attn_conv_b"].astype(f32))
    h["Alog"] = np.ascontiguousarray(
        inp["L0_attn_A_log"].astype(f32).reshape(32, 128, 16).transpose(1, 0, 2).reshape(128, 512))
    h["Dv"] = _col128(inp["L0_attn_D"].astype(f32))
    h["Wout0"] = _tile4(inp["L0_attn_out_proj_w"].T.astype(f32))
    h["W1"] = _tile4(inp["L0_ffn_w1"].T.astype(f32))
    h["W2"] = _tile4(inp["L0_ffn_w2"].T.astype(f32))
    h["W3"] = _tile4(inp["L0_ffn_w3"].T.astype(f32))
    h["Wqkv"] = _tile4(inp["L1_attn_w_qkv"].T.astype(f32))
    h["Wo1"] = _tile4(inp["L1_attn_w_out"].T.astype(f32))
    h["Wamp"] = _tile4(inp["L1_ffn_amp_w"].astype(f32))
    h["Wph"] = _tile4(inp["L1_ffn_ph_w"].astype(f32))
    h["ampb"] = _col128(inp["L1_ffn_amp_b"].astype(f32))
    h["phb"] = _col128(inp["L1_ffn_ph_b"].astype(f32))
    for l in range(2):
        h[f"Went{l}"] = _tile4(inp["L1_ffn_ent_w"][l].astype(f32))
        h[f"Wentp{l}"] = _tile4(inp["L1_ffn_ent_pw"][l].astype(f32))
        h[f"entb{l}"] = _col128(inp["L1_ffn_ent_b"][l].astype(f32))
        h[f"entpb{l}"] = _col128(inp["L1_ffn_ent_pb"][l].astype(f32))
        for p in range(2):
            h[f"Wpath{l}{p}"] = _tile4(inp["L1_ffn_path_w"][l, p].astype(f32))
        h[f"pathb{l}"] = np.ascontiguousarray(np.concatenate(
            [_col128(inp["L1_ffn_path_b"][l, p].astype(f32)) for p in range(2)], axis=1))
        h[f"phs{l}"] = np.ascontiguousarray(np.concatenate(
            [_col128(inp["L1_ffn_phase"][l, p].astype(f32)) for p in range(2)], axis=1))
        h[f"ln3g{l}"] = _col128(inp["L1_ffn_ln3_g"][l].astype(f32))
        h[f"ln3b{l}"] = _col128(inp["L1_ffn_ln3_b"][l].astype(f32))
        h[f"ln4g{l}"] = _col128(inp["L1_ffn_ln4_g"][l].astype(f32))
        h[f"ln4b{l}"] = _col128(inp["L1_ffn_ln4_b"][l].astype(f32))
    h["identb"] = np.ascontiguousarray(np.eye(128).astype(BF))
    kk = np.arange(1024)[:, None]
    qq = np.arange(512)[None, :]
    h["maskg"] = np.ascontiguousarray(
        np.where((kk >= qq) & (kk <= qq + 512), 0.0, -1e9).astype(BF))
    h["mask0"] = np.ascontiguousarray(
        np.where(kk[:512] <= qq, 0.0, -1e9).astype(BF))
    return h


_S = {}
for _n, _r in [("z_s", 4096), ("xs_s", 1024), ("b_s", 1024), ("c_s", 4096),
               ("dt_s", 4096), ("xact_s", 4096), ("yssm_s", 4096), ("yg_s", 4096),
               ("att0_s", 2048), ("x1_s", 2048), ("hid_s", 5632), ("ff0_s", 2048),
               ("x2_s", 2048), ("qkv_s", 6144), ("ao_s", 2048), ("att1_s", 2048),
               ("x3_s", 2048), ("amp_s", 2048), ("h0_s", 2048),
               ("e0_s", 8192), ("e1_s", 8192), ("s30_s", 2048), ("s31_s", 2048),
               ("hA0_s", 2048), ("hA1_s", 2048), ("s40_s", 2048), ("s41_s", 2048),
               ("hB0_s", 2048), ("hB1_s", 2048)]:
    _S[_n] = (_r, T, BF16, BF)
_S["w3q_s"] = (2048, 5632, BF16, BF)
_S["outT"] = (2048, T, F32, np.float32)
SCRATCH = _S



class Ctx:
    def __init__(self, nc, es, ext_in, ext_out, host_specs):
        self.nc = nc
        self.es = es
        self.blk = None
        self.ext_in = ext_in
        self.ext_out = ext_out
        self.host_specs = host_specs
        self.sem = {}
        self.c = {}
        for s in ["q0", "q2", "pe", "act", "dve", "pool", "pf", "wf", "st", "x0", "x1"]:
            self.sem[s] = es.enter_context(nc.semaphore(s))
            self.c[s] = 0
        self.dram = {}
        self.streams = {}
        self._uid = 0
        self.cc = None
        self.ps = None
        self.keep = {}

    def uid(self):
        self._uid += 1
        return self._uid

    def ap(self, name):
        if name in self.dram:
            return self.dram[name]
        if name in SCRATCH:
            r, c, dt, _ = SCRATCH[name]
            kind = ("ExternalInput" if name in self.ext_in else
                    "ExternalOutput" if name in self.ext_out else "Internal")
            t = self.nc.dram_tensor(name, [r, c], dt, kind=kind)
        else:
            shape, dt = self.host_specs[name]
            t = self.nc.dram_tensor(name, list(shape), dt, kind="ExternalInput")
        self.dram[name] = t.ap()
        return self.dram[name]

    def sb(self, es, tag, p, f, dt, side="left"):
        return es.enter_context(
            self.nc.sbuf_tensor(f"{tag}{self.uid()}", [p, f], dt, side=side))

    def psb(self, tag, p, f, dt):
        if tag not in self.keep:
            self.keep[tag] = self.sb(self.es, tag, p, f, dt, side="right")
        return self.keep[tag]

    def inc(self, ins, s, n=1):
        ins.then_inc(self.sem[s], n)
        self.c[s] += n
        return self.c[s]

    def stream(self, name, nslots):
        if name not in self.streams:
            self.streams[name] = Stream(self, name, nslots)
        st = self.streams[name]
        assert st.n == nslots, (name, nslots, st.n)
        return st


class Stream:

    def __init__(self, ctx, name, nslots):
        self.n = nslots
        self.sems = [ctx.es.enter_context(ctx.nc.semaphore(f"{name}{i}"))
                     for i in range(nslots)]
        self.cnt = [0] * nslots

    def put(self, e, out, in_, slot):
        d = e.dma_start(out=out, in_=in_)
        d.then_inc(self.sems[slot], 16)
        self.cnt[slot] += 16
        return (slot, self.cnt[slot])

    def wait(self, e, tok):
        e.wait_ge(self.sems[tok[0]], tok[1])

    def base(self):
        return list(self.cnt)


CONSTS = [0.0, 1.0, 0.5, -0.5, np.pi / 2, 1e-5, -1e9, 1.0 / 2048.0, np.pi,
          1.0 / (2 * np.pi), 2 * np.pi, 0.088388347648318447, 1e-8,
          1.0 / 5632.0]


def ccol(ctx, v):
    return ctx.cc[:, CONSTS.index(v):CONSTS.index(v) + 1]


def crow(ctx, v):
    return ctx.cc[0:1, CONSTS.index(v):CONSTS.index(v) + 1]


def slab_loader(ctx, wname, nk, nm, wbuf, lookahead=2, ntt=NT):
    W = ctx.ap(wname)
    wt = ctx.stream("wt", 2)
    bw = wt.base()
    base_pe = ctx.c["pe"]
    blk = ctx.blk

    @blk.gpsimd
    def _(e):
        for m in range(nm):
            if m >= lookahead:
                e.wait_ge(ctx.sem["pe"], base_pe + (m - lookahead + 1) * ntt)
            wt.put(e, wbuf[:, (m % 2) * nk * 128:(m % 2 + 1) * nk * 128]
                   .rearrange("p (k c) -> p k c", k=nk),
                   W[m].rearrange("k p c -> p k c"), m % 2)
    return lambda m: (m % 2, bw[m % 2] + 16 * (m // 2 + 1))


def load_cols(ctx, e, name, dst, q="q0"):
    d = e.dma_start(out=dst[:], in_=ctx.ap(name)[:, :])
    return ctx.inc(d, q, 16)


def mm_section(ctx, nk, nm, wbuf, rhs, w_ready, rhs_wait, bank_free, nbank=4,
               lookahead=2, rhs_stride=T, ntt=NT, c0=0):
    blk = ctx.blk
    base_pe = ctx.c["pe"]

    @blk.tensor
    def _(e):
        if rhs_wait is not None:
            e.wait_ge(ctx.sem[rhs_wait[0]], rhs_wait[1])
        for m in range(nm):
            tok = w_ready(m)
            e.wait_ge(ctx.streams["wt"].sems[tok[0]], tok[1])
            for t in range(ntt):
                j = m * ntt + t
                if j >= nbank:
                    sn, cv = bank_free(j - nbank)
                    e.wait_ge(ctx.sem[sn], cv)
                ps = ctx.ps[j % nbank]
                for k in range(nk):
                    ins = e.matmul(
                        out=ps[:],
                        lhsT=wbuf[:, (m % lookahead) * nk * 128 + k * 128:
                                  (m % lookahead) * nk * 128 + (k + 1) * 128],
                        rhs=rhs[:, k * rhs_stride + c0 + t * TT:
                                k * rhs_stride + c0 + t * TT + TT],
                        start=(k == 0), stop=(k == nk - 1))
                ctx.inc(ins, "pe")
    return base_pe


def store_section(ctx, njobs, ready_sem, ready_of, src_of, dst_of, nslots=4,
                  eng=None):
    blk = ctx.blk
    st = ctx.stream("st", 4)
    dec = getattr(blk, eng or "sync")

    @dec
    def _(e):
        for j in range(njobs):
            e.wait_ge(ctx.sem[ready_sem], ready_of(j))
            st.put(e, dst_of(j), src_of(j), j % nslots)
        for sl in range(nslots):
            e.wait_ge(st.sems[sl], st.cnt[sl])


def st_tok(ctx, base, j, nslots=4):
    return (j % nslots, base[j % nslots] + 16 * (j // nslots + 1))



def p_init(ctx):
    with ctx.nc.Block("init") as blk:
        ctx.blk = blk

        @blk.vector
        def _(e):
            for i, v in enumerate(CONSTS):
                e.memset(ctx.cc[:, i:i + 1], float(v))


def generic_linear(ctx, name, wname, nk, nm, rhs, rhs_wait, dst_of, epi=None,
                   nbank=4, ntt=NT, rhs_stride=T, c0=0, epi_eng="vector"):
    with ctx.nc.Block(name) as blk:
        ctx.blk = blk
        with ExitStack() as les:
            wbuf = ctx.sb(les, "wb", 128, 2 * nk * 128, BF16)
            ob = ctx.sb(les, "ob", 128, 4 * TT, BF16)
            w_ready = slab_loader(ctx, wname, nk, nm, wbuf, ntt=ntt)
            base_dve = ctx.c["dve"]
            base_pe = mm_section(ctx, nk, nm, wbuf, rhs, w_ready, rhs_wait,
                                 lambda j: ("dve", base_dve + j + 1), nbank=nbank,
                                 rhs_stride=rhs_stride, ntt=ntt, c0=c0)
            base_st = ctx.stream("st", 4).base()
            dec = getattr(blk, epi_eng)

            @dec
            def _(e):
                for j in range(nm * ntt):
                    e.wait_ge(ctx.sem["pe"], base_pe + j + 1)
                    if j >= 4:
                        ctx.streams["st"].wait(e, st_tok(ctx, base_st, j - 4))
                    buf = ob[:, (j % 4) * TT:(j % 4 + 1) * TT]
                    if epi is None:
                        ins = e.tensor_copy(out=buf, in_=ctx.ps[j % nbank][:])
                    else:
                        ins = epi(e, j // ntt, j % ntt, ctx.ps[j % nbank], buf)
                    ctx.inc(ins, "dve")

            store_section(ctx, nm * ntt, "dve", lambda j: base_dve + j + 1,
                          lambda j: ob[:, (j % 4) * TT:(j % 4 + 1) * TT],
                          lambda j: dst_of(j // ntt, j % ntt))


def rhs_load_block(ctx, name, src, nk, rhs, cast=False, c0=0, ncols=T):
    with ctx.nc.Block(name) as blk:
        ctx.blk = blk
        src_ap = ctx.ap(src)
        qn = "q2" if cast else "q0"
        eng = blk.gpsimd if cast else blk.sync

        @eng
        def _(e):
            for k in range(nk):
                d = e.dma_start(out=rhs[:, k * ncols:(k + 1) * ncols],
                                in_=src_ap[128 * k:128 * (k + 1), c0:c0 + ncols])
                ctx.inc(d, qn, 16)
    return (qn, ctx.c[qn])


def p_l0in(ctx):
    with ExitStack() as les:
        rhs = ctx.sb(les, "xres", 128, 16 * T, BF16)
        rw = rhs_load_block(ctx, "ldx", "xT", 16, rhs, cast=True)

        dsts = []
        for m in range(80):
            if m < 32:
                dsts.append(("z_s", m))
            elif m < 40:
                dsts.append(("xs_s", m - 32))
            elif m < 48:
                dsts.append(("b_s", m - 40))
            else:
                dsts.append(("c_s", m - 48))

        def dst_of(m, t):
            nm_, mm = dsts[m]
            return ctx.ap(nm_)[128 * mm:128 * mm + 128, t * TT:(t + 1) * TT]

        generic_linear(ctx, "inproj", "Wz", 16, 80, rhs, rw, dst_of)

        dtm = ctx.psb("dtmid", 128, T, BF16)
        with ctx.nc.Block("dtmid") as blk:
            ctx.blk = blk
            with ExitStack() as les2:
                wbuf = ctx.sb(les2, "wb", 128, 2 * 16 * 128, BF16)
                w_ready = slab_loader(ctx, "Wdtin", 16, 1, wbuf)
                base_dve = ctx.c["dve"]
                base_pe = mm_section(ctx, 16, 1, wbuf, rhs, w_ready, rw,
                                     lambda j: ("dve", base_dve + j + 1))

                @blk.vector
                def _(e):
                    for j in range(NT):
                        e.wait_ge(ctx.sem["pe"], base_pe + j + 1)
                        ins = e.tensor_copy(out=dtm[:, j * TT:(j + 1) * TT],
                                            in_=ctx.ps[j % 4][:])
                        ctx.inc(ins, "dve")

    dtm = ctx.keep["dtmid"]
    with ctx.nc.Block("dt") as blk:
        ctx.blk = blk
        with ExitStack() as les:
            wbuf = ctx.sb(les, "wb", 128, 2 * 128, BF16)
            bia = ctx.sb(les, "bia", 128, 32, F32)
            t1 = ctx.sb(les, "t1", 128, 4 * TT, F32)
            t2 = ctx.sb(les, "t2", 128, 4 * TT, F32)
            ob = ctx.sb(les, "ob", 128, 4 * TT, BF16)

            @blk.sync
            def _(e):
                load_cols(ctx, e, "dtb", bia)
            bia_n = ctx.c["q0"]

            w_ready = slab_loader(ctx, "Wdtp", 1, 32, wbuf)
            base_act = ctx.c["act"]
            base_pe = mm_section(ctx, 1, 32, wbuf, dtm, w_ready, ("q0", bia_n),
                                 lambda j: ("act", base_act + 4 * (j + 1)))
            base_dve = ctx.c["dve"]
            base_st = ctx.stream("st", 4).base()

            @blk.scalar
            def _(e):
                e.wait_ge(ctx.sem["q0"], bia_n)
                for j in range(32 * NT):
                    m = j // NT
                    e.wait_ge(ctx.sem["pe"], base_pe + j + 1)
                    if j >= 4:
                        e.wait_ge(ctx.sem["dve"], base_dve + j - 3)
                    ps = ctx.ps[j % 4]
                    ta = t1[:, (j % 4) * TT:(j % 4 + 1) * TT]
                    tb = t2[:, (j % 4) * TT:(j % 4 + 1) * TT]
                    b = bia[:, m:m + 1]
                    ins = e.activation(out=ta, in_=ps[:], func=AF.Abs, bias=b,
                                       scale=ccol(ctx, 1.0))
                    e.wait_ge(ctx.sem["act"], ctx.inc(ins, "act"))
                    ins = e.activation(out=ta, in_=ta, func=AF.Exp, scale=-1.0)
                    e.wait_ge(ctx.sem["act"], ctx.inc(ins, "act"))
                    ins = e.activation(out=ta, in_=ta, func=AF.Ln, bias=ccol(ctx, 1.0),
                                       scale=ccol(ctx, 1.0))
                    ctx.inc(ins, "act")
                    ins = e.activation(out=tb, in_=ps[:], func=AF.Relu, bias=b,
                                       scale=ccol(ctx, 1.0))
                    ctx.inc(ins, "act")

            @blk.vector
            def _(e):
                for j in range(32 * NT):
                    e.wait_ge(ctx.sem["act"], base_act + 4 * (j + 1))
                    if j >= 4:
                        ctx.streams["st"].wait(e, st_tok(ctx, base_st, j - 4))
                    o = ob[:, (j % 4) * TT:(j % 4 + 1) * TT]
                    ins = e.tensor_add(out=o, in0=t1[:, (j % 4) * TT:(j % 4 + 1) * TT],
                                       in1=t2[:, (j % 4) * TT:(j % 4 + 1) * TT])
                    ctx.inc(ins, "dve")

            store_section(ctx, 32 * NT, "dve", lambda j: base_dve + j + 1,
                          lambda j: ob[:, (j % 4) * TT:(j % 4 + 1) * TT],
                          lambda j: ctx.ap("dt_s")[128 * (j // NT):128 * (j // NT) + 128,
                                                   (j % NT) * TT:(j % NT + 1) * TT])


def p_conv(ctx):
    with ctx.nc.Block("conv") as blk:
        ctx.blk = blk
        with ExitStack() as les:
            xf = ctx.sb(les, "xf", 128, 2 * (T + 3), BF16)
            acc = ctx.sb(les, "acc", 128, 2 * T, F32)
            ob = ctx.sb(les, "ob", 128, 2 * T, BF16)
            wcv = ctx.sb(les, "wcv", 128, 128, F32)
            bcv = ctx.sb(les, "bcv", 128, 32, F32)
            xs = ctx.ap("xs_s")
            ld = ctx.stream("ld", 4)
            base_ld = ld.base()
            base_q0 = ctx.c["q0"]
            base_dve = ctx.c["dve"]
            base_act = ctx.c["act"]
            base_st = ctx.stream("st", 4).base()

            @blk.sync
            def _(e):
                load_cols(ctx, e, "convw", wcv)
                load_cols(ctx, e, "convb", bcv)
                for k in range(32):
                    if k >= 2:
                        e.wait_ge(ctx.sem["dve"], base_dve + 1 + 4 * (k - 1))
                    for g in range(2):
                        src = (xs[32 * k + 16 * g:32 * k + 16 * g + 16, :]
                               .unsqueeze(0).broadcast_to([4, 16, T]))
                        ld.put(e, xf[64 * g:64 * (g + 1),
                                     (k % 2) * (T + 3) + 3:(k % 2 + 1) * (T + 3)],
                               src, k % 2)

            @blk.vector
            def _(e):
                e.memset(xf[:, 0:3], 0.0)
                ins = e.memset(xf[:, T + 3:T + 6], 0.0)
                e.wait_ge(ctx.sem["dve"], ctx.inc(ins, "dve"))
                base_dve2 = ctx.c["dve"]
                for k in range(32):
                    e.wait_ge(ctx.sem["q0"], base_q0 + 32)
                    e.wait_ge(ld.sems[k % 2], base_ld[k % 2] + 32 * (k // 2 + 1))
                    if k >= 2:
                        e.wait_ge(ctx.sem["act"], base_act + k - 1)
                    x0 = (k % 2) * (T + 3)
                    a = acc[:, (k % 2) * T:(k % 2 + 1) * T]
                    ins = e.tensor_scalar(out=a, in0=xf[:, x0:x0 + T],
                                          scalar1=wcv[:, 4 * k:4 * k + 1],
                                          scalar2=None, op0=ALU.mult)
                    e.wait_ge(ctx.sem["dve"], ctx.inc(ins, "dve"))
                    for j in range(1, 4):
                        ins = e.scalar_tensor_tensor(
                            out=a, in0=xf[:, x0 + j:x0 + j + T],
                            scalar=wcv[:, 4 * k + j:4 * k + j + 1], in1=a,
                            op0=ALU.mult, op1=ALU.add)
                        n = ctx.inc(ins, "dve")
                        if j < 3:
                            e.wait_ge(ctx.sem["dve"], n)

            stq = ctx.stream("st", 4)

            @blk.scalar
            def _(e):
                for k in range(32):
                    e.wait_ge(ctx.sem["dve"], base_dve + 1 + 4 * (k + 1))
                    if k >= 2:
                        stq.wait(e, st_tok(ctx, base_st, k - 2, 2))
                    ins = e.activation(out=ob[:, (k % 2) * T:(k % 2 + 1) * T],
                                       in_=acc[:, (k % 2) * T:(k % 2 + 1) * T],
                                       func=AF.Silu, bias=bcv[:, k:k + 1],
                                       scale=ccol(ctx, 1.0))
                    n = ctx.inc(ins, "act")
                    e.wait_ge(ctx.sem["act"], n)
                    stq.put(e, ctx.ap("xact_s")[128 * k:128 * (k + 1), :],
                            ob[:, (k % 2) * T:(k % 2 + 1) * T], k % 2)
                for sl in range(2):
                    e.wait_ge(stq.sems[sl], stq.cnt[sl])


def p_scan(ctx):
    with ctx.nc.Block("scan") as blk:
        ctx.blk = blk
        with ExitStack() as les:
            dtk = ctx.sb(les, "dtk", 128, 2 * T, BF16)
            xak = ctx.sb(les, "xak", 128, 2 * T, BF16)
            dxk = ctx.sb(les, "dxk", 128, 2 * T, BF16)
            bnb = ctx.sb(les, "bnb", 128, 2 * T, BF16)
            cnb = ctx.sb(les, "cnb", 128, 2 * T, BF16)
            ab = ctx.sb(les, "ab", 128, 2 * T, F32)
            ub = ctx.sb(les, "ub", 128, 2 * T, BF16)
            hb = ctx.sb(les, "hb", 128, 2 * T, BF16)
            pr = ctx.sb(les, "pr", 128, 2 * T, BF16)
            ysb = ctx.sb(les, "ysb", 128, 2 * T, BF16)
            asb = ctx.sb(les, "asb", 128, 512, F32)
            idb = ctx.sb(les, "idb", 128, 128, BF16)
            bs = ctx.ap("b_s")
            cs = ctx.ap("c_s")

            @blk.sync
            def _(e):
                d = e.dma_start(out=asb[:], in_=ctx.ap("Alog")[:, :])
                ctx.inc(d, "q0", 16)
                d = e.dma_start(out=idb[:], in_=ctx.ap("identb")[:, :])
                ctx.inc(d, "q0", 16)
            pre_n = ctx.c["q0"]

            @blk.scalar
            def _(e):
                e.wait_ge(ctx.sem["q0"], pre_n)
                ins = e.activation(out=asb[:], in_=asb[:], func=AF.Exp)
                ctx.inc(ins, "act")
            aexp_n = ctx.c["act"]

            @blk.vector
            def _(e):
                e.wait_ge(ctx.sem["act"], aexp_n)
                ins = e.tensor_scalar_mul(out=asb[:], in0=asb[:], scalar1=-1.0)
                ctx.inc(ins, "dve")

            base = {sn: ctx.c[sn] for sn in
                    ["q0", "act", "dve", "pe", "wf", "x1"]}
            sd = ctx.stream("d", 2)
            sb_ = ctx.stream("b", 2)
            scb = ctx.stream("cb", 2)
            stq = ctx.stream("st", 4)
            bd, bb, bcb = sd.base(), sb_.base(), scb.base()
            bst = stq.base()
            d_tok = lambda k: (k % 2, bd[k % 2] + 32 * (k // 2 + 1))
            b_tok = lambda i: (i % 2, bb[i % 2] + 32 * (i // 2 + 1))
            c_tok = lambda i: (i % 2, bcb[i % 2] + 16 * (i // 2 + 1))

            @blk.sync
            def _(e):
                for k in range(32):
                    if k >= 2:
                        e.wait_ge(ctx.sem["act"], base["act"] + (k - 1) * 16)
                        e.wait_ge(ctx.sem["x1"], base["x1"] + k - 1)
                    sd.put(e, dtk[:, (k % 2) * T:(k % 2 + 1) * T],
                           ctx.ap("dt_s")[128 * k:128 * (k + 1), :], k % 2)
                    sd.put(e, xak[:, (k % 2) * T:(k % 2 + 1) * T],
                           ctx.ap("xact_s")[128 * k:128 * (k + 1), :], k % 2)
                    for n in range(16):
                        i = 16 * k + n
                        x0 = (i % 2) * T
                        if i >= 2:
                            e.wait_ge(ctx.sem["dve"], base["dve"] + 3 * (i - 2) + 1)
                        r0 = 32 * k + n
                        sb_.put(e, bnb[0:64, x0:x0 + T],
                                bs[r0:r0 + 1, :].partition_broadcast(64), i % 2)
                        sb_.put(e, bnb[64:128, x0:x0 + T],
                                bs[r0 + 16:r0 + 17, :].partition_broadcast(64), i % 2)

            @blk.scalar
            def _(e):
                for k in range(32):
                    for n in range(16):
                        i = 16 * k + n
                        x0 = (i % 2) * T
                        if i >= 2:
                            e.wait_ge(ctx.sem["dve"], base["dve"] + 3 * (i - 2) + 3)
                        src = (cs[128 * k:128 * (k + 1), :]
                               .rearrange("(g s) n -> g s n", s=16)[:, n, :]
                               .unsqueeze(1).broadcast_to([8, 16, T]))
                        scb.put(e, cnb[:, x0:x0 + T], src, i % 2)
                        if n == 0:
                            sd.wait(e, d_tok(k))
                        e.wait_ge(ctx.sem["dve"],
                                  base["dve"] + (3 * (i - 2) + 2 if i >= 2 else 0))
                        ins = e.activation(out=ab[:, x0:x0 + T],
                                           in_=dtk[:, (k % 2) * T:(k % 2 + 1) * T],
                                           func=AF.Exp, scale=asb[:, i:i + 1])
                        ctx.inc(ins, "act")
                    e.wait_ge(ctx.sem["pe"], base["pe"] + 16 * (k + 1))
                    if k >= 2:
                        stq.wait(e, (k % 2, bst[k % 2] + 16 * (k // 2)))
                    for t in range(NT):
                        ins = e.activation(out=ysb[:, (k % 2) * T + t * TT:
                                               (k % 2) * T + (t + 1) * TT],
                                           in_=ctx.ps[t][:], func=AF.Copy)
                        ctx.inc(ins, "wf")
                    e.wait_ge(ctx.sem["wf"], ctx.c["wf"])
                    stq.put(e, ctx.ap("yssm_s")[128 * k:128 * (k + 1), :],
                            ysb[:, (k % 2) * T:(k % 2 + 1) * T], k % 2)
                for sl in range(2):
                    e.wait_ge(stq.sems[sl], stq.cnt[sl])

            @blk.vector
            def _(e):
                for k in range(32):
                    sd.wait(e, d_tok(k))
                    if k >= 2:
                        e.wait_ge(ctx.sem["dve"], base["dve"] + 3 * (16 * (k - 1)))
                    ins = e.tensor_tensor(out=dxk[:, (k % 2) * T:(k % 2 + 1) * T],
                                          in0=dtk[:, (k % 2) * T:(k % 2 + 1) * T],
                                          in1=xak[:, (k % 2) * T:(k % 2 + 1) * T],
                                          op=ALU.mult)
                    e.wait_ge(ctx.sem["x1"], ctx.inc(ins, "x1"))
                    for n in range(16):
                        i = 16 * k + n
                        x0 = (i % 2) * T
                        sb_.wait(e, b_tok(i))
                        if i >= 2:
                            e.wait_ge(ctx.sem["dve"], base["dve"] + 3 * (i - 1))
                        ins = e.tensor_tensor(out=ub[:, x0:x0 + T],
                                              in0=dxk[:, (k % 2) * T:(k % 2 + 1) * T],
                                              in1=bnb[:, x0:x0 + T], op=ALU.mult)
                        e.wait_ge(ctx.sem["dve"], ctx.inc(ins, "dve"))
                        e.wait_ge(ctx.sem["act"], base["act"] + i + 1)
                        ins = e.tensor_tensor_scan(out=hb[:, x0:x0 + T],
                                                   data0=ab[:, x0:x0 + T],
                                                   data1=ub[:, x0:x0 + T],
                                                   initial=0.0, op0=ALU.mult,
                                                   op1=ALU.add)
                        e.wait_ge(ctx.sem["dve"], ctx.inc(ins, "dve"))
                        scb.wait(e, c_tok(i))
                        if i >= 2:
                            e.wait_ge(ctx.sem["pe"], base["pe"] + i - 1)
                        ins = e.tensor_tensor(out=pr[:, x0:x0 + T], in0=hb[:, x0:x0 + T],
                                              in1=cnb[:, x0:x0 + T], op=ALU.mult)
                        ctx.inc(ins, "dve")

            @blk.tensor
            def _(e):
                e.wait_ge(ctx.sem["q0"], pre_n)
                for k in range(32):
                    for n in range(16):
                        i = 16 * k + n
                        x0 = (i % 2) * T
                        e.wait_ge(ctx.sem["dve"], base["dve"] + 3 * i + 3)
                        if n == 0 and k >= 1:
                            e.wait_ge(ctx.sem["wf"], base["wf"] + 8 * k)
                        for t in range(NT):
                            ins = e.matmul(out=ctx.ps[t][:], lhsT=idb[:],
                                           rhs=pr[:, x0 + t * TT:x0 + (t + 1) * TT],
                                           start=(n == 0), stop=(n == 15))
                        ctx.inc(ins, "pe")


def p_gate(ctx):
    with ctx.nc.Block("gate") as blk:
        ctx.blk = blk
        with ExitStack() as les:
            ytk = ctx.sb(les, "ytk", 128, 2 * T, BF16)
            xak = ctx.sb(les, "xak", 128, 2 * T, BF16)
            zk = ctx.sb(les, "zk", 128, 2 * T, BF16)
            szk = ctx.sb(les, "szk", 128, 2 * T, BF16)
            t1 = ctx.sb(les, "t1", 128, 2 * T, BF16)
            ob = ctx.sb(les, "ob", 128, 2 * T, BF16)
            dvc = ctx.sb(les, "dvc", 128, 32, F32)
            ld = ctx.stream("ld", 4)
            base_ld = ld.base()
            ld_tok = lambda k: (k % 2, base_ld[k % 2] + 48 * (k // 2 + 1))
            stq = ctx.stream("st", 4)
            base_st = stq.base()
            base_q0 = ctx.c["q0"]
            base_act = ctx.c["act"]
            base_dve = ctx.c["dve"]

            @blk.sync
            def _(e):
                load_cols(ctx, e, "Dv", dvc)
                for k in range(32):
                    if k >= 2:
                        e.wait_ge(ctx.sem["dve"], base_dve + 2 * (k - 1))
                    for nm_, dst in [("yssm_s", ytk), ("xact_s", xak), ("z_s", zk)]:
                        ld.put(e, dst[:, (k % 2) * T:(k % 2 + 1) * T],
                               ctx.ap(nm_)[128 * k:128 * (k + 1), :], k % 2)

            @blk.scalar
            def _(e):
                e.wait_ge(ctx.sem["q0"], base_q0 + 16)
                for k in range(32):
                    ld.wait(e, ld_tok(k))
                    if k >= 2:
                        e.wait_ge(ctx.sem["dve"], base_dve + 2 * (k - 1))
                    ins = e.activation(out=szk[:, (k % 2) * T:(k % 2 + 1) * T],
                                       in_=zk[:, (k % 2) * T:(k % 2 + 1) * T],
                                       func=AF.Silu)
                    ctx.inc(ins, "act")
                    if k >= 2:
                        e.wait_ge(ctx.sem["dve"], base_dve + 2 * (k - 1) + 2)
                        stq.put(e, ctx.ap("yg_s")[128 * (k - 2):128 * (k - 1), :],
                                ob[:, (k % 2) * T:(k % 2 + 1) * T], k % 2)
                for k in (30, 31):
                    e.wait_ge(ctx.sem["dve"], base_dve + 2 * (k + 1))
                    stq.put(e, ctx.ap("yg_s")[128 * k:128 * (k + 1), :],
                            ob[:, (k % 2) * T:(k % 2 + 1) * T], k % 2)
                for sl in range(2):
                    e.wait_ge(stq.sems[sl], stq.cnt[sl])

            @blk.vector
            def _(e):
                e.wait_ge(ctx.sem["q0"], base_q0 + 16)
                for k in range(32):
                    ld.wait(e, ld_tok(k))
                    ins = e.scalar_tensor_tensor(
                        out=t1[:, (k % 2) * T:(k % 2 + 1) * T],
                        in0=xak[:, (k % 2) * T:(k % 2 + 1) * T],
                        scalar=dvc[:, k:k + 1],
                        in1=ytk[:, (k % 2) * T:(k % 2 + 1) * T],
                        op0=ALU.mult, op1=ALU.add)
                    e.wait_ge(ctx.sem["dve"], ctx.inc(ins, "dve"))
                    e.wait_ge(ctx.sem["act"], base_act + k + 1)
                    if k >= 2:
                        stq.wait(e, (k % 2, base_st[k % 2] + 16 * (k // 2)))
                    ins = e.tensor_tensor(out=ob[:, (k % 2) * T:(k % 2 + 1) * T],
                                          in0=t1[:, (k % 2) * T:(k % 2 + 1) * T],
                                          in1=szk[:, (k % 2) * T:(k % 2 + 1) * T],
                                          op=ALU.mult)
                    ctx.inc(ins, "dve")


def p_outproj(ctx):
    with ExitStack() as les:
        rhs = ctx.sb(les, "ygres", 128, 32 * 2048, BF16)
        for half in range(2):
            rw = rhs_load_block(ctx, f"ldyg{half}", "yg_s", 32, rhs,
                                c0=half * 2048, ncols=2048)

            def dst_of(m, t, _h=half):
                return ctx.ap("att0_s")[128 * m:128 * m + 128,
                                        _h * 2048 + t * TT:_h * 2048 + (t + 1) * TT]

            generic_linear(ctx, f"outproj{half}", "Wout0", 32, 16, rhs, rw,
                           dst_of, ntt=4, rhs_stride=2048)


def p_ln(ctx, name, srcA, srcB, dst, srcA_f32=False, gname=None, bname=None,
         out_f32=False):
    nk = 16
    with ctx.nc.Block(name) as blk:
        ctx.blk = blk
        with ExitStack() as les:
            at = ctx.sb(les, "at", 128, 4 * TT, F32 if srcA_f32 else BF16)
            bt = ctx.sb(les, "bt", 128, 4 * TT, BF16)
            s1 = ctx.sb(les, "s1", 128, 2 * nk * TT, BF16)
            sq = ctx.sb(les, "sq", 128, 2 * nk * TT, BF16)
            rows = ctx.sb(les, "rw", 1, 2 * 5 * TT, F32)
            onec = ctx.sb(les, "on", 128, 1, BF16)
            onef = ctx.sb(les, "onf", 1, 128, F32)
            rstdb = ctx.sb(les, "rsb", 128, 2 * TT, BF16)
            mbb = ctx.sb(les, "mbb", 128, 2 * TT, BF16)
            tmq = ctx.sb(les, "tmq", 128, 2 * TT, BF16)
            obx = ctx.sb(les, "obx", 128, 2 * nk * TT, F32 if out_f32 else BF16)
            gcols = ctx.sb(les, "gc", 128, 32, F32)
            apA, apB = ctx.ap(srcA), ctx.ap(srcB)
            ld = ctx.stream("ld", 4)
            stq = ctx.stream("st", 4)
            bld = ld.base()
            bst = stq.base()
            base = {sn: ctx.c[sn] for sn in
                    ["q0", "pe", "act", "dve", "pf", "wf", "x0", "x1"]}
            ld_tok = lambda i: (i % 4, bld[i % 4] + 32 * (i // 4 + 1))
            pe_stats, pe_bcast = {}, {}
            _c = base["pe"]
            for _t in range(NT + 1):
                if _t < NT:
                    for _k in range(nk):
                        _c += 1
                        pe_stats[(_t, _k)] = _c
                if _t >= 1:
                    _c += 1
                    pe_bcast[_t - 1] = _c
            mark = {}

            def s1v(t, k):
                return s1[:, (t % 2) * nk * TT + k * TT:(t % 2) * nk * TT + (k + 1) * TT]

            def sqv(t, k):
                return sq[:, (t % 2) * nk * TT + k * TT:(t % 2) * nk * TT + (k + 1) * TT]

            def obv(t, k):
                return obx[:, (t % 2) * nk * TT + k * TT:(t % 2) * nk * TT + (k + 1) * TT]

            def rw_(t, r):
                return rows[0:1, (t % 2) * 5 * TT + r * TT:(t % 2) * 5 * TT + (r + 1) * TT]

            @blk.sync
            def _(e):
                if gname:
                    load_cols(ctx, e, gname, gcols[:, 0:16])
                    load_cols(ctx, e, bname, gcols[:, 16:32])
                for t in range(NT):
                    for k in range(nk):
                        i = 16 * t + k
                        if i >= 4:
                            e.wait_ge(ctx.sem["x0"], base["x0"] + i - 3)
                        ld.put(e, at[:, (i % 4) * TT:(i % 4 + 1) * TT],
                               apA[128 * k:128 * (k + 1), t * TT:(t + 1) * TT], i % 4)
                        ld.put(e, bt[:, (i % 4) * TT:(i % 4 + 1) * TT],
                               apB[128 * k:128 * (k + 1), t * TT:(t + 1) * TT], i % 4)
            ngl = 32 if gname else 0

            @blk.vector
            def _(e):
                e.memset(onec[:], 1.0)
                ins = e.memset(onef[:], 1.0)
                ctx.inc(ins, "x1")
                for t in range(NT + 2):
                    if t >= 2:
                        t2 = t - 2
                        if t2 >= 2:
                            for sl in (2 * (t2 % 2), 2 * (t2 % 2) + 1):
                                e.wait_ge(stq.sems[sl],
                                          bst[sl] + 128 * ((t2 - 2) // 2 + 1))
                        for k in range(nk):
                            tq = tmq[:, (k % 2) * TT:(k % 2 + 1) * TT]
                            ins = e.tensor_tensor(
                                out=tq, in0=s1v(t2, k),
                                in1=rstdb[:, (t2 % 2) * TT:(t2 % 2 + 1) * TT],
                                op=ALU.mult)
                            e.wait_ge(ctx.sem["dve"], ctx.inc(ins, "dve"))
                            ins = e.tensor_tensor(
                                out=obv(t2, k), in0=tq,
                                in1=mbb[:, (t2 % 2) * TT:(t2 % 2 + 1) * TT],
                                op=ALU.subtract)
                            if gname:
                                e.wait_ge(ctx.sem["dve"], ctx.inc(ins, "dve"))
                                ins = e.tensor_scalar(out=obv(t2, k), in0=obv(t2, k),
                                                      scalar1=gcols[:, k:k + 1],
                                                      scalar2=gcols[:, 16 + k:17 + k],
                                                      op0=ALU.mult, op1=ALU.add)
                            ctx.inc(ins, "pf")
                    if t < NT:
                        for k in range(nk):
                            i = 16 * t + k
                            ld.wait(e, ld_tok(i))
                            ins = e.tensor_tensor(out=s1v(t, k),
                                                  in0=at[:, (i % 4) * TT:(i % 4 + 1) * TT],
                                                  in1=bt[:, (i % 4) * TT:(i % 4 + 1) * TT],
                                                  op=ALU.add)
                            ctx.inc(ins, "x0")
                    if 1 <= t <= NT:
                        t1 = t - 1
                        e.wait_ge(ctx.sem["pe"], pe_stats[(t1, nk - 1)])
                        mu, ms, var = rw_(t1, 0), rw_(t1, 1), rw_(t1, 2)
                        mb = rw_(t1, 4)
                        ins = e.tensor_scalar_mul(out=mu, in0=ctx.ps[2 * (t1 % 2)][0:1, :],
                                                  scalar1=1.0 / 2048.0)
                        e.wait_ge(ctx.sem["dve"], ctx.inc(ins, "dve"))
                        ins = e.tensor_scalar_mul(out=ms,
                                                  in0=ctx.ps[2 * (t1 % 2) + 1][0:1, :],
                                                  scalar1=1.0 / 2048.0)
                        e.wait_ge(ctx.sem["dve"], ctx.inc(ins, "dve"))
                        ins = e.tensor_tensor(out=var, in0=mu, in1=mu, op=ALU.mult)
                        e.wait_ge(ctx.sem["dve"], ctx.inc(ins, "dve"))
                        ins = e.tensor_sub(out=var, in0=ms, in1=var)
                        e.wait_ge(ctx.sem["dve"], ctx.inc(ins, "dve"))
                        mark[("var", t1)] = ctx.c["dve"]
                        e.wait_ge(ctx.sem["wf"], base["wf"] + t1 + 1)
                        ins = e.tensor_tensor(out=mb, in0=mu, in1=rw_(t1, 3),
                                              op=ALU.mult)
                        e.wait_ge(ctx.sem["dve"], ctx.inc(ins, "dve"))
                        mark[("mb", t1)] = ctx.c["dve"]
                        e.wait_ge(ctx.sem["pe"], pe_bcast[t1])
                        ins = e.tensor_copy(out=rstdb[:, (t1 % 2) * TT:(t1 % 2 + 1) * TT],
                                            in_=ctx.ps[4 + (t1 % 2)][:])
                        e.wait_ge(ctx.sem["dve"], ctx.inc(ins, "dve"))
                        ins = e.tensor_copy(out=mbb[:, (t1 % 2) * TT:(t1 % 2 + 1) * TT],
                                            in_=ctx.ps[6 + (t1 % 2)][:])
                        e.wait_ge(ctx.sem["dve"], ctx.inc(ins, "dve"))
                        mark[("evac", t1)] = ctx.c["dve"]

            @blk.scalar
            def _(e):
                e.wait_ge(ctx.sem["q0"], base["q0"] + ngl)
                for t in range(NT + 2):
                    if t < NT:
                        for k in range(nk):
                            i = 16 * t + k
                            e.wait_ge(ctx.sem["x0"], base["x0"] + i + 1)
                            if t >= 2 and k == 0:
                                e.wait_ge(ctx.sem["pe"], pe_stats[(t - 2, nk - 1)])
                            ins = e.activation(out=sqv(t, k), in_=s1v(t, k),
                                               func=AF.Square)
                            ctx.inc(ins, "act")
                    if 1 <= t <= NT:
                        t1 = t - 1
                        e.wait_ge(ctx.sem["dve"], mark[("var", t1)])
                        var = rw_(t1, 2)
                        rstd = rw_(t1, 3)
                        ins = e.activation(out=rstd, in_=var, func=AF.Ln,
                                           bias=crow(ctx, 1e-5), scale=crow(ctx, 1.0))
                        e.wait_ge(ctx.sem["x1"], ctx.inc(ins, "x1"))
                        ins = e.activation(out=rstd, in_=rstd, func=AF.Exp, scale=-0.5)
                        ctx.inc(ins, "wf")
                    if t >= 2:
                        t2 = t - 2
                        for k in range(nk):
                            i2 = 16 * t2 + k
                            e.wait_ge(ctx.sem["pf"], base["pf"] + i2 + 1)
                            stq.put(e, ctx.ap(dst)[128 * k:128 * (k + 1),
                                                   t2 * TT:(t2 + 1) * TT],
                                    obv(t2, k), 2 * (t2 % 2) + k % 2)
                for sl in range(4):
                    e.wait_ge(stq.sems[sl], stq.cnt[sl])

            @blk.tensor
            def _(e):
                e.wait_ge(ctx.sem["x1"], base["x1"] + 1)
                for t in range(NT + 1):
                    if t < NT:
                        for k in range(nk):
                            i = 16 * t + k
                            e.wait_ge(ctx.sem["x0"], base["x0"] + i + 1)
                            e.wait_ge(ctx.sem["act"], base["act"] + i + 1)
                            if k == 0 and t >= 2:
                                e.wait_ge(ctx.sem["dve"], mark[("var", t - 2)])
                            e.matmul(out=ctx.ps[2 * (t % 2)][0:1, :], lhsT=onec[:],
                                     rhs=s1v(t, k), start=(k == 0), stop=(k == nk - 1))
                            ins = e.matmul(out=ctx.ps[2 * (t % 2) + 1][0:1, :],
                                           lhsT=onec[:], rhs=sqv(t, k),
                                           start=(k == 0), stop=(k == nk - 1))
                            ctx.inc(ins, "pe")
                    if t >= 1:
                        t1 = t - 1
                        e.wait_ge(ctx.sem["wf"], base["wf"] + t1 + 1)
                        e.wait_ge(ctx.sem["dve"], mark[("mb", t1)])
                        if t1 >= 2:
                            e.wait_ge(ctx.sem["dve"], mark[("evac", t1 - 2)])
                        e.matmul(out=ctx.ps[4 + (t1 % 2)][:], lhsT=onef[:],
                                 rhs=rw_(t1, 3), start=True, stop=True)
                        ins = e.matmul(out=ctx.ps[6 + (t1 % 2)][:], lhsT=onef[:],
                                       rhs=rw_(t1, 4), start=True, stop=True)
                        ctx.inc(ins, "pe")


def p_ln0a(ctx):
    p_ln(ctx, "ln0a", "xT", "att0_s", "x1_s", srcA_f32=True)



def _gamma_thresh(ctx, name, wname, nslab, nk, thr, col):
    with ctx.nc.Block(name) as blk:
        ctx.blk = blk
        with ExitStack() as les:
            wbuf = ctx.sb(les, "gw", 128, 2 * nk * 128, F32)
            asum = ctx.sb(les, "as", 128, 2, F32)
            red = ctx.sb(les, "rd", 128, 2, F32)
            onef = ctx.sb(les, "onf", 128, 1, F32)
            onr = ctx.sb(les, "onr", 1, 128, F32)
            tot = ctx.sb(les, "tot", 1, 2, F32)
            W = ctx.ap(wname)
            ld = ctx.stream("ld", 4)
            base_ld = ld.base()
            base_dve = ctx.c["dve"]

            @blk.sync
            def _(e):
                for m in range(nslab):
                    if m >= 2:
                        e.wait_ge(ctx.sem["dve"], base_dve + 3 + 2 * (m - 1))
                    ld.put(e, wbuf[:, (m % 2) * nk * 128:(m % 2 + 1) * nk * 128]
                           .rearrange("p (k c) -> p k c", k=nk),
                           W[m].rearrange("k p c -> p k c"), m % 2)

            @blk.vector
            def _(e):
                ins = e.memset(asum[:, 0:1], 0.0)
                e.wait_ge(ctx.sem["dve"], ctx.inc(ins, "dve"))
                ins = e.memset(onef[:], 1.0)
                e.wait_ge(ctx.sem["dve"], ctx.inc(ins, "dve"))
                ins = e.memset(onr[:], 1.0)
                e.wait_ge(ctx.sem["dve"], ctx.inc(ins, "dve"))
                for m in range(nslab):
                    e.wait_ge(ld.sems[m % 2], base_ld[m % 2] + 16 * (m // 2 + 1))
                    ins = e.tensor_reduce(out=red[:, 0:1],
                                          in_=wbuf[:, (m % 2) * nk * 128:(m % 2 + 1) * nk * 128],
                                          axis=AX.X, op=ALU.add, apply_absolute_value=True)
                    e.wait_ge(ctx.sem["dve"], ctx.inc(ins, "dve"))
                    ins = e.tensor_add(out=asum[:, 0:1], in0=asum[:, 0:1], in1=red[:, 0:1])
                    e.wait_ge(ctx.sem["dve"], ctx.inc(ins, "dve"))
                ctx.inc(e.engine_nop(), "x0")

            @blk.tensor
            def _(e):
                e.wait_ge(ctx.sem["x0"], ctx.c["x0"])
                ins = e.matmul(out=ctx.ps[0][0:1, 0:1], lhsT=asum[:, 0:1],
                               rhs=onef[:], start=True, stop=True)
                ctx.inc(ins, "pe")

            @blk.vector
            def _(e):
                e.wait_ge(ctx.sem["pe"], ctx.c["pe"])
                sc = 0.5 / (nslab * nk * 128 * 128)
                ins = e.tensor_scalar(out=tot[0:1, 0:1], in0=ctx.ps[0][0:1, 0:1],
                                      scalar1=sc, scalar2=0.5e-8,
                                      op0=ALU.mult, op1=ALU.add)
                e.wait_ge(ctx.sem["dve"], ctx.inc(ins, "dve"))
                ins = e.tensor_scalar_mul(out=tot[0:1, 1:2], in0=tot[0:1, 0:1],
                                          scalar1=-1.0)
                e.wait_ge(ctx.sem["dve"], ctx.inc(ins, "dve"))
                ctx.inc(e.engine_nop(), "x1")

            @blk.tensor
            def _(e):
                e.wait_ge(ctx.sem["x1"], ctx.c["x1"])
                ins = e.matmul(out=ctx.ps[1][:, 0:2], lhsT=onr[:],
                               rhs=tot[0:1, 0:2], start=True, stop=True)
                ctx.inc(ins, "pe")

            @blk.vector
            def _(e):
                e.wait_ge(ctx.sem["pe"], ctx.c["pe"])
                ins = e.tensor_copy(out=thr[:, col:col + 2], in_=ctx.ps[1][:, 0:2])
                ctx.inc(ins, "dve")


def _quant_ops(ctx, e, wsrc, tmp1, tmp2, qdst, thr, col):
    e.tensor_single_scalar(out=tmp1, in_=wsrc, scalar=thr[:, col:col + 1], op=ALU.is_gt)
    ins = e.tensor_single_scalar(out=tmp2, in_=wsrc, scalar=thr[:, col + 1:col + 2],
                                 op=ALU.is_lt)
    e.wait_ge(ctx.sem["dve"], ctx.inc(ins, "dve"))
    ins = e.tensor_sub(out=qdst, in0=tmp1, in1=tmp2)
    return ins


def p_ffnh(ctx):
    thr = ctx.psb("thr", 128, 8, F32)
    _gamma_thresh(ctx, "gam1", "W1", 44, 16, thr, 0)
    _gamma_thresh(ctx, "gam2", "W2", 44, 16, thr, 2)
    nk, nm = 16, 44
    with ExitStack() as les0:
        rhs = ctx.sb(les0, "x1res", 128, 16 * T, BF16)
        rw = rhs_load_block(ctx, "ldx1", "x1_s", 16, rhs)
        with ctx.nc.Block("ffnh") as blk:
            ctx.blk = blk
            with ExitStack() as les:
                wf1 = ctx.sb(les, "wf1", 128, 2 * nk * 128, F32)
                wf2 = ctx.sb(les, "wf2", 128, 2 * nk * 128, F32)
                tq1 = ctx.sb(les, "tq1", 128, nk * 128, F32)
                tq2 = ctx.sb(les, "tq2", 128, nk * 128, F32)
                wq1 = ctx.sb(les, "wq1", 128, 2 * nk * 128, BF16)
                wq2 = ctx.sb(les, "wq2", 128, 2 * nk * 128, BF16)
                sg = ctx.sb(les, "sg", 128, 2 * TT, BF16)
                ob = ctx.sb(les, "ob", 128, 4 * TT, BF16)
                ld = ctx.stream("ld", 4)
                base_ld = ld.base()
                base_pe = ctx.c["pe"]
                base_dve = ctx.c["dve"]
                base_act = ctx.c["act"]
                base_pf = ctx.c["pf"]
                base_st = ctx.stream("st", 4).base()
                stq = ctx.stream("st", 4)

                @blk.sync
                def _(e):
                    for m in range(nm):
                        if m >= 2:
                            e.wait_ge(ctx.sem["dve"], base_dve + 6 * (m - 1))
                        ld.put(e, wf1[:, (m % 2) * nk * 128:(m % 2 + 1) * nk * 128]
                               .rearrange("p (k c) -> p k c", k=nk),
                               ctx.ap("W1")[m].rearrange("k p c -> p k c"), m % 2)
                        ld.put(e, wf2[:, (m % 2) * nk * 128:(m % 2 + 1) * nk * 128]
                               .rearrange("p (k c) -> p k c", k=nk),
                               ctx.ap("W2")[m].rearrange("k p c -> p k c"), 2 + m % 2)

                def quantize(e, m):
                    e.wait_ge(ld.sems[m % 2], base_ld[m % 2] + 16 * (m // 2 + 1))
                    e.wait_ge(ld.sems[2 + m % 2], base_ld[2 + m % 2] + 16 * (m // 2 + 1))
                    if m >= 2:
                        e.wait_ge(ctx.sem["pe"], base_pe + (m - 1) * NT)
                    o1 = (m % 2) * nk * 128
                    ins = _quant_ops(ctx, e, wf1[:, o1:o1 + nk * 128], tq1[:],
                                     tq2[:], wq1[:, o1:o1 + nk * 128], thr, 0)
                    e.wait_ge(ctx.sem["dve"], ctx.inc(ins, "dve"))
                    ins = _quant_ops(ctx, e, wf2[:, o1:o1 + nk * 128], tq1[:],
                                     tq2[:], wq2[:, o1:o1 + nk * 128], thr, 2)
                    e.wait_ge(ctx.sem["dve"], ctx.inc(ins, "dve"))
                    ctx.inc(e.engine_nop(), "dve")
                    ctx.inc(e.engine_nop(), "dve")

                @blk.tensor
                def _(e):
                    e.wait_ge(ctx.sem[rw[0]], rw[1])
                    for m in range(nm):
                        e.wait_ge(ctx.sem["dve"], base_dve + 6 * (m + 1))
                        for t in range(NT):
                            j = m * NT + t
                            if j >= 2:
                                e.wait_ge(ctx.sem["pf"], base_pf + j - 1)
                            o1 = (m % 2) * nk * 128
                            b0 = ctx.ps[(j % 2) * 2]
                            b1 = ctx.ps[(j % 2) * 2 + 1]
                            for k in range(nk):
                                e.matmul(out=b0[:], lhsT=wq1[:, o1 + k * 128:o1 + (k + 1) * 128],
                                         rhs=rhs[:, k * T + t * TT:k * T + t * TT + TT],
                                         start=(k == 0), stop=(k == nk - 1))
                            for k in range(nk):
                                ins = e.matmul(out=b1[:], lhsT=wq2[:, o1 + k * 128:o1 + (k + 1) * 128],
                                               rhs=rhs[:, k * T + t * TT:k * T + t * TT + TT],
                                               start=(k == 0), stop=(k == nk - 1))
                            ctx.inc(ins, "pe")

                def dst_hid(j):
                    return ctx.ap("hid_s")[128 * (j // NT):128 * (j // NT) + 128,
                                           (j % NT) * TT:(j % NT + 1) * TT]

                @blk.scalar
                def _(e):
                    for j in range(nm * NT):
                        e.wait_ge(ctx.sem["pe"], base_pe + j + 1)
                        if j >= 2:
                            e.wait_ge(ctx.sem["pf"], base_pf + j - 1)
                        ins = e.activation(out=sg[:, (j % 2) * TT:(j % 2 + 1) * TT],
                                           in_=ctx.ps[(j % 2) * 2][:], func=AF.Sigmoid)
                        ctx.inc(ins, "act")
                        if j >= 1:
                            e.wait_ge(ctx.sem["pf"], base_pf + j)
                            stq.put(e, dst_hid(j - 1),
                                    ob[:, ((j - 1) % 4) * TT:((j - 1) % 4 + 1) * TT],
                                    (j - 1) % 4)
                    e.wait_ge(ctx.sem["pf"], base_pf + nm * NT)
                    stq.put(e, dst_hid(nm * NT - 1),
                            ob[:, ((nm * NT - 1) % 4) * TT:((nm * NT - 1) % 4 + 1) * TT],
                            (nm * NT - 1) % 4)
                    for sl in range(4):
                        e.wait_ge(stq.sems[sl], stq.cnt[sl])

                @blk.vector
                def _(e):
                    quantize(e, 0)
                    quantize(e, 1)
                    for j in range(nm * NT):
                        e.wait_ge(ctx.sem["act"], base_act + j + 1)
                        if j >= 4:
                            stq.wait(e, st_tok(ctx, base_st, j - 4))
                        ins = e.tensor_tensor(out=ob[:, (j % 4) * TT:(j % 4 + 1) * TT],
                                              in0=sg[:, (j % 2) * TT:(j % 2 + 1) * TT],
                                              in1=ctx.ps[(j % 2) * 2 + 1][:], op=ALU.mult)
                        ctx.inc(ins, "pf")
                        if j % NT == NT - 1 and j // NT + 2 < nm:
                            quantize(e, j // NT + 2)


def p_ffn3(ctx):
    thr = ctx.psb("thr", 128, 8, F32)
    _gamma_thresh(ctx, "gam3", "W3", 16, 44, thr, 4)
    with ctx.nc.Block("q3") as blk:
        ctx.blk = blk
        with ExitStack() as les:
            wf = ctx.sb(les, "wf", 128, 2 * 44 * 128, F32)
            tq1 = ctx.sb(les, "tq1", 128, 44 * 128, F32)
            tq2 = ctx.sb(les, "tq2", 128, 44 * 128, F32)
            qb = ctx.sb(les, "qb", 128, 2 * 44 * 128, BF16)
            ld = ctx.stream("ld", 4)
            base_ld = ld.base()
            base_dve = ctx.c["dve"]
            stq = ctx.stream("st", 4)
            base_st = stq.base()

            @blk.sync
            def _(e):
                for m in range(16):
                    if m >= 2:
                        e.wait_ge(ctx.sem["dve"], base_dve + 3 * (m - 1))
                    ld.put(e, wf[:, (m % 2) * 44 * 128:(m % 2 + 1) * 44 * 128]
                           .rearrange("p (k c) -> p k c", k=44),
                           ctx.ap("W3")[m].rearrange("k p c -> p k c"), m % 2)

            @blk.vector
            def _(e):
                for m in range(16):
                    e.wait_ge(ld.sems[m % 2], base_ld[m % 2] + 16 * (m // 2 + 1))
                    if m >= 2:
                        stq.wait(e, st_tok(ctx, base_st, m - 2, 2))
                    o = (m % 2) * 44 * 128
                    ins = _quant_ops(ctx, e, wf[:, o:o + 44 * 128], tq1[:], tq2[:],
                                     qb[:, o:o + 44 * 128], thr, 4)
                    e.wait_ge(ctx.sem["dve"], ctx.inc(ins, "dve"))
                    ctx.inc(e.engine_nop(), "dve")

            store_section(ctx, 16, "dve", lambda m: base_dve + 3 * (m + 1),
                          lambda m: qb[:, (m % 2) * 44 * 128:(m % 2 + 1) * 44 * 128],
                          lambda m: ctx.ap("w3q_s")[128 * m:128 * (m + 1), :],
                          nslots=2, eng="scalar")

    with ExitStack() as les0:
        rhs = ctx.sb(les0, "hidres", 128, 44 * 1024, BF16)
        for tb in range(4):
            rw = rhs_load_block(ctx, f"ldh{tb}", "hid_s", 44, rhs,
                                c0=tb * 1024, ncols=1024)
            with ctx.nc.Block(f"ffn3_{tb}") as blk:
                ctx.blk = blk
                with ExitStack() as les:
                    wq = ctx.sb(les, "wq", 128, 2 * 44 * 128, BF16)
                    ob = ctx.sb(les, "ob", 128, 4 * TT, BF16)
                    ld = ctx.stream("ld", 4)
                    base_ld = ld.base()
                    base_pe = ctx.c["pe"]
                    base_dve = ctx.c["dve"]
                    stq = ctx.stream("st", 4)
                    base_st = stq.base()

                    @blk.sync
                    def _(e):
                        for m in range(16):
                            if m >= 2:
                                e.wait_ge(ctx.sem["pe"], base_pe + 2 * (m - 1))
                            ld.put(e, wq[:, (m % 2) * 44 * 128:(m % 2 + 1) * 44 * 128],
                                   ctx.ap("w3q_s")[128 * m:128 * (m + 1), :], m % 2)

                    @blk.tensor
                    def _(e):
                        e.wait_ge(ctx.sem[rw[0]], rw[1])
                        for m in range(16):
                            e.wait_ge(ld.sems[m % 2], base_ld[m % 2] + 16 * (m // 2 + 1))
                            for t in range(2):
                                j = 2 * m + t
                                if j >= 4:
                                    e.wait_ge(ctx.sem["dve"], base_dve + j - 3)
                                o = (m % 2) * 44 * 128
                                for k in range(44):
                                    ins = e.matmul(
                                        out=ctx.ps[j % 4][:],
                                        lhsT=wq[:, o + k * 128:o + (k + 1) * 128],
                                        rhs=rhs[:, k * 1024 + t * TT:k * 1024 + t * TT + TT],
                                        start=(k == 0), stop=(k == 43))
                                ctx.inc(ins, "pe")

                    @blk.vector
                    def _(e):
                        for j in range(32):
                            e.wait_ge(ctx.sem["pe"], base_pe + j + 1)
                            if j >= 4:
                                stq.wait(e, st_tok(ctx, base_st, j - 4))
                            ins = e.tensor_copy(out=ob[:, (j % 4) * TT:(j % 4 + 1) * TT],
                                                in_=ctx.ps[j % 4][:])
                            ctx.inc(ins, "dve")

                    store_section(ctx, 32, "dve", lambda j: base_dve + j + 1,
                                  lambda j: ob[:, (j % 4) * TT:(j % 4 + 1) * TT],
                                  lambda j: ctx.ap("ff0_s")[
                                      128 * (j // 2):128 * (j // 2) + 128,
                                      tb * 1024 + (j % 2) * TT:tb * 1024 + (j % 2 + 1) * TT],
                                  eng="scalar")


def p_ln0b(ctx):
    p_ln(ctx, "ln0b", "x1_s", "ff0_s", "x2_s")



def p_qkv(ctx):
    with ExitStack() as les:
        rhs = ctx.sb(les, "x2res", 128, 16 * T, BF16)
        rw = rhs_load_block(ctx, "ldx2", "x2_s", 16, rhs)
        generic_linear(ctx, "qkv", "Wqkv", 16, 48, rhs, rw,
                       lambda m, t: ctx.ap("qkv_s")[128 * m:128 * m + 128,
                                                    t * TT:(t + 1) * TT])


def _attn_blk(ctx, h, ci, cst):
    nw = 4 if ci == 0 else 8
    qkv = ctx.ap("qkv_s")
    mk = cst["mk0"] if ci == 0 else cst["mkg"]
    with ctx.nc.Block(f"at{h}_{ci}") as blk:
        ctx.blk = blk
        with ExitStack() as les:
            qt = ctx.sb(les, "qt", 128, TT, BF16)
            kw = ctx.sb(les, "kw", 128, nw * 128, BF16)
            vw = ctx.sb(les, "vw", 128, nw * 128, BF16)
            vtm = ctx.sb(les, "vtm", 128, nw * 128, BF16)
            pre2 = ctx.sb(les, "pre2", 128, nw * TT, F32)
            tmp = ctx.sb(les, "tmp", 128, 2 * TT, F32)
            au = ctx.sb(les, "au", 128, nw * TT, BF16)
            mxb = ctx.sb(les, "mxb", 128, 4, BF16)
            cred = ctx.sb(les, "cred", 128, 2, BF16)
            mrow = ctx.sb(les, "mrow", 1, TT, BF16)
            srow = ctx.sb(les, "srow", 1, TT, F32)
            rrow = ctx.sb(les, "rrow", 1, TT, F32)
            mxs = ctx.sb(les, "mxs", 128, TT, F32)
            rbs = ctx.sb(les, "rbs", 128, TT, F32)
            ob = ctx.sb(les, "ob", 128, TT, BF16)
            ld = ctx.stream("ld", 4)
            base = {sn: ctx.c[sn] for sn in
                    ["pe", "act", "dve", "pool", "pf", "x0", "x1", "st"]}
            kv0 = (ci - 1) * TT if ci > 0 else 0
            toks = {}
            nhalf = nw // 4
            PE_VT = 1
            PE_R1 = 2
            PE_MAX = lambda qb_: 2 + nhalf * (qb_ + 1)
            PE_MROW = 2 + 4 * nhalf + 1
            PE_MAXB = PE_MROW + 1
            PE_R2 = PE_MAXB + (1 if nw == 8 else 0)
            PE_SUM = PE_R2 + 1
            PE_RECB = PE_SUM + 1
            PE_AV = PE_RECB + 1
            DVE_VTM = 1
            DVE_MXS = DVE_VTM + 8 + 1

            @blk.sync
            def _(e):
                toks[0] = ld.put(e, qt[:], qkv[128 * h:128 * (h + 1),
                                              ci * TT:(ci + 1) * TT], 0)
                toks[1] = ld.put(e, kw[:], qkv[2048 + 128 * h:2048 + 128 * (h + 1),
                                               kv0:kv0 + nw * 128], 1)
                toks[2] = ld.put(e, vw[:], qkv[4096 + 128 * h:4096 + 128 * (h + 1),
                                               kv0:kv0 + nw * 128], 2)
                e.wait_ge(ctx.sem["x1"], base["x1"] + 3)
                d = e.dma_start(out=ctx.ap("ao_s")[128 * h:128 * (h + 1),
                                                   ci * TT:(ci + 1) * TT], in_=ob[:])
                d.then_inc(ctx.sem["st"], 16)
                ctx.c["st"] += 16
                e.wait_ge(ctx.sem["st"], ctx.c["st"])

            @blk.tensor
            def _(e):
                ld.wait(e, toks[2])
                for j in range(nw):
                    ins = e.matmul(out=ctx.ps[4 + j // 4][:, (j % 4) * 128:(j % 4 + 1) * 128],
                                   lhsT=vw[:, j * 128:(j + 1) * 128], rhs=cst["idb"][:],
                                   start=(j % 4 == 0), stop=(j % 4 == 3 or j == nw - 1))
                ctx.inc(ins, "pe")
                ld.wait(e, toks[0])
                ld.wait(e, toks[1])
                e.wait_ge(ctx.sem["dve"], base["dve"] + DVE_VTM)
                for j in range(min(nw, 4)):
                    ins = e.matmul(out=ctx.ps[j][:], lhsT=kw[:, j * 128:(j + 1) * 128],
                                   rhs=qt[:], start=True, stop=True)
                ctx.inc(ins, "pe")
                for qb_ in range(4):
                    bks = (4, 5) if qb_ % 2 == 0 else (6, 7)
                    if qb_ >= 2:
                        e.wait_ge(ctx.sem["x0"], base["x0"] + qb_ - 1)
                    for half in range(nhalf):
                        ins = e.matmul(out=ctx.ps[bks[half]][:],
                                       lhsT=qt[:, qb_ * 128:(qb_ + 1) * 128],
                                       rhs=kw[:, half * 512:half * 512 + min(512, nw * 128)],
                                       start=True, stop=True)
                        ctx.inc(ins, "pe")
                e.wait_ge(ctx.sem["x0"], base["x0"] + 4)
                for qb_ in range(4):
                    ins = e.matmul(out=ctx.ps[6][0:1, qb_ * 128:(qb_ + 1) * 128],
                                   lhsT=mxb[:, qb_:qb_ + 1], rhs=cst["idb"][:],
                                   start=(qb_ == 0), stop=(qb_ == 3))
                ctx.inc(ins, "pe")
                e.wait_ge(ctx.sem["x1"], base["x1"] + 1)
                ins = e.matmul(out=ctx.ps[7][:], lhsT=cst["onrb"][:], rhs=mrow[:],
                               start=True, stop=True)
                ctx.inc(ins, "pe")
                if nw == 8:
                    e.wait_ge(ctx.sem["pf"], base["pf"] + 4)
                    for j in range(4, 8):
                        ins = e.matmul(out=ctx.ps[j - 4][:],
                                       lhsT=kw[:, j * 128:(j + 1) * 128],
                                       rhs=qt[:], start=True, stop=True)
                    ctx.inc(ins, "pe")
                for j in range(nw):
                    e.wait_ge(ctx.sem["act"], base["act"] + j + 1)
                    ins = e.matmul(out=ctx.ps[6][0:1, 0:TT], lhsT=cst["onecb"][:],
                                   rhs=au[:, j * TT:(j + 1) * TT],
                                   start=(j == 0), stop=(j == nw - 1))
                ctx.inc(ins, "pe")
                e.wait_ge(ctx.sem["x1"], base["x1"] + 2)
                ins = e.matmul(out=ctx.ps[4][:], lhsT=cst["onrf"][:], rhs=rrow[:],
                               start=True, stop=True)
                ctx.inc(ins, "pe")
                for j in range(nw):
                    ins = e.matmul(out=ctx.ps[5][:], lhsT=vtm[:, j * 128:(j + 1) * 128],
                                   rhs=au[:, j * TT:(j + 1) * TT],
                                   start=(j == 0), stop=(j == nw - 1))
                ctx.inc(ins, "pe")

            @blk.vector
            def _(e):
                e.wait_ge(ctx.sem["pe"], base["pe"] + PE_VT)
                if nw == 4:
                    ins = e.tensor_copy(out=vtm[:], in_=ctx.ps[4][:])
                else:
                    e.tensor_copy(out=vtm[:, 0:512], in_=ctx.ps[4][:])
                    ins = e.tensor_copy(out=vtm[:, 512:1024], in_=ctx.ps[5][:])
                e.wait_ge(ctx.sem["dve"], ctx.inc(ins, "dve"))
                for qb_ in range(4):
                    bks = (4, 5) if qb_ % 2 == 0 else (6, 7)
                    e.wait_ge(ctx.sem["pe"], base["pe"] + PE_MAX(qb_))
                    if nhalf == 2:
                        ins = e.tensor_reduce(out=cred[:, 0:1], in_=ctx.ps[bks[0]][:],
                                              axis=AX.X, op=ALU.max)
                        e.wait_ge(ctx.sem["dve"], ctx.inc(ins, "dve"))
                        ins = e.tensor_reduce(out=cred[:, 1:2], in_=ctx.ps[bks[1]][:],
                                              axis=AX.X, op=ALU.max)
                        e.wait_ge(ctx.sem["dve"], ctx.inc(ins, "dve"))
                        ins = e.tensor_max(out=mxb[:, qb_:qb_ + 1],
                                           in0=cred[:, 0:1], in1=cred[:, 1:2])
                    else:
                        ins = e.tensor_reduce(out=mxb[:, qb_:qb_ + 1],
                                              in_=ctx.ps[bks[0]][:],
                                              axis=AX.X, op=ALU.max)
                        e.wait_ge(ctx.sem["dve"], ctx.inc(ins, "dve"))
                        ctx.inc(e.engine_nop(), "dve")
                        ins = e.engine_nop()
                    e.wait_ge(ctx.sem["x0"], ctx.inc(ins, "x0"))
                e.wait_ge(ctx.sem["pe"], base["pe"] + PE_MROW)
                ins = e.tensor_copy(out=mrow[:], in_=ctx.ps[6][0:1, :])
                e.wait_ge(ctx.sem["x1"], ctx.inc(ins, "x1"))
                e.wait_ge(ctx.sem["pe"], base["pe"] + PE_MAXB)
                ins = e.tensor_copy(out=mxs[:], in_=ctx.ps[7][:])
                e.wait_ge(ctx.sem["dve"], ctx.inc(ins, "dve"))
                for j in range(nw):
                    ins = e.tensor_tensor(out=pre2[:, j * TT:(j + 1) * TT],
                                          in0=mk[:, j * TT:(j + 1) * TT],
                                          in1=mxs[:], op=ALU.subtract)
                    e.wait_ge(ctx.sem["dve"], ctx.inc(ins, "dve"))
                    e.wait_ge(ctx.sem["pe"],
                              base["pe"] + (PE_R1 if j < 4 else PE_R2))
                    if j >= 2:
                        e.wait_ge(ctx.sem["act"], base["act"] + j - 1)
                    ins = e.scalar_tensor_tensor(
                        out=tmp[:, (j % 2) * TT:(j % 2 + 1) * TT],
                        in0=ctx.ps[j % 4][:], scalar=ccol(ctx, 0.088388347648318447),
                        in1=pre2[:, j * TT:(j + 1) * TT],
                        op0=ALU.mult, op1=ALU.add)
                    ctx.inc(ins, "pf")
                e.wait_ge(ctx.sem["pe"], base["pe"] + PE_SUM)
                ins = e.tensor_copy(out=srow[:], in_=ctx.ps[6][0:1, 0:TT])
                e.wait_ge(ctx.sem["dve"], ctx.inc(ins, "dve"))
                ins = e.reciprocal(out=rrow[:], in_=srow[:])
                e.wait_ge(ctx.sem["x1"], ctx.inc(ins, "x1"))
                e.wait_ge(ctx.sem["pe"], base["pe"] + PE_RECB)
                ins = e.tensor_copy(out=rbs[:], in_=ctx.ps[4][:])
                e.wait_ge(ctx.sem["dve"], ctx.inc(ins, "dve"))
                e.wait_ge(ctx.sem["pe"], base["pe"] + PE_AV)
                ins = e.tensor_tensor(out=ob[:], in0=ctx.ps[5][:], in1=rbs[:],
                                      op=ALU.mult)
                ctx.inc(ins, "x1")


            @blk.scalar
            def _(e):
                for j in range(nw):
                    e.wait_ge(ctx.sem["pf"], base["pf"] + j + 1)
                    ins = e.activation(out=au[:, j * TT:(j + 1) * TT],
                                       in_=tmp[:, (j % 2) * TT:(j % 2 + 1) * TT],
                                       func=AF.Exp)
                    ctx.inc(ins, "act")


def p_attn(ctx):
    cst = {
        "idb": ctx.psb("idb", 128, 128, BF16),
        "mkg": ctx.psb("mkg", 128, 8 * TT, BF16),
        "mk0": ctx.psb("mk0", 128, 4 * TT, BF16),
        "onrb": ctx.psb("onrb", 1, 128, BF16),
        "onrf": ctx.psb("onrf", 1, 128, F32),
        "onecb": ctx.psb("onecb", 128, 1, BF16),
    }
    with ctx.nc.Block("atpro") as blk:
        ctx.blk = blk

        @blk.sync
        def _(e):
            d = e.dma_start(out=cst["idb"][:], in_=ctx.ap("identb")[:, :])
            ctx.inc(d, "q0", 16)
            for j in range(8):
                d = e.dma_start(out=cst["mkg"][:, j * TT:(j + 1) * TT],
                                in_=ctx.ap("maskg")[128 * j:128 * (j + 1), :])
                ctx.inc(d, "q0", 16)
            for j in range(4):
                d = e.dma_start(out=cst["mk0"][:, j * TT:(j + 1) * TT],
                                in_=ctx.ap("mask0")[128 * j:128 * (j + 1), :])
                ctx.inc(d, "q0", 16)
            e.wait_ge(ctx.sem["q0"], ctx.c["q0"])

        @blk.vector
        def _(e):
            e.memset(cst["onrb"][:], 1.0)
            e.memset(cst["onrf"][:], 1.0)
            e.memset(cst["onecb"][:], 1.0)
    for ci in range(8):
        for h in range(16):
            _attn_blk(ctx, h, ci, cst)


def p_wout(ctx):
    with ExitStack() as les:
        rhs = ctx.sb(les, "aores", 128, 16 * T, BF16)
        rw = rhs_load_block(ctx, "ldao", "ao_s", 16, rhs)
        generic_linear(ctx, "wout", "Wo1", 16, 16, rhs, rw,
                       lambda m, t: ctx.ap("att1_s")[128 * m:128 * m + 128,
                                                     t * TT:(t + 1) * TT])


def p_ln1a(ctx):
    p_ln(ctx, "ln1a", "x2_s", "att1_s", "x3_s")



def p_l1fa(ctx):
    modc = ctx.psb("modc", 128, 64, F32)
    ampbc = ctx.psb("ampbc", 128, 16, F32)
    phbc = ctx.psb("phbc", 128, 16, F32)
    pbc = ctx.psb("pbc", 128, 64, F32)
    with ctx.nc.Block("mod") as blk:
        ctx.blk = blk
        with ExitStack() as les:
            ph = ctx.sb(les, "phx", 128, 64, F32)
            t1 = ctx.sb(les, "mt1", 128, 64, F32)
            ti = ctx.sb(les, "mti", 128, 64, I32)
            sr = ctx.sb(les, "msr", 128, 64, F32)

            @blk.sync
            def _(e):
                load_cols(ctx, e, "phs0", ph[:, 0:32])
                load_cols(ctx, e, "phs1", ph[:, 32:64])
                load_cols(ctx, e, "ampb", ampbc)
                load_cols(ctx, e, "phb", phbc)
                load_cols(ctx, e, "pathb0", pbc[:, 0:32])
                load_cols(ctx, e, "pathb1", pbc[:, 32:64])
            nq = ctx.c["q0"]

            @blk.vector
            def _(e):
                e.wait_ge(ctx.sem["q0"], nq)
                ins = e.tensor_scalar_mul(out=t1[:], in0=ph[:],
                                          scalar1=float(1.0 / (2 * np.pi)))
                e.wait_ge(ctx.sem["dve"], ctx.inc(ins, "dve"))
                ins = e.tensor_scalar_add(out=t1[:], in0=t1[:], scalar1=12582912.0)
                e.wait_ge(ctx.sem["dve"], ctx.inc(ins, "dve"))
                ins = e.tensor_scalar_add(out=t1[:], in0=t1[:], scalar1=-12582912.0)
                e.wait_ge(ctx.sem["dve"], ctx.inc(ins, "dve"))
                ins = e.tensor_scalar_mul(out=t1[:], in0=t1[:],
                                          scalar1=float(-2.0 * np.pi))
                e.wait_ge(ctx.sem["dve"], ctx.inc(ins, "dve"))
                ins = e.tensor_add(out=t1[:], in0=t1[:], in1=ph[:])
                ctx.inc(ins, "x0")

            @blk.scalar
            def _(e):
                e.wait_ge(ctx.sem["x0"], ctx.c["x0"])
                ins = e.activation(out=sr[:], in_=t1[:], func=AF.Abs)
                e.wait_ge(ctx.sem["act"], ctx.inc(ins, "act"))
                ins = e.activation(out=sr[:], in_=sr[:], func=AF.Sin, scale=-1.0,
                                   bias=ccol(ctx, np.pi / 2))
                e.wait_ge(ctx.sem["act"], ctx.inc(ins, "act"))
                ins = e.activation(out=t1[:], in_=t1[:], func=AF.Sin)
                ctx.inc(ins, "act")
            nact_mod = ctx.c["act"]

            @blk.vector
            def _(e):
                e.wait_ge(ctx.sem["act"], nact_mod)
                ins = e.tensor_add(out=modc[:], in0=t1[:], in1=sr[:])
                ctx.inc(ins, "dve")

    with ExitStack() as les:
        rhs = ctx.sb(les, "x3res", 128, 16 * T, BF16)
        rw = rhs_load_block(ctx, "ldx3", "x3_s", 16, rhs)

        def epi_amp(e, m, t, ps, buf):
            return e.activation(out=buf, in_=ps[:], func=AF.Sigmoid,
                                bias=ampbc[:, m:m + 1], scale=ccol(ctx, 1.0))

        generic_linear(ctx, "amp", "Wamp", 16, 16, rhs, rw,
                       lambda m, t: ctx.ap("amp_s")[128 * m:128 * m + 128,
                                                    t * TT:(t + 1) * TT],
                       epi=epi_amp, epi_eng="scalar")

        with ctx.nc.Block("phc") as blk:
            ctx.blk = blk
            with ExitStack() as les2:
                wbuf = ctx.sb(les2, "wb", 128, 2 * 16 * 128, BF16)
                th = ctx.sb(les2, "th", 128, 2 * TT, F32)
                sn = ctx.sb(les2, "sn", 128, 2 * TT, F32)
                ab = ctx.sb(les2, "abq", 128, 2 * TT, F32)
                cs_ = ctx.sb(les2, "csq", 128, 2 * TT, F32)
                amt = ctx.sb(les2, "amt", 128, 4 * TT, BF16)
                sm = ctx.sb(les2, "sm", 128, 2 * TT, F32)
                ob = ctx.sb(les2, "ob", 128, 4 * TT, BF16)
                ld = ctx.stream("ld", 4)
                base_ld = ld.base()
                w_ready = slab_loader(ctx, "Wph", 16, 16, wbuf)
                base_act = ctx.c["act"]
                base_dve = ctx.c["dve"]
                base_pe = mm_section(ctx, 16, 16, wbuf, rhs, w_ready, rw,
                                     lambda j: ("act", base_act + 4 * j + 1))
                stq = ctx.stream("st", 4)
                base_st = stq.base()
                base_x0 = ctx.c["x0"]

                @blk.sync
                def _(e):
                    for j in range(16 * NT):
                        if j >= 4:
                            e.wait_ge(ctx.sem["dve"], base_dve + 3 * (j - 3))
                        ld.put(e, amt[:, (j % 4) * TT:(j % 4 + 1) * TT],
                               ctx.ap("amp_s")[128 * (j // NT):128 * (j // NT) + 128,
                                               (j % NT) * TT:(j % NT + 1) * TT],
                               j % 4)
                        if j >= 2:
                            e.wait_ge(ctx.sem["dve"], base_dve + 3 * (j - 1))
                            stq.put(e, ctx.ap("h0_s")[
                                128 * ((j - 2) // NT):128 * ((j - 2) // NT) + 128,
                                ((j - 2) % NT) * TT:((j - 2) % NT + 1) * TT],
                                ob[:, ((j - 2) % 4) * TT:((j - 2) % 4 + 1) * TT],
                                (j - 2) % 4)
                    for j in (16 * NT - 2, 16 * NT - 1):
                        e.wait_ge(ctx.sem["dve"], base_dve + 3 * (j + 1))
                        stq.put(e, ctx.ap("h0_s")[
                            128 * (j // NT):128 * (j // NT) + 128,
                            (j % NT) * TT:(j % NT + 1) * TT],
                            ob[:, (j % 4) * TT:(j % 4 + 1) * TT], j % 4)
                    for sl in range(4):
                        e.wait_ge(stq.sems[sl], stq.cnt[sl])

                @blk.scalar
                def _(e):
                    for j in range(16 * NT):
                        m = j // NT
                        e.wait_ge(ctx.sem["pe"], base_pe + j + 1)
                        if j >= 2:
                            e.wait_ge(ctx.sem["dve"], base_dve + 3 * (j - 1))
                        tv = th[:, (j % 2) * TT:(j % 2 + 1) * TT]
                        ins = e.activation(out=tv, in_=ctx.ps[j % 4][:], func=AF.Tanh,
                                           bias=phbc[:, m:m + 1], scale=ccol(ctx, 1.0))
                        e.wait_ge(ctx.sem["act"], ctx.inc(ins, "act"))
                        ins = e.activation(out=ab[:, (j % 2) * TT:(j % 2 + 1) * TT],
                                           in_=tv, func=AF.Abs)
                        e.wait_ge(ctx.sem["act"], ctx.inc(ins, "act"))
                        ins = e.activation(out=sn[:, (j % 2) * TT:(j % 2 + 1) * TT],
                                           in_=tv, func=AF.Sin, scale=float(np.pi))
                        ctx.inc(ins, "act")
                        ins = e.activation(out=cs_[:, (j % 2) * TT:(j % 2 + 1) * TT],
                                           in_=ab[:, (j % 2) * TT:(j % 2 + 1) * TT],
                                           func=AF.Sin, scale=float(-np.pi),
                                           bias=ccol(ctx, np.pi / 2))
                        ctx.inc(ins, "act")

                @blk.vector
                def _(e):
                    for j in range(16 * NT):
                        e.wait_ge(ctx.sem["act"], base_act + 4 * (j + 1))
                        if j >= 2:
                            e.wait_ge(ctx.sem["dve"], base_dve + 3 * (j - 1))
                        ins = e.tensor_add(out=sm[:, (j % 2) * TT:(j % 2 + 1) * TT],
                                           in0=sn[:, (j % 2) * TT:(j % 2 + 1) * TT],
                                           in1=cs_[:, (j % 2) * TT:(j % 2 + 1) * TT])
                        e.wait_ge(ctx.sem["dve"], ctx.inc(ins, "dve"))
                        ctx.inc(e.engine_nop(), "dve")
                        ld.wait(e, (j % 4, base_ld[j % 4] + 16 * (j // 4 + 1)))
                        if j >= 4:
                            stq.wait(e, st_tok(ctx, base_st, j - 4))
                        ins = e.tensor_tensor(out=ob[:, (j % 4) * TT:(j % 4 + 1) * TT],
                                              in0=sm[:, (j % 2) * TT:(j % 2 + 1) * TT],
                                              in1=amt[:, (j % 4) * TT:(j % 4 + 1) * TT],
                                              op=ALU.mult)
                        ctx.inc(ins, "dve")


def p_ent(ctx, l):
    name = f"ent{l}"
    hsrc = "h0_s" if l == 0 else "hB0_s"
    ebc = ctx.psb(f"entbc{l}", 128, 64, F32)
    with ctx.nc.Block(name + "c") as blk:
        ctx.blk = blk

        @blk.sync
        def _(e):
            load_cols(ctx, e, f"entb{l}", ebc)
            e.wait_ge(ctx.sem["q0"], ctx.c["q0"])
    with ExitStack() as les:
        rhs = ctx.sb(les, "hres", 128, 16 * T, BF16)
        rw = rhs_load_block(ctx, "ldh" + name, hsrc, 16, rhs)

        def epi(e, m, t, ps, buf):
            return e.tensor_scalar_add(out=buf, in0=ps[:], scalar1=ebc[:, m:m + 1])

        generic_linear(ctx, name, f"Went{l}", 16, 64, rhs, rw,
                       lambda m, t: ctx.ap(f"e{l}_s")[128 * m:128 * m + 128,
                                                      t * TT:(t + 1) * TT], epi=epi)


def p_entp(ctx, l):
    ebc = ctx.psb(f"entpbc{l}", 128, 16, F32)
    with ctx.nc.Block(f"entp{l}c") as blk:
        ctx.blk = blk

        @blk.sync
        def _(e):
            load_cols(ctx, e, f"entpb{l}", ebc)
            e.wait_ge(ctx.sem["q0"], ctx.c["q0"])
    with ExitStack() as les:
        rhs = ctx.sb(les, "eres", 128, 64 * 1024, BF16)
        for tb in range(4):
            rw = rhs_load_block(ctx, f"lde{l}_{tb}", f"e{l}_s", 64, rhs,
                                c0=tb * 1024, ncols=1024)

            def epi(e, m, t, ps, buf):
                return e.tensor_scalar_add(out=buf, in0=ps[:], scalar1=ebc[:, m:m + 1])

            def dst_of(m, t, _tb=tb):
                return ctx.ap(f"s3{l}_s")[128 * m:128 * m + 128,
                                          _tb * 1024 + t * TT:_tb * 1024 + (t + 1) * TT]

            generic_linear(ctx, f"entp{l}_{tb}", f"Wentp{l}", 64, 16, rhs, rw,
                           dst_of, epi=epi, ntt=2, rhs_stride=1024)


def p_path(ctx, l):
    hsrc = f"hA{l}_s"
    modc = ctx.keep["modc"]
    pbc = ctx.keep["pbc"]
    nk, nm = 16, 16
    with ExitStack() as les0:
        rhs = ctx.sb(les0, "hres2", 128, 16 * T, BF16)
        rw = rhs_load_block(ctx, f"ldhA{l}", hsrc, 16, rhs)
        with ctx.nc.Block(f"path{l}") as blk:
            ctx.blk = blk
            with ExitStack() as les:
                wb0 = ctx.sb(les, "wb0", 128, 2 * nk * 128, BF16)
                wb1 = ctx.sb(les, "wb1", 128, 2 * nk * 128, BF16)
                e0t = ctx.sb(les, "e0t", 128, 2 * TT, F32)
                e1t = ctx.sb(les, "e1t", 128, 2 * TT, F32)
                ob = ctx.sb(les, "ob", 128, 4 * TT, BF16)
                wt = ctx.stream("wt", 2)
                wt2 = ctx.stream("wt2", 2)
                bw = wt.base()
                bw2 = wt2.base()
                base_pe = ctx.c["pe"]
                base_dve = ctx.c["dve"]
                stq = ctx.stream("st", 4)
                base_st = stq.base()
                W0 = ctx.ap(f"Wpath{l}0")
                W1 = ctx.ap(f"Wpath{l}1")

                @blk.gpsimd
                def _(e):
                    for m in range(nm):
                        if m >= 2:
                            e.wait_ge(ctx.sem["pe"], base_pe + (m - 1) * NT)
                        wt.put(e, wb0[:, (m % 2) * nk * 128:(m % 2 + 1) * nk * 128]
                               .rearrange("p (k c) -> p k c", k=nk),
                               W0[m].rearrange("k p c -> p k c"), m % 2)
                        wt2.put(e, wb1[:, (m % 2) * nk * 128:(m % 2 + 1) * nk * 128]
                                .rearrange("p (k c) -> p k c", k=nk),
                                W1[m].rearrange("k p c -> p k c"), m % 2)

                @blk.tensor
                def _(e):
                    e.wait_ge(ctx.sem[rw[0]], rw[1])
                    for m in range(nm):
                        e.wait_ge(wt.sems[m % 2], bw[m % 2] + 16 * (m // 2 + 1))
                        e.wait_ge(wt2.sems[m % 2], bw2[m % 2] + 16 * (m // 2 + 1))
                        for t in range(NT):
                            j = m * NT + t
                            if j >= 2:
                                e.wait_ge(ctx.sem["dve"], base_dve + 4 * (j - 1))
                            o1 = (m % 2) * nk * 128
                            b0 = ctx.ps[(j % 2) * 2]
                            b1 = ctx.ps[(j % 2) * 2 + 1]
                            for k in range(nk):
                                e.matmul(out=b0[:], lhsT=wb0[:, o1 + k * 128:o1 + (k + 1) * 128],
                                         rhs=rhs[:, k * T + t * TT:k * T + t * TT + TT],
                                         start=(k == 0), stop=(k == nk - 1))
                            for k in range(nk):
                                ins = e.matmul(out=b1[:], lhsT=wb1[:, o1 + k * 128:o1 + (k + 1) * 128],
                                               rhs=rhs[:, k * T + t * TT:k * T + t * TT + TT],
                                               start=(k == 0), stop=(k == nk - 1))
                            ctx.inc(ins, "pe")

                @blk.sync
                def _(e):
                    for j in range(nm * NT):
                        e.wait_ge(ctx.sem["dve"], base_dve + 4 * (j + 1))
                        stq.put(e, ctx.ap(f"s4{l}_s")[
                            128 * (j // NT):128 * (j // NT) + 128,
                            (j % NT) * TT:(j % NT + 1) * TT],
                            ob[:, (j % 4) * TT:(j % 4 + 1) * TT], j % 4)
                    for sl in range(4):
                        e.wait_ge(stq.sems[sl], stq.cnt[sl])

                @blk.vector
                def _(e):
                    for j in range(nm * NT):
                        m = j // NT
                        e.wait_ge(ctx.sem["pe"], base_pe + j + 1)
                        ins = e.tensor_scalar(out=e0t[:, (j % 2) * TT:(j % 2 + 1) * TT],
                                              in0=ctx.ps[(j % 2) * 2][:],
                                              scalar1=pbc[:, l * 32 + m:l * 32 + m + 1],
                                              scalar2=modc[:, l * 32 + m:l * 32 + m + 1],
                                              op0=ALU.add, op1=ALU.mult)
                        e.wait_ge(ctx.sem["dve"], ctx.inc(ins, "dve"))
                        ins = e.tensor_scalar(out=e1t[:, (j % 2) * TT:(j % 2 + 1) * TT],
                                              in0=ctx.ps[(j % 2) * 2 + 1][:],
                                              scalar1=pbc[:, l * 32 + 16 + m:l * 32 + 17 + m],
                                              scalar2=modc[:, l * 32 + 16 + m:l * 32 + 17 + m],
                                              op0=ALU.add, op1=ALU.mult)
                        e.wait_ge(ctx.sem["dve"], ctx.inc(ins, "dve"))
                        ins = e.tensor_add(out=e0t[:, (j % 2) * TT:(j % 2 + 1) * TT],
                                           in0=e0t[:, (j % 2) * TT:(j % 2 + 1) * TT],
                                           in1=e1t[:, (j % 2) * TT:(j % 2 + 1) * TT])
                        e.wait_ge(ctx.sem["dve"], ctx.inc(ins, "dve"))
                        if j >= 4:
                            stq.wait(e, st_tok(ctx, base_st, j - 4))
                        ins = e.tensor_copy(out=ob[:, (j % 4) * TT:(j % 4 + 1) * TT],
                                            in_=e0t[:, (j % 2) * TT:(j % 2 + 1) * TT])
                        ctx.inc(ins, "dve")


def p_ent0(ctx):
    p_ent(ctx, 0)


def p_entp0(ctx):
    p_entp(ctx, 0)


def p_ln30(ctx):
    p_ln(ctx, "ln30", "h0_s", "s30_s", "hA0_s", gname="ln3g0", bname="ln3b0")


def p_path0(ctx):
    p_path(ctx, 0)


def p_ln40(ctx):
    p_ln(ctx, "ln40", "hA0_s", "s40_s", "hB0_s", gname="ln4g0", bname="ln4b0")


def p_ent1(ctx):
    p_ent(ctx, 1)


def p_entp1(ctx):
    p_entp(ctx, 1)


def p_ln31(ctx):
    p_ln(ctx, "ln31", "hB0_s", "s31_s", "hA1_s", gname="ln3g1", bname="ln3b1")


def p_path1(ctx):
    p_path(ctx, 1)


def p_ln41(ctx):
    p_ln(ctx, "ln41", "hA1_s", "s41_s", "hB1_s", gname="ln4g1", bname="ln4b1")


def p_lnout(ctx):
    p_ln(ctx, "lnout", "x3_s", "hB1_s", "outT", out_f32=True)


PHASES = {}


def _reg(name, fn, uses, defs):
    PHASES[name] = (fn, uses, defs)


_reg("init", p_init, [], [])
_reg("l0in", p_l0in, ["xT", "Wz", "Wdtin", "Wdtp", "dtb"],
     ["z_s", "xs_s", "b_s", "c_s", "dt_s"])
_reg("conv", p_conv, ["xs_s", "convw", "convb"], ["xact_s"])
_reg("scan", p_scan, ["dt_s", "xact_s", "b_s", "c_s", "Alog"], ["yssm_s"])
_reg("gate", p_gate, ["yssm_s", "xact_s", "z_s", "Dv"], ["yg_s"])
_reg("outproj", p_outproj, ["yg_s", "Wout0"], ["att0_s"])
_reg("ln0a", p_ln0a, ["xT", "att0_s"], ["x1_s"])
_reg("ffnh", p_ffnh, ["x1_s", "W1", "W2"], ["hid_s"])
_reg("ffn3", p_ffn3, ["hid_s", "W3"], ["ff0_s", "w3q_s"])
_reg("ln0b", p_ln0b, ["x1_s", "ff0_s"], ["x2_s"])
_reg("qkv", p_qkv, ["x2_s", "Wqkv"], ["qkv_s"])
_reg("attn", p_attn, ["qkv_s", "identb", "maskg", "mask0"], ["ao_s"])
_reg("wout", p_wout, ["ao_s", "Wo1"], ["att1_s"])
_reg("ln1a", p_ln1a, ["x2_s", "att1_s"], ["x3_s"])
_reg("l1fa", p_l1fa, ["x3_s", "Wamp", "Wph", "ampb", "phb", "phs0", "phs1",
                      "pathb0", "pathb1"], ["amp_s", "h0_s"])
_reg("ent0", p_ent0, ["h0_s", "Went0", "entb0"], ["e0_s"])
_reg("entp0", p_entp0, ["e0_s", "Wentp0", "entpb0"], ["s30_s"])
_reg("ln30", p_ln30, ["h0_s", "s30_s", "ln3g0", "ln3b0"], ["hA0_s"])
_reg("path0", p_path0, ["hA0_s", "Wpath00", "Wpath01"], ["s40_s"])
_reg("ln40", p_ln40, ["hA0_s", "s40_s", "ln4g0", "ln4b0"], ["hB0_s"])
_reg("ent1", p_ent1, ["hB0_s", "Went1", "entb1"], ["e1_s"])
_reg("entp1", p_entp1, ["e1_s", "Wentp1", "entpb1"], ["s31_s"])
_reg("ln31", p_ln31, ["hB0_s", "s31_s", "ln3g1", "ln3b1"], ["hA1_s"])
_reg("path1", p_path1, ["hA1_s", "Wpath10", "Wpath11"], ["s41_s"])
_reg("ln41", p_ln41, ["hA1_s", "s41_s", "ln4g1", "ln4b1"], ["hB1_s"])
_reg("lnout", p_lnout, ["x3_s", "hB1_s"], ["outT"])

ORDER = ["init", "l0in", "conv", "scan", "gate", "outproj", "ln0a",
         "ffnh", "ffn3", "ln0b", "qkv", "attn", "wout", "ln1a",
         "l1fa", "ent0", "entp0", "ln30", "path0", "ln40",
         "ent1", "entp1", "ln31", "path1", "ln41", "lnout"]
GROUPS = [list(ORDER)]


def build_group(phases, host_specs, keep_all=False):
    nc = bass.Bass("TRN2", target_bir_lowering=False, debug=False)
    used, defined = set(), set()
    for p in phases:
        _, u, d = PHASES[p]
        used.update(n for n in u if n not in defined)
        defined.update(d)
    later_used = set()
    idx = max(ORDER.index(p) for p in phases)
    for g_later in ORDER[idx + 1:]:
        later_used.update(PHASES[g_later][1])
    ext_in = {n for n in used if n in SCRATCH}
    ext_out = {n for n in defined
               if (n == "outT" or n in later_used or keep_all)}
    with ExitStack() as es:
        ctx = Ctx(nc, es, ext_in, ext_out, host_specs)
        ctx.cc = es.enter_context(
            nc.sbuf_tensor("ccpool", [128, len(CONSTS)], F32, side="right"))
        ctx.ps = [es.enter_context(nc.psum_tensor(f"psb{i}", [128, 512], F32))
                  for i in range(8)]
        for p in phases:
            with nc.named_scope(p):
                PHASES[p][0](ctx)
    return nc, [n for n in ctx.dram if n not in SCRATCH or n in ext_in], \
        sorted(n for n in ext_out if n in ctx.dram)


def host_specs_of(host):
    hs = {k: (v.shape, BF16 if v.dtype == BF else F32) for k, v in host.items()}
    hs["xT"] = ((2048, T), F32)
    return hs


def kernel(**inputs):
    host = prep_host(inputs)
    hs = host_specs_of(host)
    xTs = [np.ascontiguousarray(inputs["L0_attn_hidden_states"][i].T.astype(np.float32))
           for i in range(NCORE)]
    vals = [dict() for _ in range(NCORE)]
    for g in GROUPS:
        nc, ext_in, ext_out = build_group(g, hs)
        in_maps = []
        for i in range(NCORE):
            m = {}
            for n in ext_in:
                m[n] = xTs[i] if n == "xT" else host.get(n, vals[i].get(n))
            in_maps.append(m)
        res = run_bass_kernel_spmd(nc, in_maps, core_ids=list(range(NCORE)))
        for i in range(NCORE):
            for n in ext_out:
                vals[i][n] = np.asarray(res.results[i][n])
    out = np.stack([np.ascontiguousarray(vals[i]["outT"].T).astype(np.float32)
                    for i in range(NCORE)])
    return out


# version 03 (final)
# speedup vs baseline: 1.2593x; 1.0282x over previous
import numpy as np
import ml_dtypes
from contextlib import ExitStack
import concourse.bass as bass
import concourse.mybir as mybir
from concourse.bass_utils import run_bass_kernel_spmd

F32 = mybir.dt.float32
BF16 = mybir.dt.bfloat16
I32 = mybir.dt.int32
AF = mybir.ActivationFunctionType
ALU = mybir.AluOpType
AX = mybir.AxisListType
BF = ml_dtypes.bfloat16

T = 4096
TT = 512
NT = T // TT
NCORE = 8


def _tile4(m):
    K, F = m.shape
    assert K % 128 == 0 and F % 128 == 0, (K, F)
    return np.ascontiguousarray(
        m.reshape(K // 128, 128, F // 128, 128).transpose(2, 0, 1, 3))


def _col128(v):
    return np.ascontiguousarray(v.reshape(-1, 128).T)


def prep_host(inp):
    h = {}
    f32 = np.float32
    h["Wz"] = _tile4(inp["L0_attn_in_proj_w"].T.astype(f32))
    h["Wdtin"] = _tile4(inp["L0_attn_dt_in_proj_w"].T.astype(f32))
    h["Wdtp"] = _tile4(inp["L0_attn_dt_proj_w"].T.astype(f32))
    h["dtb"] = _col128(inp["L0_attn_dt_proj_b"].astype(f32))
    h["convw"] = np.ascontiguousarray(
        inp["L0_attn_conv_w"].astype(f32).reshape(32, 128, 4).transpose(1, 0, 2).reshape(128, 128))
    h["convb"] = _col128(inp["L0_attn_conv_b"].astype(f32))
    h["Alog"] = np.ascontiguousarray(
        inp["L0_attn_A_log"].astype(f32).reshape(32, 128, 16).transpose(1, 0, 2).reshape(128, 512))
    h["Dv"] = _col128(inp["L0_attn_D"].astype(f32))
    h["Wout0"] = _tile4(inp["L0_attn_out_proj_w"].T.astype(f32))
    h["W1"] = _tile4(inp["L0_ffn_w1"].T.astype(f32))
    h["W2"] = _tile4(inp["L0_ffn_w2"].T.astype(f32))
    h["W3"] = _tile4(inp["L0_ffn_w3"].T.astype(f32))
    h["Wqkv"] = _tile4(inp["L1_attn_w_qkv"].T.astype(f32))
    h["Wo1"] = _tile4(inp["L1_attn_w_out"].T.astype(f32))
    h["Wamp"] = _tile4(inp["L1_ffn_amp_w"].astype(f32))
    h["Wph"] = _tile4(inp["L1_ffn_ph_w"].astype(f32))
    h["ampb"] = _col128(inp["L1_ffn_amp_b"].astype(f32))
    h["phb"] = _col128(inp["L1_ffn_ph_b"].astype(f32))
    for l in range(2):
        h[f"Went{l}"] = _tile4(inp["L1_ffn_ent_w"][l].astype(f32))
        h[f"Wentp{l}"] = _tile4(inp["L1_ffn_ent_pw"][l].astype(f32))
        h[f"entb{l}"] = _col128(inp["L1_ffn_ent_b"][l].astype(f32))
        h[f"entpb{l}"] = _col128(inp["L1_ffn_ent_pb"][l].astype(f32))
        for p in range(2):
            h[f"Wpath{l}{p}"] = _tile4(inp["L1_ffn_path_w"][l, p].astype(f32))
        h[f"pathb{l}"] = np.ascontiguousarray(np.concatenate(
            [_col128(inp["L1_ffn_path_b"][l, p].astype(f32)) for p in range(2)], axis=1))
        h[f"phs{l}"] = np.ascontiguousarray(np.concatenate(
            [_col128(inp["L1_ffn_phase"][l, p].astype(f32)) for p in range(2)], axis=1))
        h[f"ln3g{l}"] = _col128(inp["L1_ffn_ln3_g"][l].astype(f32))
        h[f"ln3b{l}"] = _col128(inp["L1_ffn_ln3_b"][l].astype(f32))
        h[f"ln4g{l}"] = _col128(inp["L1_ffn_ln4_g"][l].astype(f32))
        h[f"ln4b{l}"] = _col128(inp["L1_ffn_ln4_b"][l].astype(f32))
    h["identb"] = np.ascontiguousarray(np.eye(128).astype(BF))
    kk = np.arange(1024)[:, None]
    qq = np.arange(512)[None, :]
    h["maskg"] = np.ascontiguousarray(
        np.where((kk >= qq) & (kk <= qq + 512), 0.0, -1e9).astype(BF))
    h["mask0"] = np.ascontiguousarray(
        np.where(kk[:512] <= qq, 0.0, -1e9).astype(BF))
    return h


_S = {}
for _n, _r in [("z_s", 4096), ("xs_s", 1024), ("b_s", 1024), ("c_s", 4096),
               ("dt_s", 4096), ("xact_s", 4096), ("yssm_s", 4096), ("yg_s", 4096),
               ("att0_s", 2048), ("x1_s", 2048), ("hid_s", 5632), ("ff0_s", 2048),
               ("x2_s", 2048), ("qkv_s", 6144), ("ao_s", 2048), ("att1_s", 2048),
               ("x3_s", 2048), ("amp_s", 2048), ("h0_s", 2048),
               ("e0_s", 8192), ("e1_s", 8192), ("s30_s", 2048), ("s31_s", 2048),
               ("hA0_s", 2048), ("hA1_s", 2048), ("s40_s", 2048), ("s41_s", 2048),
               ("hB0_s", 2048), ("hB1_s", 2048)]:
    _S[_n] = (_r, T, BF16, BF)
_S["w3q_s"] = (2048, 5632, BF16, BF)
_S["outT"] = (2048, T, F32, np.float32)
SCRATCH = _S



class Ctx:
    def __init__(self, nc, es, ext_in, ext_out, host_specs):
        self.nc = nc
        self.es = es
        self.blk = None
        self.ext_in = ext_in
        self.ext_out = ext_out
        self.host_specs = host_specs
        self.sem = {}
        self.c = {}
        for s in ["q0", "q2", "pe", "act", "dve", "pool", "pf", "wf", "st", "x0", "x1",
                  "pe2", "act2", "dve2", "pf2", "x02", "x12", "sta", "stb"]:
            self.sem[s] = es.enter_context(nc.semaphore(s))
            self.c[s] = 0
        self.dram = {}
        self.streams = {}
        self._uid = 0
        self.cc = None
        self.ps = None
        self.keep = {}

    def uid(self):
        self._uid += 1
        return self._uid

    def ap(self, name):
        if name in self.dram:
            return self.dram[name]
        if name in SCRATCH:
            r, c, dt, _ = SCRATCH[name]
            kind = ("ExternalInput" if name in self.ext_in else
                    "ExternalOutput" if name in self.ext_out else "Internal")
            t = self.nc.dram_tensor(name, [r, c], dt, kind=kind)
        else:
            shape, dt = self.host_specs[name]
            t = self.nc.dram_tensor(name, list(shape), dt, kind="ExternalInput")
        self.dram[name] = t.ap()
        return self.dram[name]

    def sb(self, es, tag, p, f, dt, side="left"):
        return es.enter_context(
            self.nc.sbuf_tensor(f"{tag}{self.uid()}", [p, f], dt, side=side))

    def psb(self, tag, p, f, dt):
        if tag not in self.keep:
            self.keep[tag] = self.sb(self.es, tag, p, f, dt, side="right")
        return self.keep[tag]

    def inc(self, ins, s, n=1):
        ins.then_inc(self.sem[s], n)
        self.c[s] += n
        return self.c[s]

    def stream(self, name, nslots):
        if name not in self.streams:
            self.streams[name] = Stream(self, name, nslots)
        st = self.streams[name]
        assert st.n == nslots, (name, nslots, st.n)
        return st


class Stream:

    def __init__(self, ctx, name, nslots):
        self.n = nslots
        self.sems = [ctx.es.enter_context(ctx.nc.semaphore(f"{name}{i}"))
                     for i in range(nslots)]
        self.cnt = [0] * nslots

    def put(self, e, out, in_, slot):
        d = e.dma_start(out=out, in_=in_)
        d.then_inc(self.sems[slot], 16)
        self.cnt[slot] += 16
        return (slot, self.cnt[slot])

    def wait(self, e, tok):
        e.wait_ge(self.sems[tok[0]], tok[1])

    def base(self):
        return list(self.cnt)


CONSTS = [0.0, 1.0, 0.5, -0.5, np.pi / 2, 1e-5, -1e9, 1.0 / 2048.0, np.pi,
          1.0 / (2 * np.pi), 2 * np.pi, 0.088388347648318447, 1e-8,
          1.0 / 5632.0]


def ccol(ctx, v):
    return ctx.cc[:, CONSTS.index(v):CONSTS.index(v) + 1]


def crow(ctx, v):
    return ctx.cc[0:1, CONSTS.index(v):CONSTS.index(v) + 1]


def slab_loader(ctx, wname, nk, nm, wbuf, lookahead=2, ntt=NT):
    W = ctx.ap(wname)
    wt = ctx.stream("wt", 2)
    bw = wt.base()
    base_pe = ctx.c["pe"]
    blk = ctx.blk

    @blk.gpsimd
    def _(e):
        for m in range(nm):
            if m >= lookahead:
                e.wait_ge(ctx.sem["pe"], base_pe + (m - lookahead + 1) * ntt)
            wt.put(e, wbuf[:, (m % 2) * nk * 128:(m % 2 + 1) * nk * 128]
                   .rearrange("p (k c) -> p k c", k=nk),
                   W[m].rearrange("k p c -> p k c"), m % 2)
    return lambda m: (m % 2, bw[m % 2] + 16 * (m // 2 + 1))


def load_cols(ctx, e, name, dst, q="q0"):
    d = e.dma_start(out=dst[:], in_=ctx.ap(name)[:, :])
    return ctx.inc(d, q, 16)


def mm_section(ctx, nk, nm, wbuf, rhs, w_ready, rhs_wait, bank_free, nbank=4,
               lookahead=2, rhs_stride=T, ntt=NT, c0=0):
    blk = ctx.blk
    base_pe = ctx.c["pe"]

    @blk.tensor
    def _(e):
        if rhs_wait is not None:
            e.wait_ge(ctx.sem[rhs_wait[0]], rhs_wait[1])
        for m in range(nm):
            tok = w_ready(m)
            e.wait_ge(ctx.streams["wt"].sems[tok[0]], tok[1])
            for t in range(ntt):
                j = m * ntt + t
                if j >= nbank:
                    sn, cv = bank_free(j - nbank)
                    e.wait_ge(ctx.sem[sn], cv)
                ps = ctx.ps[j % nbank]
                for k in range(nk):
                    ins = e.matmul(
                        out=ps[:],
                        lhsT=wbuf[:, (m % lookahead) * nk * 128 + k * 128:
                                  (m % lookahead) * nk * 128 + (k + 1) * 128],
                        rhs=rhs[:, k * rhs_stride + c0 + t * TT:
                                k * rhs_stride + c0 + t * TT + TT],
                        start=(k == 0), stop=(k == nk - 1))
                ctx.inc(ins, "pe")
    return base_pe


def store_section(ctx, njobs, ready_sem, ready_of, src_of, dst_of, nslots=4,
                  eng=None):
    blk = ctx.blk
    st = ctx.stream("st", 4)
    dec = getattr(blk, eng or "sync")

    @dec
    def _(e):
        for j in range(njobs):
            e.wait_ge(ctx.sem[ready_sem], ready_of(j))
            st.put(e, dst_of(j), src_of(j), j % nslots)
        for sl in range(nslots):
            e.wait_ge(st.sems[sl], st.cnt[sl])


def st_tok(ctx, base, j, nslots=4):
    return (j % nslots, base[j % nslots] + 16 * (j // nslots + 1))



def p_init(ctx):
    with ctx.nc.Block("init") as blk:
        ctx.blk = blk

        @blk.vector
        def _(e):
            for i, v in enumerate(CONSTS):
                e.memset(ctx.cc[:, i:i + 1], float(v))


def generic_linear(ctx, name, wname, nk, nm, rhs, rhs_wait, dst_of, epi=None,
                   nbank=4, ntt=NT, rhs_stride=T, c0=0, epi_eng="vector"):
    with ctx.nc.Block(name) as blk:
        ctx.blk = blk
        with ExitStack() as les:
            wbuf = ctx.sb(les, "wb", 128, 2 * nk * 128, BF16)
            ob = ctx.sb(les, "ob", 128, 4 * TT, BF16)
            w_ready = slab_loader(ctx, wname, nk, nm, wbuf, ntt=ntt)
            base_dve = ctx.c["dve"]
            base_pe = mm_section(ctx, nk, nm, wbuf, rhs, w_ready, rhs_wait,
                                 lambda j: ("dve", base_dve + j + 1), nbank=nbank,
                                 rhs_stride=rhs_stride, ntt=ntt, c0=c0)
            base_st = ctx.stream("st", 4).base()
            dec = getattr(blk, epi_eng)

            @dec
            def _(e):
                for j in range(nm * ntt):
                    e.wait_ge(ctx.sem["pe"], base_pe + j + 1)
                    if j >= 4:
                        ctx.streams["st"].wait(e, st_tok(ctx, base_st, j - 4))
                    buf = ob[:, (j % 4) * TT:(j % 4 + 1) * TT]
                    if epi is None:
                        ins = e.tensor_copy(out=buf, in_=ctx.ps[j % nbank][:])
                    else:
                        ins = epi(e, j // ntt, j % ntt, ctx.ps[j % nbank], buf)
                    ctx.inc(ins, "dve")

            store_section(ctx, nm * ntt, "dve", lambda j: base_dve + j + 1,
                          lambda j: ob[:, (j % 4) * TT:(j % 4 + 1) * TT],
                          lambda j: dst_of(j // ntt, j % ntt))


def rhs_load_block(ctx, name, src, nk, rhs, cast=False, c0=0, ncols=T):
    with ctx.nc.Block(name) as blk:
        ctx.blk = blk
        src_ap = ctx.ap(src)
        qn = "q2" if cast else "q0"
        eng = blk.gpsimd if cast else blk.sync

        @eng
        def _(e):
            for k in range(nk):
                d = e.dma_start(out=rhs[:, k * ncols:(k + 1) * ncols],
                                in_=src_ap[128 * k:128 * (k + 1), c0:c0 + ncols])
                ctx.inc(d, qn, 16)
    return (qn, ctx.c[qn])


def p_l0in(ctx):
    with ExitStack() as les:
        rhs = ctx.sb(les, "xres", 128, 16 * T, BF16)
        rw = rhs_load_block(ctx, "ldx", "xT", 16, rhs, cast=True)

        dsts = []
        for m in range(80):
            if m < 32:
                dsts.append(("z_s", m))
            elif m < 40:
                dsts.append(("xs_s", m - 32))
            elif m < 48:
                dsts.append(("b_s", m - 40))
            else:
                dsts.append(("c_s", m - 48))

        def dst_of(m, t):
            nm_, mm = dsts[m]
            return ctx.ap(nm_)[128 * mm:128 * mm + 128, t * TT:(t + 1) * TT]

        generic_linear(ctx, "inproj", "Wz", 16, 80, rhs, rw, dst_of)

        dtm = ctx.psb("dtmid", 128, T, BF16)
        with ctx.nc.Block("dtmid") as blk:
            ctx.blk = blk
            with ExitStack() as les2:
                wbuf = ctx.sb(les2, "wb", 128, 2 * 16 * 128, BF16)
                w_ready = slab_loader(ctx, "Wdtin", 16, 1, wbuf)
                base_dve = ctx.c["dve"]
                base_pe = mm_section(ctx, 16, 1, wbuf, rhs, w_ready, rw,
                                     lambda j: ("dve", base_dve + j + 1))

                @blk.vector
                def _(e):
                    for j in range(NT):
                        e.wait_ge(ctx.sem["pe"], base_pe + j + 1)
                        ins = e.tensor_copy(out=dtm[:, j * TT:(j + 1) * TT],
                                            in_=ctx.ps[j % 4][:])
                        ctx.inc(ins, "dve")

    dtm = ctx.keep["dtmid"]
    with ctx.nc.Block("dt") as blk:
        ctx.blk = blk
        with ExitStack() as les:
            wbuf = ctx.sb(les, "wb", 128, 2 * 128, BF16)
            bia = ctx.sb(les, "bia", 128, 32, F32)
            t1 = ctx.sb(les, "t1", 128, 4 * TT, F32)
            t2 = ctx.sb(les, "t2", 128, 4 * TT, F32)
            ob = ctx.sb(les, "ob", 128, 4 * TT, BF16)

            @blk.sync
            def _(e):
                load_cols(ctx, e, "dtb", bia)
            bia_n = ctx.c["q0"]

            w_ready = slab_loader(ctx, "Wdtp", 1, 32, wbuf)
            base_act = ctx.c["act"]
            base_pe = mm_section(ctx, 1, 32, wbuf, dtm, w_ready, ("q0", bia_n),
                                 lambda j: ("act", base_act + 4 * (j + 1)))
            base_dve = ctx.c["dve"]
            base_st = ctx.stream("st", 4).base()

            @blk.scalar
            def _(e):
                e.wait_ge(ctx.sem["q0"], bia_n)
                for j in range(32 * NT):
                    m = j // NT
                    e.wait_ge(ctx.sem["pe"], base_pe + j + 1)
                    if j >= 4:
                        e.wait_ge(ctx.sem["dve"], base_dve + j - 3)
                    ps = ctx.ps[j % 4]
                    ta = t1[:, (j % 4) * TT:(j % 4 + 1) * TT]
                    tb = t2[:, (j % 4) * TT:(j % 4 + 1) * TT]
                    b = bia[:, m:m + 1]
                    ins = e.activation(out=ta, in_=ps[:], func=AF.Abs, bias=b,
                                       scale=ccol(ctx, 1.0))
                    e.wait_ge(ctx.sem["act"], ctx.inc(ins, "act"))
                    ins = e.activation(out=ta, in_=ta, func=AF.Exp, scale=-1.0)
                    e.wait_ge(ctx.sem["act"], ctx.inc(ins, "act"))
                    ins = e.activation(out=ta, in_=ta, func=AF.Ln, bias=ccol(ctx, 1.0),
                                       scale=ccol(ctx, 1.0))
                    ctx.inc(ins, "act")
                    ins = e.activation(out=tb, in_=ps[:], func=AF.Relu, bias=b,
                                       scale=ccol(ctx, 1.0))
                    ctx.inc(ins, "act")

            @blk.vector
            def _(e):
                for j in range(32 * NT):
                    e.wait_ge(ctx.sem["act"], base_act + 4 * (j + 1))
                    if j >= 4:
                        ctx.streams["st"].wait(e, st_tok(ctx, base_st, j - 4))
                    o = ob[:, (j % 4) * TT:(j % 4 + 1) * TT]
                    ins = e.tensor_add(out=o, in0=t1[:, (j % 4) * TT:(j % 4 + 1) * TT],
                                       in1=t2[:, (j % 4) * TT:(j % 4 + 1) * TT])
                    ctx.inc(ins, "dve")

            store_section(ctx, 32 * NT, "dve", lambda j: base_dve + j + 1,
                          lambda j: ob[:, (j % 4) * TT:(j % 4 + 1) * TT],
                          lambda j: ctx.ap("dt_s")[128 * (j // NT):128 * (j // NT) + 128,
                                                   (j % NT) * TT:(j % NT + 1) * TT])


def p_conv(ctx):
    with ctx.nc.Block("conv") as blk:
        ctx.blk = blk
        with ExitStack() as les:
            xf = ctx.sb(les, "xf", 128, 2 * (T + 3), BF16)
            acc = ctx.sb(les, "acc", 128, 2 * T, F32)
            ob = ctx.sb(les, "ob", 128, 2 * T, BF16)
            wcv = ctx.sb(les, "wcv", 128, 128, F32)
            bcv = ctx.sb(les, "bcv", 128, 32, F32)
            xs = ctx.ap("xs_s")
            ld = ctx.stream("ld", 4)
            base_ld = ld.base()
            base_q0 = ctx.c["q0"]
            base_dve = ctx.c["dve"]
            base_act = ctx.c["act"]
            base_st = ctx.stream("st", 4).base()

            @blk.sync
            def _(e):
                load_cols(ctx, e, "convw", wcv)
                load_cols(ctx, e, "convb", bcv)
                for k in range(32):
                    if k >= 2:
                        e.wait_ge(ctx.sem["dve"], base_dve + 1 + 4 * (k - 1))
                    for g in range(2):
                        src = (xs[32 * k + 16 * g:32 * k + 16 * g + 16, :]
                               .unsqueeze(0).broadcast_to([4, 16, T]))
                        ld.put(e, xf[64 * g:64 * (g + 1),
                                     (k % 2) * (T + 3) + 3:(k % 2 + 1) * (T + 3)],
                               src, k % 2)

            @blk.vector
            def _(e):
                e.memset(xf[:, 0:3], 0.0)
                ins = e.memset(xf[:, T + 3:T + 6], 0.0)
                e.wait_ge(ctx.sem["dve"], ctx.inc(ins, "dve"))
                base_dve2 = ctx.c["dve"]
                for k in range(32):
                    e.wait_ge(ctx.sem["q0"], base_q0 + 32)
                    e.wait_ge(ld.sems[k % 2], base_ld[k % 2] + 32 * (k // 2 + 1))
                    if k >= 2:
                        e.wait_ge(ctx.sem["act"], base_act + k - 1)
                    x0 = (k % 2) * (T + 3)
                    a = acc[:, (k % 2) * T:(k % 2 + 1) * T]
                    ins = e.tensor_scalar(out=a, in0=xf[:, x0:x0 + T],
                                          scalar1=wcv[:, 4 * k:4 * k + 1],
                                          scalar2=None, op0=ALU.mult)
                    e.wait_ge(ctx.sem["dve"], ctx.inc(ins, "dve"))
                    for j in range(1, 4):
                        ins = e.scalar_tensor_tensor(
                            out=a, in0=xf[:, x0 + j:x0 + j + T],
                            scalar=wcv[:, 4 * k + j:4 * k + j + 1], in1=a,
                            op0=ALU.mult, op1=ALU.add)
                        n = ctx.inc(ins, "dve")
                        if j < 3:
                            e.wait_ge(ctx.sem["dve"], n)

            stq = ctx.stream("st", 4)

            @blk.scalar
            def _(e):
                for k in range(32):
                    e.wait_ge(ctx.sem["dve"], base_dve + 1 + 4 * (k + 1))
                    if k >= 2:
                        stq.wait(e, st_tok(ctx, base_st, k - 2, 2))
                    ins = e.activation(out=ob[:, (k % 2) * T:(k % 2 + 1) * T],
                                       in_=acc[:, (k % 2) * T:(k % 2 + 1) * T],
                                       func=AF.Silu, bias=bcv[:, k:k + 1],
                                       scale=ccol(ctx, 1.0))
                    n = ctx.inc(ins, "act")
                    e.wait_ge(ctx.sem["act"], n)
                    stq.put(e, ctx.ap("xact_s")[128 * k:128 * (k + 1), :],
                            ob[:, (k % 2) * T:(k % 2 + 1) * T], k % 2)
                for sl in range(2):
                    e.wait_ge(stq.sems[sl], stq.cnt[sl])


def p_scan(ctx):
    with ctx.nc.Block("scan") as blk:
        ctx.blk = blk
        with ExitStack() as les:
            dtk = ctx.sb(les, "dtk", 128, 2 * T, BF16)
            xak = ctx.sb(les, "xak", 128, 2 * T, BF16)
            dxk = ctx.sb(les, "dxk", 128, 2 * T, BF16)
            bnb = ctx.sb(les, "bnb", 128, 2 * T, BF16)
            cnb = ctx.sb(les, "cnb", 128, 2 * T, BF16)
            ab = ctx.sb(les, "ab", 128, 2 * T, F32)
            ub = ctx.sb(les, "ub", 128, 2 * T, BF16)
            hb = ctx.sb(les, "hb", 128, 2 * T, BF16)
            pr = ctx.sb(les, "pr", 128, 2 * T, BF16)
            ysb = ctx.sb(les, "ysb", 128, 2 * T, BF16)
            asb = ctx.sb(les, "asb", 128, 512, F32)
            idb = ctx.sb(les, "idb", 128, 128, BF16)
            bs = ctx.ap("b_s")
            cs = ctx.ap("c_s")

            @blk.sync
            def _(e):
                d = e.dma_start(out=asb[:], in_=ctx.ap("Alog")[:, :])
                ctx.inc(d, "q0", 16)
                d = e.dma_start(out=idb[:], in_=ctx.ap("identb")[:, :])
                ctx.inc(d, "q0", 16)
            pre_n = ctx.c["q0"]

            @blk.scalar
            def _(e):
                e.wait_ge(ctx.sem["q0"], pre_n)
                ins = e.activation(out=asb[:], in_=asb[:], func=AF.Exp)
                ctx.inc(ins, "act")
            aexp_n = ctx.c["act"]

            @blk.vector
            def _(e):
                e.wait_ge(ctx.sem["act"], aexp_n)
                ins = e.tensor_scalar_mul(out=asb[:], in0=asb[:], scalar1=-1.0)
                ctx.inc(ins, "dve")

            base = {sn: ctx.c[sn] for sn in
                    ["q0", "act", "dve", "pe", "wf", "x1"]}
            sd = ctx.stream("d", 2)
            sb_ = ctx.stream("b", 2)
            scb = ctx.stream("cb", 2)
            stq = ctx.stream("st", 4)
            bd, bb, bcb = sd.base(), sb_.base(), scb.base()
            bst = stq.base()
            d_tok = lambda k: (k % 2, bd[k % 2] + 32 * (k // 2 + 1))
            b_tok = lambda i: (i % 2, bb[i % 2] + 32 * (i // 2 + 1))
            c_tok = lambda i: (i % 2, bcb[i % 2] + 16 * (i // 2 + 1))

            @blk.sync
            def _(e):
                for k in range(32):
                    if k >= 2:
                        e.wait_ge(ctx.sem["act"], base["act"] + (k - 1) * 16)
                        e.wait_ge(ctx.sem["x1"], base["x1"] + k - 1)
                    sd.put(e, dtk[:, (k % 2) * T:(k % 2 + 1) * T],
                           ctx.ap("dt_s")[128 * k:128 * (k + 1), :], k % 2)
                    sd.put(e, xak[:, (k % 2) * T:(k % 2 + 1) * T],
                           ctx.ap("xact_s")[128 * k:128 * (k + 1), :], k % 2)
                    for n in range(16):
                        i = 16 * k + n
                        x0 = (i % 2) * T
                        if i >= 2:
                            e.wait_ge(ctx.sem["dve"], base["dve"] + 3 * (i - 2) + 1)
                        r0 = 32 * k + n
                        sb_.put(e, bnb[0:64, x0:x0 + T],
                                bs[r0:r0 + 1, :].partition_broadcast(64), i % 2)
                        sb_.put(e, bnb[64:128, x0:x0 + T],
                                bs[r0 + 16:r0 + 17, :].partition_broadcast(64), i % 2)

            @blk.scalar
            def _(e):
                for k in range(32):
                    for n in range(16):
                        i = 16 * k + n
                        x0 = (i % 2) * T
                        if i >= 2:
                            e.wait_ge(ctx.sem["dve"], base["dve"] + 3 * (i - 2) + 3)
                        src = (cs[128 * k:128 * (k + 1), :]
                               .rearrange("(g s) n -> g s n", s=16)[:, n, :]
                               .unsqueeze(1).broadcast_to([8, 16, T]))
                        scb.put(e, cnb[:, x0:x0 + T], src, i % 2)
                        if n == 0:
                            sd.wait(e, d_tok(k))
                        e.wait_ge(ctx.sem["dve"],
                                  base["dve"] + (3 * (i - 2) + 2 if i >= 2 else 0))
                        ins = e.activation(out=ab[:, x0:x0 + T],
                                           in_=dtk[:, (k % 2) * T:(k % 2 + 1) * T],
                                           func=AF.Exp, scale=asb[:, i:i + 1])
                        ctx.inc(ins, "act")
                    e.wait_ge(ctx.sem["pe"], base["pe"] + 16 * (k + 1))
                    if k >= 2:
                        stq.wait(e, (k % 2, bst[k % 2] + 16 * (k // 2)))
                    for t in range(NT):
                        ins = e.activation(out=ysb[:, (k % 2) * T + t * TT:
                                               (k % 2) * T + (t + 1) * TT],
                                           in_=ctx.ps[t][:], func=AF.Copy)
                        ctx.inc(ins, "wf")
                    e.wait_ge(ctx.sem["wf"], ctx.c["wf"])
                    stq.put(e, ctx.ap("yssm_s")[128 * k:128 * (k + 1), :],
                            ysb[:, (k % 2) * T:(k % 2 + 1) * T], k % 2)
                for sl in range(2):
                    e.wait_ge(stq.sems[sl], stq.cnt[sl])

            @blk.vector
            def _(e):
                for k in range(32):
                    sd.wait(e, d_tok(k))
                    if k >= 2:
                        e.wait_ge(ctx.sem["dve"], base["dve"] + 3 * (16 * (k - 1)))
                    ins = e.tensor_tensor(out=dxk[:, (k % 2) * T:(k % 2 + 1) * T],
                                          in0=dtk[:, (k % 2) * T:(k % 2 + 1) * T],
                                          in1=xak[:, (k % 2) * T:(k % 2 + 1) * T],
                                          op=ALU.mult)
                    e.wait_ge(ctx.sem["x1"], ctx.inc(ins, "x1"))
                    for n in range(16):
                        i = 16 * k + n
                        x0 = (i % 2) * T
                        sb_.wait(e, b_tok(i))
                        if i >= 2:
                            e.wait_ge(ctx.sem["dve"], base["dve"] + 3 * (i - 1))
                        ins = e.tensor_tensor(out=ub[:, x0:x0 + T],
                                              in0=dxk[:, (k % 2) * T:(k % 2 + 1) * T],
                                              in1=bnb[:, x0:x0 + T], op=ALU.mult)
                        e.wait_ge(ctx.sem["dve"], ctx.inc(ins, "dve"))
                        e.wait_ge(ctx.sem["act"], base["act"] + i + 1)
                        ins = e.tensor_tensor_scan(out=hb[:, x0:x0 + T],
                                                   data0=ab[:, x0:x0 + T],
                                                   data1=ub[:, x0:x0 + T],
                                                   initial=0.0, op0=ALU.mult,
                                                   op1=ALU.add)
                        e.wait_ge(ctx.sem["dve"], ctx.inc(ins, "dve"))
                        scb.wait(e, c_tok(i))
                        if i >= 2:
                            e.wait_ge(ctx.sem["pe"], base["pe"] + i - 1)
                        ins = e.tensor_tensor(out=pr[:, x0:x0 + T], in0=hb[:, x0:x0 + T],
                                              in1=cnb[:, x0:x0 + T], op=ALU.mult)
                        ctx.inc(ins, "dve")

            @blk.tensor
            def _(e):
                e.wait_ge(ctx.sem["q0"], pre_n)
                for k in range(32):
                    for n in range(16):
                        i = 16 * k + n
                        x0 = (i % 2) * T
                        e.wait_ge(ctx.sem["dve"], base["dve"] + 3 * i + 3)
                        if n == 0 and k >= 1:
                            e.wait_ge(ctx.sem["wf"], base["wf"] + 8 * k)
                        for t in range(NT):
                            ins = e.matmul(out=ctx.ps[t][:], lhsT=idb[:],
                                           rhs=pr[:, x0 + t * TT:x0 + (t + 1) * TT],
                                           start=(n == 0), stop=(n == 15))
                        ctx.inc(ins, "pe")


def p_gate(ctx):
    with ctx.nc.Block("gate") as blk:
        ctx.blk = blk
        with ExitStack() as les:
            ytk = ctx.sb(les, "ytk", 128, 2 * T, BF16)
            xak = ctx.sb(les, "xak", 128, 2 * T, BF16)
            zk = ctx.sb(les, "zk", 128, 2 * T, BF16)
            szk = ctx.sb(les, "szk", 128, 2 * T, BF16)
            t1 = ctx.sb(les, "t1", 128, 2 * T, BF16)
            ob = ctx.sb(les, "ob", 128, 2 * T, BF16)
            dvc = ctx.sb(les, "dvc", 128, 32, F32)
            ld = ctx.stream("ld", 4)
            base_ld = ld.base()
            ld_tok = lambda k: (k % 2, base_ld[k % 2] + 48 * (k // 2 + 1))
            stq = ctx.stream("st", 4)
            base_st = stq.base()
            base_q0 = ctx.c["q0"]
            base_act = ctx.c["act"]
            base_dve = ctx.c["dve"]

            @blk.sync
            def _(e):
                load_cols(ctx, e, "Dv", dvc)
                for k in range(32):
                    if k >= 2:
                        e.wait_ge(ctx.sem["dve"], base_dve + 2 * (k - 1))
                    for nm_, dst in [("yssm_s", ytk), ("xact_s", xak), ("z_s", zk)]:
                        ld.put(e, dst[:, (k % 2) * T:(k % 2 + 1) * T],
                               ctx.ap(nm_)[128 * k:128 * (k + 1), :], k % 2)

            @blk.scalar
            def _(e):
                e.wait_ge(ctx.sem["q0"], base_q0 + 16)
                for k in range(32):
                    ld.wait(e, ld_tok(k))
                    if k >= 2:
                        e.wait_ge(ctx.sem["dve"], base_dve + 2 * (k - 1))
                    ins = e.activation(out=szk[:, (k % 2) * T:(k % 2 + 1) * T],
                                       in_=zk[:, (k % 2) * T:(k % 2 + 1) * T],
                                       func=AF.Silu)
                    ctx.inc(ins, "act")
                    if k >= 2:
                        e.wait_ge(ctx.sem["dve"], base_dve + 2 * (k - 1) + 2)
                        stq.put(e, ctx.ap("yg_s")[128 * (k - 2):128 * (k - 1), :],
                                ob[:, (k % 2) * T:(k % 2 + 1) * T], k % 2)
                for k in (30, 31):
                    e.wait_ge(ctx.sem["dve"], base_dve + 2 * (k + 1))
                    stq.put(e, ctx.ap("yg_s")[128 * k:128 * (k + 1), :],
                            ob[:, (k % 2) * T:(k % 2 + 1) * T], k % 2)
                for sl in range(2):
                    e.wait_ge(stq.sems[sl], stq.cnt[sl])

            @blk.vector
            def _(e):
                e.wait_ge(ctx.sem["q0"], base_q0 + 16)
                for k in range(32):
                    ld.wait(e, ld_tok(k))
                    ins = e.scalar_tensor_tensor(
                        out=t1[:, (k % 2) * T:(k % 2 + 1) * T],
                        in0=xak[:, (k % 2) * T:(k % 2 + 1) * T],
                        scalar=dvc[:, k:k + 1],
                        in1=ytk[:, (k % 2) * T:(k % 2 + 1) * T],
                        op0=ALU.mult, op1=ALU.add)
                    e.wait_ge(ctx.sem["dve"], ctx.inc(ins, "dve"))
                    e.wait_ge(ctx.sem["act"], base_act + k + 1)
                    if k >= 2:
                        stq.wait(e, (k % 2, base_st[k % 2] + 16 * (k // 2)))
                    ins = e.tensor_tensor(out=ob[:, (k % 2) * T:(k % 2 + 1) * T],
                                          in0=t1[:, (k % 2) * T:(k % 2 + 1) * T],
                                          in1=szk[:, (k % 2) * T:(k % 2 + 1) * T],
                                          op=ALU.mult)
                    ctx.inc(ins, "dve")


def p_outproj(ctx):
    with ExitStack() as les:
        rhs = ctx.sb(les, "ygres", 128, 32 * 2048, BF16)
        for half in range(2):
            rw = rhs_load_block(ctx, f"ldyg{half}", "yg_s", 32, rhs,
                                c0=half * 2048, ncols=2048)

            def dst_of(m, t, _h=half):
                return ctx.ap("att0_s")[128 * m:128 * m + 128,
                                        _h * 2048 + t * TT:_h * 2048 + (t + 1) * TT]

            generic_linear(ctx, f"outproj{half}", "Wout0", 32, 16, rhs, rw,
                           dst_of, ntt=4, rhs_stride=2048)


def p_ln(ctx, name, srcA, srcB, dst, srcA_f32=False, gname=None, bname=None,
         out_f32=False):
    nk = 16
    with ctx.nc.Block(name) as blk:
        ctx.blk = blk
        with ExitStack() as les:
            at = ctx.sb(les, "at", 128, 4 * TT, F32 if srcA_f32 else BF16)
            bt = ctx.sb(les, "bt", 128, 4 * TT, BF16)
            s1 = ctx.sb(les, "s1", 128, 2 * nk * TT, BF16)
            sq = ctx.sb(les, "sq", 128, 2 * nk * TT, BF16)
            rows = ctx.sb(les, "rw", 1, 2 * 5 * TT, F32)
            onec = ctx.sb(les, "on", 128, 1, BF16)
            onef = ctx.sb(les, "onf", 1, 128, F32)
            rstdb = ctx.sb(les, "rsb", 128, 2 * TT, BF16)
            mbb = ctx.sb(les, "mbb", 128, 2 * TT, BF16)
            tmq = ctx.sb(les, "tmq", 128, 2 * TT, BF16)
            obx = ctx.sb(les, "obx", 128, 2 * nk * TT, F32 if out_f32 else BF16)
            gcols = ctx.sb(les, "gc", 128, 32, F32)
            apA, apB = ctx.ap(srcA), ctx.ap(srcB)
            ld = ctx.stream("ld", 4)
            stq = ctx.stream("st", 4)
            bld = ld.base()
            bst = stq.base()
            base = {sn: ctx.c[sn] for sn in
                    ["q0", "pe", "act", "dve", "pf", "wf", "x0", "x1"]}
            ld_tok = lambda i: (i % 4, bld[i % 4] + 32 * (i // 4 + 1))
            pe_stats, pe_bcast = {}, {}
            _c = base["pe"]
            for _t in range(NT + 1):
                if _t < NT:
                    for _k in range(nk):
                        _c += 1
                        pe_stats[(_t, _k)] = _c
                if _t >= 1:
                    _c += 1
                    pe_bcast[_t - 1] = _c
            mark = {}

            def s1v(t, k):
                return s1[:, (t % 2) * nk * TT + k * TT:(t % 2) * nk * TT + (k + 1) * TT]

            def sqv(t, k):
                return sq[:, (t % 2) * nk * TT + k * TT:(t % 2) * nk * TT + (k + 1) * TT]

            def obv(t, k):
                return obx[:, (t % 2) * nk * TT + k * TT:(t % 2) * nk * TT + (k + 1) * TT]

            def rw_(t, r):
                return rows[0:1, (t % 2) * 5 * TT + r * TT:(t % 2) * 5 * TT + (r + 1) * TT]

            @blk.sync
            def _(e):
                if gname:
                    load_cols(ctx, e, gname, gcols[:, 0:16])
                    load_cols(ctx, e, bname, gcols[:, 16:32])
                for t in range(NT):
                    for k in range(nk):
                        i = 16 * t + k
                        if i >= 4:
                            e.wait_ge(ctx.sem["x0"], base["x0"] + i - 3)
                        ld.put(e, at[:, (i % 4) * TT:(i % 4 + 1) * TT],
                               apA[128 * k:128 * (k + 1), t * TT:(t + 1) * TT], i % 4)
                        ld.put(e, bt[:, (i % 4) * TT:(i % 4 + 1) * TT],
                               apB[128 * k:128 * (k + 1), t * TT:(t + 1) * TT], i % 4)
            ngl = 32 if gname else 0

            @blk.vector
            def _(e):
                e.memset(onec[:], 1.0)
                ins = e.memset(onef[:], 1.0)
                ctx.inc(ins, "x1")
                for t in range(NT + 2):
                    if t >= 2:
                        t2 = t - 2
                        if t2 >= 2:
                            for sl in (2 * (t2 % 2), 2 * (t2 % 2) + 1):
                                e.wait_ge(stq.sems[sl],
                                          bst[sl] + 128 * ((t2 - 2) // 2 + 1))
                        for k in range(nk):
                            tq = tmq[:, (k % 2) * TT:(k % 2 + 1) * TT]
                            ins = e.tensor_tensor(
                                out=tq, in0=s1v(t2, k),
                                in1=rstdb[:, (t2 % 2) * TT:(t2 % 2 + 1) * TT],
                                op=ALU.mult)
                            e.wait_ge(ctx.sem["dve"], ctx.inc(ins, "dve"))
                            ins = e.tensor_tensor(
                                out=obv(t2, k), in0=tq,
                                in1=mbb[:, (t2 % 2) * TT:(t2 % 2 + 1) * TT],
                                op=ALU.subtract)
                            if gname:
                                e.wait_ge(ctx.sem["dve"], ctx.inc(ins, "dve"))
                                ins = e.tensor_scalar(out=obv(t2, k), in0=obv(t2, k),
                                                      scalar1=gcols[:, k:k + 1],
                                                      scalar2=gcols[:, 16 + k:17 + k],
                                                      op0=ALU.mult, op1=ALU.add)
                            ctx.inc(ins, "pf")
                    if t < NT:
                        for k in range(nk):
                            i = 16 * t + k
                            ld.wait(e, ld_tok(i))
                            ins = e.tensor_tensor(out=s1v(t, k),
                                                  in0=at[:, (i % 4) * TT:(i % 4 + 1) * TT],
                                                  in1=bt[:, (i % 4) * TT:(i % 4 + 1) * TT],
                                                  op=ALU.add)
                            ctx.inc(ins, "x0")
                    if 1 <= t <= NT:
                        t1 = t - 1
                        e.wait_ge(ctx.sem["pe"], pe_stats[(t1, nk - 1)])
                        mu, ms, var = rw_(t1, 0), rw_(t1, 1), rw_(t1, 2)
                        mb = rw_(t1, 4)
                        ins = e.tensor_scalar_mul(out=mu, in0=ctx.ps[2 * (t1 % 2)][0:1, :],
                                                  scalar1=1.0 / 2048.0)
                        e.wait_ge(ctx.sem["dve"], ctx.inc(ins, "dve"))
                        ins = e.tensor_scalar_mul(out=ms,
                                                  in0=ctx.ps[2 * (t1 % 2) + 1][0:1, :],
                                                  scalar1=1.0 / 2048.0)
                        e.wait_ge(ctx.sem["dve"], ctx.inc(ins, "dve"))
                        ins = e.tensor_tensor(out=var, in0=mu, in1=mu, op=ALU.mult)
                        e.wait_ge(ctx.sem["dve"], ctx.inc(ins, "dve"))
                        ins = e.tensor_sub(out=var, in0=ms, in1=var)
                        e.wait_ge(ctx.sem["dve"], ctx.inc(ins, "dve"))
                        mark[("var", t1)] = ctx.c["dve"]
                        e.wait_ge(ctx.sem["wf"], base["wf"] + t1 + 1)
                        ins = e.tensor_tensor(out=mb, in0=mu, in1=rw_(t1, 3),
                                              op=ALU.mult)
                        e.wait_ge(ctx.sem["dve"], ctx.inc(ins, "dve"))
                        mark[("mb", t1)] = ctx.c["dve"]
                        e.wait_ge(ctx.sem["pe"], pe_bcast[t1])
                        ins = e.tensor_copy(out=rstdb[:, (t1 % 2) * TT:(t1 % 2 + 1) * TT],
                                            in_=ctx.ps[4 + (t1 % 2)][:])
                        e.wait_ge(ctx.sem["dve"], ctx.inc(ins, "dve"))
                        ins = e.tensor_copy(out=mbb[:, (t1 % 2) * TT:(t1 % 2 + 1) * TT],
                                            in_=ctx.ps[6 + (t1 % 2)][:])
                        e.wait_ge(ctx.sem["dve"], ctx.inc(ins, "dve"))
                        mark[("evac", t1)] = ctx.c["dve"]

            @blk.scalar
            def _(e):
                e.wait_ge(ctx.sem["q0"], base["q0"] + ngl)
                for t in range(NT + 2):
                    if t < NT:
                        for k in range(nk):
                            i = 16 * t + k
                            e.wait_ge(ctx.sem["x0"], base["x0"] + i + 1)
                            if t >= 2 and k == 0:
                                e.wait_ge(ctx.sem["pe"], pe_stats[(t - 2, nk - 1)])
                            ins = e.activation(out=sqv(t, k), in_=s1v(t, k),
                                               func=AF.Square)
                            ctx.inc(ins, "act")
                    if 1 <= t <= NT:
                        t1 = t - 1
                        e.wait_ge(ctx.sem["dve"], mark[("var", t1)])
                        var = rw_(t1, 2)
                        rstd = rw_(t1, 3)
                        ins = e.activation(out=rstd, in_=var, func=AF.Ln,
                                           bias=crow(ctx, 1e-5), scale=crow(ctx, 1.0))
                        e.wait_ge(ctx.sem["x1"], ctx.inc(ins, "x1"))
                        ins = e.activation(out=rstd, in_=rstd, func=AF.Exp, scale=-0.5)
                        ctx.inc(ins, "wf")
                    if t >= 2:
                        t2 = t - 2
                        for k in range(nk):
                            i2 = 16 * t2 + k
                            e.wait_ge(ctx.sem["pf"], base["pf"] + i2 + 1)
                            stq.put(e, ctx.ap(dst)[128 * k:128 * (k + 1),
                                                   t2 * TT:(t2 + 1) * TT],
                                    obv(t2, k), 2 * (t2 % 2) + k % 2)
                for sl in range(4):
                    e.wait_ge(stq.sems[sl], stq.cnt[sl])

            @blk.tensor
            def _(e):
                e.wait_ge(ctx.sem["x1"], base["x1"] + 1)
                for t in range(NT + 1):
                    if t < NT:
                        for k in range(nk):
                            i = 16 * t + k
                            e.wait_ge(ctx.sem["x0"], base["x0"] + i + 1)
                            e.wait_ge(ctx.sem["act"], base["act"] + i + 1)
                            if k == 0 and t >= 2:
                                e.wait_ge(ctx.sem["dve"], mark[("var", t - 2)])
                            e.matmul(out=ctx.ps[2 * (t % 2)][0:1, :], lhsT=onec[:],
                                     rhs=s1v(t, k), start=(k == 0), stop=(k == nk - 1))
                            ins = e.matmul(out=ctx.ps[2 * (t % 2) + 1][0:1, :],
                                           lhsT=onec[:], rhs=sqv(t, k),
                                           start=(k == 0), stop=(k == nk - 1))
                            ctx.inc(ins, "pe")
                    if t >= 1:
                        t1 = t - 1
                        e.wait_ge(ctx.sem["wf"], base["wf"] + t1 + 1)
                        e.wait_ge(ctx.sem["dve"], mark[("mb", t1)])
                        if t1 >= 2:
                            e.wait_ge(ctx.sem["dve"], mark[("evac", t1 - 2)])
                        e.matmul(out=ctx.ps[4 + (t1 % 2)][:], lhsT=onef[:],
                                 rhs=rw_(t1, 3), start=True, stop=True)
                        ins = e.matmul(out=ctx.ps[6 + (t1 % 2)][:], lhsT=onef[:],
                                       rhs=rw_(t1, 4), start=True, stop=True)
                        ctx.inc(ins, "pe")


def p_ln0a(ctx):
    p_ln(ctx, "ln0a", "xT", "att0_s", "x1_s", srcA_f32=True)



def _gamma_thresh(ctx, name, wname, nslab, nk, thr, col):
    with ctx.nc.Block(name) as blk:
        ctx.blk = blk
        with ExitStack() as les:
            wbuf = ctx.sb(les, "gw", 128, 2 * nk * 128, F32)
            asum = ctx.sb(les, "as", 128, 2, F32)
            red = ctx.sb(les, "rd", 128, 2, F32)
            onef = ctx.sb(les, "onf", 128, 1, F32)
            onr = ctx.sb(les, "onr", 1, 128, F32)
            tot = ctx.sb(les, "tot", 1, 2, F32)
            W = ctx.ap(wname)
            ld = ctx.stream("ld", 4)
            base_ld = ld.base()
            base_dve = ctx.c["dve"]

            @blk.sync
            def _(e):
                for m in range(nslab):
                    if m >= 2:
                        e.wait_ge(ctx.sem["dve"], base_dve + 3 + 2 * (m - 1))
                    ld.put(e, wbuf[:, (m % 2) * nk * 128:(m % 2 + 1) * nk * 128]
                           .rearrange("p (k c) -> p k c", k=nk),
                           W[m].rearrange("k p c -> p k c"), m % 2)

            @blk.vector
            def _(e):
                ins = e.memset(asum[:, 0:1], 0.0)
                e.wait_ge(ctx.sem["dve"], ctx.inc(ins, "dve"))
                ins = e.memset(onef[:], 1.0)
                e.wait_ge(ctx.sem["dve"], ctx.inc(ins, "dve"))
                ins = e.memset(onr[:], 1.0)
                e.wait_ge(ctx.sem["dve"], ctx.inc(ins, "dve"))
                for m in range(nslab):
                    e.wait_ge(ld.sems[m % 2], base_ld[m % 2] + 16 * (m // 2 + 1))
                    ins = e.tensor_reduce(out=red[:, 0:1],
                                          in_=wbuf[:, (m % 2) * nk * 128:(m % 2 + 1) * nk * 128],
                                          axis=AX.X, op=ALU.add, apply_absolute_value=True)
                    e.wait_ge(ctx.sem["dve"], ctx.inc(ins, "dve"))
                    ins = e.tensor_add(out=asum[:, 0:1], in0=asum[:, 0:1], in1=red[:, 0:1])
                    e.wait_ge(ctx.sem["dve"], ctx.inc(ins, "dve"))
                ctx.inc(e.engine_nop(), "x0")

            @blk.tensor
            def _(e):
                e.wait_ge(ctx.sem["x0"], ctx.c["x0"])
                ins = e.matmul(out=ctx.ps[0][0:1, 0:1], lhsT=asum[:, 0:1],
                               rhs=onef[:], start=True, stop=True)
                ctx.inc(ins, "pe")

            @blk.vector
            def _(e):
                e.wait_ge(ctx.sem["pe"], ctx.c["pe"])
                sc = 0.5 / (nslab * nk * 128 * 128)
                ins = e.tensor_scalar(out=tot[0:1, 0:1], in0=ctx.ps[0][0:1, 0:1],
                                      scalar1=sc, scalar2=0.5e-8,
                                      op0=ALU.mult, op1=ALU.add)
                e.wait_ge(ctx.sem["dve"], ctx.inc(ins, "dve"))
                ins = e.tensor_scalar_mul(out=tot[0:1, 1:2], in0=tot[0:1, 0:1],
                                          scalar1=-1.0)
                e.wait_ge(ctx.sem["dve"], ctx.inc(ins, "dve"))
                ctx.inc(e.engine_nop(), "x1")

            @blk.tensor
            def _(e):
                e.wait_ge(ctx.sem["x1"], ctx.c["x1"])
                ins = e.matmul(out=ctx.ps[1][:, 0:2], lhsT=onr[:],
                               rhs=tot[0:1, 0:2], start=True, stop=True)
                ctx.inc(ins, "pe")

            @blk.vector
            def _(e):
                e.wait_ge(ctx.sem["pe"], ctx.c["pe"])
                ins = e.tensor_copy(out=thr[:, col:col + 2], in_=ctx.ps[1][:, 0:2])
                ctx.inc(ins, "dve")


def _quant_ops(ctx, e, wsrc, tmp1, tmp2, qdst, thr, col):
    e.tensor_single_scalar(out=tmp1, in_=wsrc, scalar=thr[:, col:col + 1], op=ALU.is_gt)
    ins = e.tensor_single_scalar(out=tmp2, in_=wsrc, scalar=thr[:, col + 1:col + 2],
                                 op=ALU.is_lt)
    e.wait_ge(ctx.sem["dve"], ctx.inc(ins, "dve"))
    ins = e.tensor_sub(out=qdst, in0=tmp1, in1=tmp2)
    return ins


def p_ffnh(ctx):
    thr = ctx.psb("thr", 128, 8, F32)
    _gamma_thresh(ctx, "gam1", "W1", 44, 16, thr, 0)
    _gamma_thresh(ctx, "gam2", "W2", 44, 16, thr, 2)
    nk, nm = 16, 44
    with ExitStack() as les0:
        rhs = ctx.sb(les0, "x1res", 128, 16 * T, BF16)
        rw = rhs_load_block(ctx, "ldx1", "x1_s", 16, rhs)
        with ctx.nc.Block("ffnh") as blk:
            ctx.blk = blk
            with ExitStack() as les:
                wf1 = ctx.sb(les, "wf1", 128, 2 * nk * 128, F32)
                wf2 = ctx.sb(les, "wf2", 128, 2 * nk * 128, F32)
                tq1 = ctx.sb(les, "tq1", 128, nk * 128, F32)
                tq2 = ctx.sb(les, "tq2", 128, nk * 128, F32)
                wq1 = ctx.sb(les, "wq1", 128, 2 * nk * 128, BF16)
                wq2 = ctx.sb(les, "wq2", 128, 2 * nk * 128, BF16)
                sg = ctx.sb(les, "sg", 128, 2 * TT, BF16)
                ob = ctx.sb(les, "ob", 128, 4 * TT, BF16)
                ld = ctx.stream("ld", 4)
                base_ld = ld.base()
                base_pe = ctx.c["pe"]
                base_dve = ctx.c["dve"]
                base_act = ctx.c["act"]
                base_pf = ctx.c["pf"]
                base_st = ctx.stream("st", 4).base()
                stq = ctx.stream("st", 4)

                @blk.sync
                def _(e):
                    for m in range(nm):
                        if m >= 2:
                            e.wait_ge(ctx.sem["dve"], base_dve + 6 * (m - 1))
                        ld.put(e, wf1[:, (m % 2) * nk * 128:(m % 2 + 1) * nk * 128]
                               .rearrange("p (k c) -> p k c", k=nk),
                               ctx.ap("W1")[m].rearrange("k p c -> p k c"), m % 2)
                        ld.put(e, wf2[:, (m % 2) * nk * 128:(m % 2 + 1) * nk * 128]
                               .rearrange("p (k c) -> p k c", k=nk),
                               ctx.ap("W2")[m].rearrange("k p c -> p k c"), 2 + m % 2)

                def quantize(e, m):
                    e.wait_ge(ld.sems[m % 2], base_ld[m % 2] + 16 * (m // 2 + 1))
                    e.wait_ge(ld.sems[2 + m % 2], base_ld[2 + m % 2] + 16 * (m // 2 + 1))
                    if m >= 2:
                        e.wait_ge(ctx.sem["pe"], base_pe + (m - 1) * NT)
                    o1 = (m % 2) * nk * 128
                    ins = _quant_ops(ctx, e, wf1[:, o1:o1 + nk * 128], tq1[:],
                                     tq2[:], wq1[:, o1:o1 + nk * 128], thr, 0)
                    e.wait_ge(ctx.sem["dve"], ctx.inc(ins, "dve"))
                    ins = _quant_ops(ctx, e, wf2[:, o1:o1 + nk * 128], tq1[:],
                                     tq2[:], wq2[:, o1:o1 + nk * 128], thr, 2)
                    e.wait_ge(ctx.sem["dve"], ctx.inc(ins, "dve"))
                    ctx.inc(e.engine_nop(), "dve")
                    ctx.inc(e.engine_nop(), "dve")

                @blk.tensor
                def _(e):
                    e.wait_ge(ctx.sem[rw[0]], rw[1])
                    for m in range(nm):
                        e.wait_ge(ctx.sem["dve"], base_dve + 6 * (m + 1))
                        for t in range(NT):
                            j = m * NT + t
                            if j >= 2:
                                e.wait_ge(ctx.sem["pf"], base_pf + j - 1)
                            o1 = (m % 2) * nk * 128
                            b0 = ctx.ps[(j % 2) * 2]
                            b1 = ctx.ps[(j % 2) * 2 + 1]
                            for k in range(nk):
                                e.matmul(out=b0[:], lhsT=wq1[:, o1 + k * 128:o1 + (k + 1) * 128],
                                         rhs=rhs[:, k * T + t * TT:k * T + t * TT + TT],
                                         start=(k == 0), stop=(k == nk - 1))
                            for k in range(nk):
                                ins = e.matmul(out=b1[:], lhsT=wq2[:, o1 + k * 128:o1 + (k + 1) * 128],
                                               rhs=rhs[:, k * T + t * TT:k * T + t * TT + TT],
                                               start=(k == 0), stop=(k == nk - 1))
                            ctx.inc(ins, "pe")

                def dst_hid(j):
                    return ctx.ap("hid_s")[128 * (j // NT):128 * (j // NT) + 128,
                                           (j % NT) * TT:(j % NT + 1) * TT]

                @blk.scalar
                def _(e):
                    for j in range(nm * NT):
                        e.wait_ge(ctx.sem["pe"], base_pe + j + 1)
                        if j >= 2:
                            e.wait_ge(ctx.sem["pf"], base_pf + j - 1)
                        ins = e.activation(out=sg[:, (j % 2) * TT:(j % 2 + 1) * TT],
                                           in_=ctx.ps[(j % 2) * 2][:], func=AF.Sigmoid)
                        ctx.inc(ins, "act")
                        if j >= 1:
                            e.wait_ge(ctx.sem["pf"], base_pf + j)
                            stq.put(e, dst_hid(j - 1),
                                    ob[:, ((j - 1) % 4) * TT:((j - 1) % 4 + 1) * TT],
                                    (j - 1) % 4)
                    e.wait_ge(ctx.sem["pf"], base_pf + nm * NT)
                    stq.put(e, dst_hid(nm * NT - 1),
                            ob[:, ((nm * NT - 1) % 4) * TT:((nm * NT - 1) % 4 + 1) * TT],
                            (nm * NT - 1) % 4)
                    for sl in range(4):
                        e.wait_ge(stq.sems[sl], stq.cnt[sl])

                @blk.vector
                def _(e):
                    quantize(e, 0)
                    quantize(e, 1)
                    for j in range(nm * NT):
                        e.wait_ge(ctx.sem["act"], base_act + j + 1)
                        if j >= 4:
                            stq.wait(e, st_tok(ctx, base_st, j - 4))
                        ins = e.tensor_tensor(out=ob[:, (j % 4) * TT:(j % 4 + 1) * TT],
                                              in0=sg[:, (j % 2) * TT:(j % 2 + 1) * TT],
                                              in1=ctx.ps[(j % 2) * 2 + 1][:], op=ALU.mult)
                        ctx.inc(ins, "pf")
                        if j % NT == NT - 1 and j // NT + 2 < nm:
                            quantize(e, j // NT + 2)


def p_ffn3(ctx):
    thr = ctx.psb("thr", 128, 8, F32)
    _gamma_thresh(ctx, "gam3", "W3", 16, 44, thr, 4)
    with ctx.nc.Block("q3") as blk:
        ctx.blk = blk
        with ExitStack() as les:
            wf = ctx.sb(les, "wf", 128, 2 * 44 * 128, F32)
            tq1 = ctx.sb(les, "tq1", 128, 44 * 128, F32)
            tq2 = ctx.sb(les, "tq2", 128, 44 * 128, F32)
            qb = ctx.sb(les, "qb", 128, 2 * 44 * 128, BF16)
            ld = ctx.stream("ld", 4)
            base_ld = ld.base()
            base_dve = ctx.c["dve"]
            stq = ctx.stream("st", 4)
            base_st = stq.base()

            @blk.sync
            def _(e):
                for m in range(16):
                    if m >= 2:
                        e.wait_ge(ctx.sem["dve"], base_dve + 3 * (m - 1))
                    ld.put(e, wf[:, (m % 2) * 44 * 128:(m % 2 + 1) * 44 * 128]
                           .rearrange("p (k c) -> p k c", k=44),
                           ctx.ap("W3")[m].rearrange("k p c -> p k c"), m % 2)

            @blk.vector
            def _(e):
                for m in range(16):
                    e.wait_ge(ld.sems[m % 2], base_ld[m % 2] + 16 * (m // 2 + 1))
                    if m >= 2:
                        stq.wait(e, st_tok(ctx, base_st, m - 2, 2))
                    o = (m % 2) * 44 * 128
                    ins = _quant_ops(ctx, e, wf[:, o:o + 44 * 128], tq1[:], tq2[:],
                                     qb[:, o:o + 44 * 128], thr, 4)
                    e.wait_ge(ctx.sem["dve"], ctx.inc(ins, "dve"))
                    ctx.inc(e.engine_nop(), "dve")

            store_section(ctx, 16, "dve", lambda m: base_dve + 3 * (m + 1),
                          lambda m: qb[:, (m % 2) * 44 * 128:(m % 2 + 1) * 44 * 128],
                          lambda m: ctx.ap("w3q_s")[128 * m:128 * (m + 1), :],
                          nslots=2, eng="scalar")

    with ExitStack() as les0:
        rhs = ctx.sb(les0, "hidres", 128, 44 * 1024, BF16)
        for tb in range(4):
            rw = rhs_load_block(ctx, f"ldh{tb}", "hid_s", 44, rhs,
                                c0=tb * 1024, ncols=1024)
            with ctx.nc.Block(f"ffn3_{tb}") as blk:
                ctx.blk = blk
                with ExitStack() as les:
                    wq = ctx.sb(les, "wq", 128, 2 * 44 * 128, BF16)
                    ob = ctx.sb(les, "ob", 128, 4 * TT, BF16)
                    ld = ctx.stream("ld", 4)
                    base_ld = ld.base()
                    base_pe = ctx.c["pe"]
                    base_dve = ctx.c["dve"]
                    stq = ctx.stream("st", 4)
                    base_st = stq.base()

                    @blk.sync
                    def _(e):
                        for m in range(16):
                            if m >= 2:
                                e.wait_ge(ctx.sem["pe"], base_pe + 2 * (m - 1))
                            ld.put(e, wq[:, (m % 2) * 44 * 128:(m % 2 + 1) * 44 * 128],
                                   ctx.ap("w3q_s")[128 * m:128 * (m + 1), :], m % 2)

                    @blk.tensor
                    def _(e):
                        e.wait_ge(ctx.sem[rw[0]], rw[1])
                        for m in range(16):
                            e.wait_ge(ld.sems[m % 2], base_ld[m % 2] + 16 * (m // 2 + 1))
                            for t in range(2):
                                j = 2 * m + t
                                if j >= 4:
                                    e.wait_ge(ctx.sem["dve"], base_dve + j - 3)
                                o = (m % 2) * 44 * 128
                                for k in range(44):
                                    ins = e.matmul(
                                        out=ctx.ps[j % 4][:],
                                        lhsT=wq[:, o + k * 128:o + (k + 1) * 128],
                                        rhs=rhs[:, k * 1024 + t * TT:k * 1024 + t * TT + TT],
                                        start=(k == 0), stop=(k == 43))
                                ctx.inc(ins, "pe")

                    @blk.vector
                    def _(e):
                        for j in range(32):
                            e.wait_ge(ctx.sem["pe"], base_pe + j + 1)
                            if j >= 4:
                                stq.wait(e, st_tok(ctx, base_st, j - 4))
                            ins = e.tensor_copy(out=ob[:, (j % 4) * TT:(j % 4 + 1) * TT],
                                                in_=ctx.ps[j % 4][:])
                            ctx.inc(ins, "dve")

                    store_section(ctx, 32, "dve", lambda j: base_dve + j + 1,
                                  lambda j: ob[:, (j % 4) * TT:(j % 4 + 1) * TT],
                                  lambda j: ctx.ap("ff0_s")[
                                      128 * (j // 2):128 * (j // 2) + 128,
                                      tb * 1024 + (j % 2) * TT:tb * 1024 + (j % 2 + 1) * TT],
                                  eng="scalar")


def p_ln0b(ctx):
    p_ln(ctx, "ln0b", "x1_s", "ff0_s", "x2_s")



def p_qkv(ctx):
    with ExitStack() as les:
        rhs = ctx.sb(les, "x2res", 128, 16 * T, BF16)
        rw = rhs_load_block(ctx, "ldx2", "x2_s", 16, rhs)
        generic_linear(ctx, "qkv", "Wqkv", 16, 48, rhs, rw,
                       lambda m, t: ctx.ap("qkv_s")[128 * m:128 * m + 128,
                                                    t * TT:(t + 1) * TT])


def _attn_emit(ctx, h, ci, cst, bufs, S):
    nw = 4 if ci == 0 else 8
    qkv = ctx.ap("qkv_s")
    mk = cst["mk0"] if ci == 0 else cst["mkg"]
    if True:
        blk = ctx.blk
        if True:
            qt, kw, vw, vtm = bufs["qt"], bufs["kw"], bufs["vw"], bufs["vtm"]
            pre2, tmp, au = bufs["pre2"], bufs["tmp"], bufs["au"]
            mxb, cred, mrow = bufs["mxb"], bufs["cred"], bufs["mrow"]
            srow, rrow, mxs = bufs["srow"], bufs["rrow"], bufs["mxs"]
            rbs, ob = bufs["rbs"], bufs["ob"]
            ld = bufs["ld"]
            base = {sn: ctx.c[S[sn]] for sn in
                    ["pe", "act", "dve", "pf", "x0", "x1", "st"]}
            kv0 = (ci - 1) * TT if ci > 0 else 0
            toks = {}
            nhalf = nw // 4
            PE_VT = 1
            PE_R1 = 2
            PE_MAX = lambda qb_: 2 + nhalf * (qb_ + 1)
            PE_MROW = 2 + 4 * nhalf + 1
            PE_MAXB = PE_MROW + 1
            PE_R2 = PE_MAXB + (1 if nw == 8 else 0)
            PE_SUM = PE_R2 + 1
            PE_RECB = PE_SUM + 1
            PE_AV = PE_RECB + 1
            DVE_VTM = 1
            DVE_MXS = DVE_VTM + 8 + 1

            @blk.sync
            def _(e):
                e.wait_ge(ctx.sem[S["pe"]], base["pe"])
                toks[0] = ld.put(e, qt[:], qkv[128 * h:128 * (h + 1),
                                              ci * TT:(ci + 1) * TT], 0)
                toks[1] = ld.put(e, kw[:, 0:nw * 128],
                                 qkv[2048 + 128 * h:2048 + 128 * (h + 1),
                                     kv0:kv0 + nw * 128], 1)
                toks[2] = ld.put(e, vw[:, 0:nw * 128],
                                 qkv[4096 + 128 * h:4096 + 128 * (h + 1),
                                     kv0:kv0 + nw * 128], 2)
                e.wait_ge(ctx.sem[S["x1"]], base["x1"] + 3)
                d = e.dma_start(out=ctx.ap("ao_s")[128 * h:128 * (h + 1),
                                                   ci * TT:(ci + 1) * TT], in_=ob[:])
                d.then_inc(ctx.sem[S["st"]], 16)
                ctx.c[S["st"]] += 16

            @blk.tensor
            def _(e):
                e.wait_ge(ctx.sem[S["x1"]], base["x1"])
                e.wait_ge(ctx.sem[S["ox1"]], ctx.c[S["ox1"]])
                ld.wait(e, toks[2])
                for j in range(nw):
                    ins = e.matmul(out=ctx.ps[4 + j // 4][:, (j % 4) * 128:(j % 4 + 1) * 128],
                                   lhsT=vw[:, j * 128:(j + 1) * 128], rhs=cst["idb"][:],
                                   start=(j % 4 == 0), stop=(j % 4 == 3 or j == nw - 1))
                ctx.inc(ins, S["pe"])
                ld.wait(e, toks[0])
                ld.wait(e, toks[1])
                e.wait_ge(ctx.sem[S["dve"]], base["dve"] + DVE_VTM)
                for j in range(min(nw, 4)):
                    ins = e.matmul(out=ctx.ps[j][:], lhsT=kw[:, j * 128:(j + 1) * 128],
                                   rhs=qt[:], start=True, stop=True)
                ctx.inc(ins, S["pe"])
                for qb_ in range(4):
                    bks = (4, 5) if qb_ % 2 == 0 else (6, 7)
                    if qb_ >= 2:
                        e.wait_ge(ctx.sem[S["x0"]], base["x0"] + qb_ - 1)
                    for half in range(nhalf):
                        ins = e.matmul(out=ctx.ps[bks[half]][:],
                                       lhsT=qt[:, qb_ * 128:(qb_ + 1) * 128],
                                       rhs=kw[:, half * 512:half * 512 + min(512, nw * 128)],
                                       start=True, stop=True)
                        ctx.inc(ins, S["pe"])
                e.wait_ge(ctx.sem[S["x0"]], base["x0"] + 4)
                for qb_ in range(4):
                    ins = e.matmul(out=ctx.ps[6][0:1, qb_ * 128:(qb_ + 1) * 128],
                                   lhsT=mxb[:, qb_:qb_ + 1], rhs=cst["idb"][:],
                                   start=(qb_ == 0), stop=(qb_ == 3))
                ctx.inc(ins, S["pe"])
                e.wait_ge(ctx.sem[S["x1"]], base["x1"] + 1)
                ins = e.matmul(out=ctx.ps[7][:], lhsT=cst["onrb"][:], rhs=mrow[:],
                               start=True, stop=True)
                ctx.inc(ins, S["pe"])
                if nw == 8:
                    e.wait_ge(ctx.sem[S["pf"]], base["pf"] + 4)
                    for j in range(4, 8):
                        ins = e.matmul(out=ctx.ps[j - 4][:],
                                       lhsT=kw[:, j * 128:(j + 1) * 128],
                                       rhs=qt[:], start=True, stop=True)
                    ctx.inc(ins, S["pe"])
                for j in range(nw):
                    e.wait_ge(ctx.sem[S["act"]], base["act"] + j + 1)
                    ins = e.matmul(out=ctx.ps[6][0:1, 0:TT], lhsT=cst["onecb"][:],
                                   rhs=au[:, j * TT:(j + 1) * TT],
                                   start=(j == 0), stop=(j == nw - 1))
                ctx.inc(ins, S["pe"])
                e.wait_ge(ctx.sem[S["x1"]], base["x1"] + 2)
                ins = e.matmul(out=ctx.ps[4][:], lhsT=cst["onrf"][:], rhs=rrow[:],
                               start=True, stop=True)
                ctx.inc(ins, S["pe"])
                for j in range(nw):
                    ins = e.matmul(out=ctx.ps[5][:], lhsT=vtm[:, j * 128:(j + 1) * 128],
                                   rhs=au[:, j * TT:(j + 1) * TT],
                                   start=(j == 0), stop=(j == nw - 1))
                ctx.inc(ins, S["pe"])

            @blk.vector
            def _(e):
                e.wait_ge(ctx.sem[S["pe"]], base["pe"] + PE_VT)
                if nw == 4:
                    ins = e.tensor_copy(out=vtm[:, 0:512], in_=ctx.ps[4][:])
                else:
                    e.tensor_copy(out=vtm[:, 0:512], in_=ctx.ps[4][:])
                    ins = e.tensor_copy(out=vtm[:, 512:1024], in_=ctx.ps[5][:])
                e.wait_ge(ctx.sem[S["dve"]], ctx.inc(ins, S["dve"]))
                for qb_ in range(4):
                    bks = (4, 5) if qb_ % 2 == 0 else (6, 7)
                    e.wait_ge(ctx.sem[S["pe"]], base["pe"] + PE_MAX(qb_))
                    if nhalf == 2:
                        ins = e.tensor_reduce(out=cred[:, 0:1], in_=ctx.ps[bks[0]][:],
                                              axis=AX.X, op=ALU.max)
                        e.wait_ge(ctx.sem[S["dve"]], ctx.inc(ins, S["dve"]))
                        ins = e.tensor_reduce(out=cred[:, 1:2], in_=ctx.ps[bks[1]][:],
                                              axis=AX.X, op=ALU.max)
                        e.wait_ge(ctx.sem[S["dve"]], ctx.inc(ins, S["dve"]))
                        ins = e.tensor_max(out=mxb[:, qb_:qb_ + 1],
                                           in0=cred[:, 0:1], in1=cred[:, 1:2])
                    else:
                        ins = e.tensor_reduce(out=mxb[:, qb_:qb_ + 1],
                                              in_=ctx.ps[bks[0]][:],
                                              axis=AX.X, op=ALU.max)
                        e.wait_ge(ctx.sem[S["dve"]], ctx.inc(ins, S["dve"]))
                        ctx.inc(e.engine_nop(), "dve")
                        ins = e.engine_nop()
                    e.wait_ge(ctx.sem[S["x0"]], ctx.inc(ins, S["x0"]))
                e.wait_ge(ctx.sem[S["pe"]], base["pe"] + PE_MROW)
                ins = e.tensor_copy(out=mrow[:], in_=ctx.ps[6][0:1, :])
                e.wait_ge(ctx.sem[S["x1"]], ctx.inc(ins, S["x1"]))
                e.wait_ge(ctx.sem[S["pe"]], base["pe"] + PE_MAXB)
                ins = e.tensor_copy(out=mxs[:], in_=ctx.ps[7][:])
                e.wait_ge(ctx.sem[S["dve"]], ctx.inc(ins, S["dve"]))
                for j in range(nw):
                    ins = e.tensor_tensor(out=pre2[:, j * TT:(j + 1) * TT],
                                          in0=mk[:, j * TT:(j + 1) * TT],
                                          in1=mxs[:], op=ALU.subtract)
                    e.wait_ge(ctx.sem[S["dve"]], ctx.inc(ins, S["dve"]))
                    e.wait_ge(ctx.sem[S["pe"]],
                              base["pe"] + (PE_R1 if j < 4 else PE_R2))
                    if j >= 2:
                        e.wait_ge(ctx.sem[S["act"]], base["act"] + j - 1)
                    ins = e.scalar_tensor_tensor(
                        out=tmp[:, (j % 2) * TT:(j % 2 + 1) * TT],
                        in0=ctx.ps[j % 4][:], scalar=ccol(ctx, 0.088388347648318447),
                        in1=pre2[:, j * TT:(j + 1) * TT],
                        op0=ALU.mult, op1=ALU.add)
                    ctx.inc(ins, S["pf"])
                e.wait_ge(ctx.sem[S["pe"]], base["pe"] + PE_SUM)
                ins = e.tensor_copy(out=srow[:], in_=ctx.ps[6][0:1, 0:TT])
                e.wait_ge(ctx.sem[S["dve"]], ctx.inc(ins, S["dve"]))
                ins = e.reciprocal(out=rrow[:], in_=srow[:])
                e.wait_ge(ctx.sem[S["x1"]], ctx.inc(ins, S["x1"]))
                e.wait_ge(ctx.sem[S["pe"]], base["pe"] + PE_RECB)
                ins = e.tensor_copy(out=rbs[:], in_=ctx.ps[4][:])
                e.wait_ge(ctx.sem[S["dve"]], ctx.inc(ins, S["dve"]))
                e.wait_ge(ctx.sem[S["pe"]], base["pe"] + PE_AV)
                e.wait_ge(ctx.sem[S["st"]], base["st"])
                ins = e.tensor_tensor(out=ob[:], in0=ctx.ps[5][:], in1=rbs[:],
                                      op=ALU.mult)
                ctx.inc(ins, S["x1"])


            @blk.scalar
            def _(e):
                e.wait_ge(ctx.sem[S["pe"]], base["pe"])
                for j in range(nw):
                    e.wait_ge(ctx.sem[S["pf"]], base["pf"] + j + 1)
                    ins = e.activation(out=au[:, j * TT:(j + 1) * TT],
                                       in_=tmp[:, (j % 2) * TT:(j % 2 + 1) * TT],
                                       func=AF.Exp)
                    ctx.inc(ins, S["act"])


def p_attn(ctx):
    cst = {
        "idb": ctx.psb("idb", 128, 128, BF16),
        "mkg": ctx.psb("mkg", 128, 8 * TT, BF16),
        "mk0": ctx.psb("mk0", 128, 4 * TT, BF16),
        "onrb": ctx.psb("onrb", 1, 128, BF16),
        "onrf": ctx.psb("onrf", 1, 128, F32),
        "onecb": ctx.psb("onecb", 128, 1, BF16),
    }
    with ctx.nc.Block("atpro") as blk:
        ctx.blk = blk

        @blk.sync
        def _(e):
            d = e.dma_start(out=cst["idb"][:], in_=ctx.ap("identb")[:, :])
            ctx.inc(d, "q0", 16)
            for j in range(8):
                d = e.dma_start(out=cst["mkg"][:, j * TT:(j + 1) * TT],
                                in_=ctx.ap("maskg")[128 * j:128 * (j + 1), :])
                ctx.inc(d, "q0", 16)
            for j in range(4):
                d = e.dma_start(out=cst["mk0"][:, j * TT:(j + 1) * TT],
                                in_=ctx.ap("mask0")[128 * j:128 * (j + 1), :])
                ctx.inc(d, "q0", 16)
            e.wait_ge(ctx.sem["q0"], ctx.c["q0"])

        @blk.vector
        def _(e):
            e.memset(cst["onrb"][:], 1.0)
            e.memset(cst["onrf"][:], 1.0)
            e.memset(cst["onecb"][:], 1.0)

    SEMS = [
        dict(pe="pe", act="act", dve="dve", pf="pf", x0="x0", x1="x1",
             st="sta", ox1="x12"),
        dict(pe="pe2", act="act2", dve="dve2", pf="pf2", x0="x02", x1="x12",
             st="stb", ox1="x1"),
    ]
    with ExitStack() as les:
        bufs = []
        for par in range(2):
            b = dict(
                qt=ctx.sb(les, "qt", 128, TT, BF16),
                kw=ctx.sb(les, "kw", 128, 8 * 128, BF16),
                vw=ctx.sb(les, "vw", 128, 8 * 128, BF16),
                vtm=ctx.sb(les, "vtm", 128, 8 * 128, BF16),
                pre2=ctx.sb(les, "pre2", 128, 8 * TT, F32),
                tmp=ctx.sb(les, "tmp", 128, 2 * TT, F32),
                au=ctx.sb(les, "au", 128, 8 * TT, BF16),
                mxb=ctx.sb(les, "mxb", 128, 4, BF16),
                cred=ctx.sb(les, "cred", 128, 2, BF16),
                mrow=ctx.sb(les, "mrow", 1, TT, BF16),
                srow=ctx.sb(les, "srow", 1, TT, F32),
                rrow=ctx.sb(les, "rrow", 1, TT, F32),
                mxs=ctx.sb(les, "mxs", 128, TT, F32),
                rbs=ctx.sb(les, "rbs", 128, TT, F32),
                ob=ctx.sb(les, "ob", 128, TT, BF16),
                ld=ctx.stream(f"aq{par}", 4),
            )
            bufs.append(b)
        for ci in range(8):
            with ctx.nc.Block(f"atc{ci}") as blk:
                ctx.blk = blk
                for h in range(16):
                    _attn_emit(ctx, h, ci, cst, bufs[h % 2], SEMS[h % 2])

                @blk.sync
                def _(e):
                    e.wait_ge(ctx.sem["sta"], ctx.c["sta"])
                    e.wait_ge(ctx.sem["stb"], ctx.c["stb"])


def p_wout(ctx):
    with ExitStack() as les:
        rhs = ctx.sb(les, "aores", 128, 16 * T, BF16)
        rw = rhs_load_block(ctx, "ldao", "ao_s", 16, rhs)
        generic_linear(ctx, "wout", "Wo1", 16, 16, rhs, rw,
                       lambda m, t: ctx.ap("att1_s")[128 * m:128 * m + 128,
                                                     t * TT:(t + 1) * TT])


def p_ln1a(ctx):
    p_ln(ctx, "ln1a", "x2_s", "att1_s", "x3_s")



def p_l1fa(ctx):
    modc = ctx.psb("modc", 128, 64, F32)
    ampbc = ctx.psb("ampbc", 128, 16, F32)
    phbc = ctx.psb("phbc", 128, 16, F32)
    pbc = ctx.psb("pbc", 128, 64, F32)
    with ctx.nc.Block("mod") as blk:
        ctx.blk = blk
        with ExitStack() as les:
            ph = ctx.sb(les, "phx", 128, 64, F32)
            t1 = ctx.sb(les, "mt1", 128, 64, F32)
            ti = ctx.sb(les, "mti", 128, 64, I32)
            sr = ctx.sb(les, "msr", 128, 64, F32)

            @blk.sync
            def _(e):
                load_cols(ctx, e, "phs0", ph[:, 0:32])
                load_cols(ctx, e, "phs1", ph[:, 32:64])
                load_cols(ctx, e, "ampb", ampbc)
                load_cols(ctx, e, "phb", phbc)
                load_cols(ctx, e, "pathb0", pbc[:, 0:32])
                load_cols(ctx, e, "pathb1", pbc[:, 32:64])
            nq = ctx.c["q0"]

            @blk.vector
            def _(e):
                e.wait_ge(ctx.sem["q0"], nq)
                ins = e.tensor_scalar_mul(out=t1[:], in0=ph[:],
                                          scalar1=float(1.0 / (2 * np.pi)))
                e.wait_ge(ctx.sem["dve"], ctx.inc(ins, "dve"))
                ins = e.tensor_scalar_add(out=t1[:], in0=t1[:], scalar1=12582912.0)
                e.wait_ge(ctx.sem["dve"], ctx.inc(ins, "dve"))
                ins = e.tensor_scalar_add(out=t1[:], in0=t1[:], scalar1=-12582912.0)
                e.wait_ge(ctx.sem["dve"], ctx.inc(ins, "dve"))
                ins = e.tensor_scalar_mul(out=t1[:], in0=t1[:],
                                          scalar1=float(-2.0 * np.pi))
                e.wait_ge(ctx.sem["dve"], ctx.inc(ins, "dve"))
                ins = e.tensor_add(out=t1[:], in0=t1[:], in1=ph[:])
                ctx.inc(ins, "x0")

            @blk.scalar
            def _(e):
                e.wait_ge(ctx.sem["x0"], ctx.c["x0"])
                ins = e.activation(out=sr[:], in_=t1[:], func=AF.Abs)
                e.wait_ge(ctx.sem["act"], ctx.inc(ins, "act"))
                ins = e.activation(out=sr[:], in_=sr[:], func=AF.Sin, scale=-1.0,
                                   bias=ccol(ctx, np.pi / 2))
                e.wait_ge(ctx.sem["act"], ctx.inc(ins, "act"))
                ins = e.activation(out=t1[:], in_=t1[:], func=AF.Sin)
                ctx.inc(ins, "act")
            nact_mod = ctx.c["act"]

            @blk.vector
            def _(e):
                e.wait_ge(ctx.sem["act"], nact_mod)
                ins = e.tensor_add(out=modc[:], in0=t1[:], in1=sr[:])
                ctx.inc(ins, "dve")

    with ExitStack() as les:
        rhs = ctx.sb(les, "x3res", 128, 16 * T, BF16)
        rw = rhs_load_block(ctx, "ldx3", "x3_s", 16, rhs)

        def epi_amp(e, m, t, ps, buf):
            return e.activation(out=buf, in_=ps[:], func=AF.Sigmoid,
                                bias=ampbc[:, m:m + 1], scale=ccol(ctx, 1.0))

        generic_linear(ctx, "amp", "Wamp", 16, 16, rhs, rw,
                       lambda m, t: ctx.ap("amp_s")[128 * m:128 * m + 128,
                                                    t * TT:(t + 1) * TT],
                       epi=epi_amp, epi_eng="scalar")

        with ctx.nc.Block("phc") as blk:
            ctx.blk = blk
            with ExitStack() as les2:
                wbuf = ctx.sb(les2, "wb", 128, 2 * 16 * 128, BF16)
                th = ctx.sb(les2, "th", 128, 2 * TT, F32)
                sn = ctx.sb(les2, "sn", 128, 2 * TT, F32)
                ab = ctx.sb(les2, "abq", 128, 2 * TT, F32)
                cs_ = ctx.sb(les2, "csq", 128, 2 * TT, F32)
                amt = ctx.sb(les2, "amt", 128, 4 * TT, BF16)
                sm = ctx.sb(les2, "sm", 128, 2 * TT, F32)
                ob = ctx.sb(les2, "ob", 128, 4 * TT, BF16)
                ld = ctx.stream("ld", 4)
                base_ld = ld.base()
                w_ready = slab_loader(ctx, "Wph", 16, 16, wbuf)
                base_act = ctx.c["act"]
                base_dve = ctx.c["dve"]
                base_pe = mm_section(ctx, 16, 16, wbuf, rhs, w_ready, rw,
                                     lambda j: ("act", base_act + 4 * j + 1))
                stq = ctx.stream("st", 4)
                base_st = stq.base()
                base_x0 = ctx.c["x0"]

                @blk.sync
                def _(e):
                    for j in range(16 * NT):
                        if j >= 4:
                            e.wait_ge(ctx.sem["dve"], base_dve + 3 * (j - 3))
                        ld.put(e, amt[:, (j % 4) * TT:(j % 4 + 1) * TT],
                               ctx.ap("amp_s")[128 * (j // NT):128 * (j // NT) + 128,
                                               (j % NT) * TT:(j % NT + 1) * TT],
                               j % 4)
                        if j >= 2:
                            e.wait_ge(ctx.sem["dve"], base_dve + 3 * (j - 1))
                            stq.put(e, ctx.ap("h0_s")[
                                128 * ((j - 2) // NT):128 * ((j - 2) // NT) + 128,
                                ((j - 2) % NT) * TT:((j - 2) % NT + 1) * TT],
                                ob[:, ((j - 2) % 4) * TT:((j - 2) % 4 + 1) * TT],
                                (j - 2) % 4)
                    for j in (16 * NT - 2, 16 * NT - 1):
                        e.wait_ge(ctx.sem["dve"], base_dve + 3 * (j + 1))
                        stq.put(e, ctx.ap("h0_s")[
                            128 * (j // NT):128 * (j // NT) + 128,
                            (j % NT) * TT:(j % NT + 1) * TT],
                            ob[:, (j % 4) * TT:(j % 4 + 1) * TT], j % 4)
                    for sl in range(4):
                        e.wait_ge(stq.sems[sl], stq.cnt[sl])

                @blk.scalar
                def _(e):
                    for j in range(16 * NT):
                        m = j // NT
                        e.wait_ge(ctx.sem["pe"], base_pe + j + 1)
                        if j >= 2:
                            e.wait_ge(ctx.sem["dve"], base_dve + 3 * (j - 1))
                        tv = th[:, (j % 2) * TT:(j % 2 + 1) * TT]
                        ins = e.activation(out=tv, in_=ctx.ps[j % 4][:], func=AF.Tanh,
                                           bias=phbc[:, m:m + 1], scale=ccol(ctx, 1.0))
                        e.wait_ge(ctx.sem["act"], ctx.inc(ins, "act"))
                        ins = e.activation(out=ab[:, (j % 2) * TT:(j % 2 + 1) * TT],
                                           in_=tv, func=AF.Abs)
                        e.wait_ge(ctx.sem["act"], ctx.inc(ins, "act"))
                        ins = e.activation(out=sn[:, (j % 2) * TT:(j % 2 + 1) * TT],
                                           in_=tv, func=AF.Sin, scale=float(np.pi))
                        ctx.inc(ins, "act")
                        ins = e.activation(out=cs_[:, (j % 2) * TT:(j % 2 + 1) * TT],
                                           in_=ab[:, (j % 2) * TT:(j % 2 + 1) * TT],
                                           func=AF.Sin, scale=float(-np.pi),
                                           bias=ccol(ctx, np.pi / 2))
                        ctx.inc(ins, "act")

                @blk.vector
                def _(e):
                    for j in range(16 * NT):
                        e.wait_ge(ctx.sem["act"], base_act + 4 * (j + 1))
                        if j >= 2:
                            e.wait_ge(ctx.sem["dve"], base_dve + 3 * (j - 1))
                        ins = e.tensor_add(out=sm[:, (j % 2) * TT:(j % 2 + 1) * TT],
                                           in0=sn[:, (j % 2) * TT:(j % 2 + 1) * TT],
                                           in1=cs_[:, (j % 2) * TT:(j % 2 + 1) * TT])
                        e.wait_ge(ctx.sem["dve"], ctx.inc(ins, "dve"))
                        ctx.inc(e.engine_nop(), "dve")
                        ld.wait(e, (j % 4, base_ld[j % 4] + 16 * (j // 4 + 1)))
                        if j >= 4:
                            stq.wait(e, st_tok(ctx, base_st, j - 4))
                        ins = e.tensor_tensor(out=ob[:, (j % 4) * TT:(j % 4 + 1) * TT],
                                              in0=sm[:, (j % 2) * TT:(j % 2 + 1) * TT],
                                              in1=amt[:, (j % 4) * TT:(j % 4 + 1) * TT],
                                              op=ALU.mult)
                        ctx.inc(ins, "dve")


def p_ent(ctx, l):
    name = f"ent{l}"
    hsrc = "h0_s" if l == 0 else "hB0_s"
    ebc = ctx.psb(f"entbc{l}", 128, 64, F32)
    with ctx.nc.Block(name + "c") as blk:
        ctx.blk = blk

        @blk.sync
        def _(e):
            load_cols(ctx, e, f"entb{l}", ebc)
            e.wait_ge(ctx.sem["q0"], ctx.c["q0"])
    with ExitStack() as les:
        rhs = ctx.sb(les, "hres", 128, 16 * T, BF16)
        rw = rhs_load_block(ctx, "ldh" + name, hsrc, 16, rhs)

        def epi(e, m, t, ps, buf):
            return e.tensor_scalar_add(out=buf, in0=ps[:], scalar1=ebc[:, m:m + 1])

        generic_linear(ctx, name, f"Went{l}", 16, 64, rhs, rw,
                       lambda m, t: ctx.ap(f"e{l}_s")[128 * m:128 * m + 128,
                                                      t * TT:(t + 1) * TT], epi=epi)


def p_entp(ctx, l):
    ebc = ctx.psb(f"entpbc{l}", 128, 16, F32)
    with ctx.nc.Block(f"entp{l}c") as blk:
        ctx.blk = blk

        @blk.sync
        def _(e):
            load_cols(ctx, e, f"entpb{l}", ebc)
            e.wait_ge(ctx.sem["q0"], ctx.c["q0"])
    with ExitStack() as les:
        rhs = ctx.sb(les, "eres", 128, 64 * 1024, BF16)
        for tb in range(4):
            rw = rhs_load_block(ctx, f"lde{l}_{tb}", f"e{l}_s", 64, rhs,
                                c0=tb * 1024, ncols=1024)

            def epi(e, m, t, ps, buf):
                return e.tensor_scalar_add(out=buf, in0=ps[:], scalar1=ebc[:, m:m + 1])

            def dst_of(m, t, _tb=tb):
                return ctx.ap(f"s3{l}_s")[128 * m:128 * m + 128,
                                          _tb * 1024 + t * TT:_tb * 1024 + (t + 1) * TT]

            generic_linear(ctx, f"entp{l}_{tb}", f"Wentp{l}", 64, 16, rhs, rw,
                           dst_of, epi=epi, ntt=2, rhs_stride=1024)


def p_path(ctx, l):
    hsrc = f"hA{l}_s"
    modc = ctx.keep["modc"]
    pbc = ctx.keep["pbc"]
    nk, nm = 16, 16
    with ExitStack() as les0:
        rhs = ctx.sb(les0, "hres2", 128, 16 * T, BF16)
        rw = rhs_load_block(ctx, f"ldhA{l}", hsrc, 16, rhs)
        with ctx.nc.Block(f"path{l}") as blk:
            ctx.blk = blk
            with ExitStack() as les:
                wb0 = ctx.sb(les, "wb0", 128, 2 * nk * 128, BF16)
                wb1 = ctx.sb(les, "wb1", 128, 2 * nk * 128, BF16)
                e0t = ctx.sb(les, "e0t", 128, 2 * TT, F32)
                e1t = ctx.sb(les, "e1t", 128, 2 * TT, F32)
                ob = ctx.sb(les, "ob", 128, 4 * TT, BF16)
                wt = ctx.stream("wt", 2)
                wt2 = ctx.stream("wt2", 2)
                bw = wt.base()
                bw2 = wt2.base()
                base_pe = ctx.c["pe"]
                base_dve = ctx.c["dve"]
                stq = ctx.stream("st", 4)
                base_st = stq.base()
                W0 = ctx.ap(f"Wpath{l}0")
                W1 = ctx.ap(f"Wpath{l}1")

                @blk.gpsimd
                def _(e):
                    for m in range(nm):
                        if m >= 2:
                            e.wait_ge(ctx.sem["pe"], base_pe + (m - 1) * NT)
                        wt.put(e, wb0[:, (m % 2) * nk * 128:(m % 2 + 1) * nk * 128]
                               .rearrange("p (k c) -> p k c", k=nk),
                               W0[m].rearrange("k p c -> p k c"), m % 2)
                        wt2.put(e, wb1[:, (m % 2) * nk * 128:(m % 2 + 1) * nk * 128]
                                .rearrange("p (k c) -> p k c", k=nk),
                                W1[m].rearrange("k p c -> p k c"), m % 2)

                @blk.tensor
                def _(e):
                    e.wait_ge(ctx.sem[rw[0]], rw[1])
                    for m in range(nm):
                        e.wait_ge(wt.sems[m % 2], bw[m % 2] + 16 * (m // 2 + 1))
                        e.wait_ge(wt2.sems[m % 2], bw2[m % 2] + 16 * (m // 2 + 1))
                        for t in range(NT):
                            j = m * NT + t
                            if j >= 2:
                                e.wait_ge(ctx.sem["dve"], base_dve + 4 * (j - 1))
                            o1 = (m % 2) * nk * 128
                            b0 = ctx.ps[(j % 2) * 2]
                            b1 = ctx.ps[(j % 2) * 2 + 1]
                            for k in range(nk):
                                e.matmul(out=b0[:], lhsT=wb0[:, o1 + k * 128:o1 + (k + 1) * 128],
                                         rhs=rhs[:, k * T + t * TT:k * T + t * TT + TT],
                                         start=(k == 0), stop=(k == nk - 1))
                            for k in range(nk):
                                ins = e.matmul(out=b1[:], lhsT=wb1[:, o1 + k * 128:o1 + (k + 1) * 128],
                                               rhs=rhs[:, k * T + t * TT:k * T + t * TT + TT],
                                               start=(k == 0), stop=(k == nk - 1))
                            ctx.inc(ins, "pe")

                @blk.sync
                def _(e):
                    for j in range(nm * NT):
                        e.wait_ge(ctx.sem["dve"], base_dve + 4 * (j + 1))
                        stq.put(e, ctx.ap(f"s4{l}_s")[
                            128 * (j // NT):128 * (j // NT) + 128,
                            (j % NT) * TT:(j % NT + 1) * TT],
                            ob[:, (j % 4) * TT:(j % 4 + 1) * TT], j % 4)
                    for sl in range(4):
                        e.wait_ge(stq.sems[sl], stq.cnt[sl])

                @blk.vector
                def _(e):
                    for j in range(nm * NT):
                        m = j // NT
                        e.wait_ge(ctx.sem["pe"], base_pe + j + 1)
                        ins = e.tensor_scalar(out=e0t[:, (j % 2) * TT:(j % 2 + 1) * TT],
                                              in0=ctx.ps[(j % 2) * 2][:],
                                              scalar1=pbc[:, l * 32 + m:l * 32 + m + 1],
                                              scalar2=modc[:, l * 32 + m:l * 32 + m + 1],
                                              op0=ALU.add, op1=ALU.mult)
                        e.wait_ge(ctx.sem["dve"], ctx.inc(ins, "dve"))
                        ins = e.tensor_scalar(out=e1t[:, (j % 2) * TT:(j % 2 + 1) * TT],
                                              in0=ctx.ps[(j % 2) * 2 + 1][:],
                                              scalar1=pbc[:, l * 32 + 16 + m:l * 32 + 17 + m],
                                              scalar2=modc[:, l * 32 + 16 + m:l * 32 + 17 + m],
                                              op0=ALU.add, op1=ALU.mult)
                        e.wait_ge(ctx.sem["dve"], ctx.inc(ins, "dve"))
                        ins = e.tensor_add(out=e0t[:, (j % 2) * TT:(j % 2 + 1) * TT],
                                           in0=e0t[:, (j % 2) * TT:(j % 2 + 1) * TT],
                                           in1=e1t[:, (j % 2) * TT:(j % 2 + 1) * TT])
                        e.wait_ge(ctx.sem["dve"], ctx.inc(ins, "dve"))
                        if j >= 4:
                            stq.wait(e, st_tok(ctx, base_st, j - 4))
                        ins = e.tensor_copy(out=ob[:, (j % 4) * TT:(j % 4 + 1) * TT],
                                            in_=e0t[:, (j % 2) * TT:(j % 2 + 1) * TT])
                        ctx.inc(ins, "dve")


def p_ent0(ctx):
    p_ent(ctx, 0)


def p_entp0(ctx):
    p_entp(ctx, 0)


def p_ln30(ctx):
    p_ln(ctx, "ln30", "h0_s", "s30_s", "hA0_s", gname="ln3g0", bname="ln3b0")


def p_path0(ctx):
    p_path(ctx, 0)


def p_ln40(ctx):
    p_ln(ctx, "ln40", "hA0_s", "s40_s", "hB0_s", gname="ln4g0", bname="ln4b0")


def p_ent1(ctx):
    p_ent(ctx, 1)


def p_entp1(ctx):
    p_entp(ctx, 1)


def p_ln31(ctx):
    p_ln(ctx, "ln31", "hB0_s", "s31_s", "hA1_s", gname="ln3g1", bname="ln3b1")


def p_path1(ctx):
    p_path(ctx, 1)


def p_ln41(ctx):
    p_ln(ctx, "ln41", "hA1_s", "s41_s", "hB1_s", gname="ln4g1", bname="ln4b1")


def p_lnout(ctx):
    p_ln(ctx, "lnout", "x3_s", "hB1_s", "outT", out_f32=True)


PHASES = {}


def _reg(name, fn, uses, defs):
    PHASES[name] = (fn, uses, defs)


_reg("init", p_init, [], [])
_reg("l0in", p_l0in, ["xT", "Wz", "Wdtin", "Wdtp", "dtb"],
     ["z_s", "xs_s", "b_s", "c_s", "dt_s"])
_reg("conv", p_conv, ["xs_s", "convw", "convb"], ["xact_s"])
_reg("scan", p_scan, ["dt_s", "xact_s", "b_s", "c_s", "Alog"], ["yssm_s"])
_reg("gate", p_gate, ["yssm_s", "xact_s", "z_s", "Dv"], ["yg_s"])
_reg("outproj", p_outproj, ["yg_s", "Wout0"], ["att0_s"])
_reg("ln0a", p_ln0a, ["xT", "att0_s"], ["x1_s"])
_reg("ffnh", p_ffnh, ["x1_s", "W1", "W2"], ["hid_s"])
_reg("ffn3", p_ffn3, ["hid_s", "W3"], ["ff0_s", "w3q_s"])
_reg("ln0b", p_ln0b, ["x1_s", "ff0_s"], ["x2_s"])
_reg("qkv", p_qkv, ["x2_s", "Wqkv"], ["qkv_s"])
_reg("attn", p_attn, ["qkv_s", "identb", "maskg", "mask0"], ["ao_s"])
_reg("wout", p_wout, ["ao_s", "Wo1"], ["att1_s"])
_reg("ln1a", p_ln1a, ["x2_s", "att1_s"], ["x3_s"])
_reg("l1fa", p_l1fa, ["x3_s", "Wamp", "Wph", "ampb", "phb", "phs0", "phs1",
                      "pathb0", "pathb1"], ["amp_s", "h0_s"])
_reg("ent0", p_ent0, ["h0_s", "Went0", "entb0"], ["e0_s"])
_reg("entp0", p_entp0, ["e0_s", "Wentp0", "entpb0"], ["s30_s"])
_reg("ln30", p_ln30, ["h0_s", "s30_s", "ln3g0", "ln3b0"], ["hA0_s"])
_reg("path0", p_path0, ["hA0_s", "Wpath00", "Wpath01"], ["s40_s"])
_reg("ln40", p_ln40, ["hA0_s", "s40_s", "ln4g0", "ln4b0"], ["hB0_s"])
_reg("ent1", p_ent1, ["hB0_s", "Went1", "entb1"], ["e1_s"])
_reg("entp1", p_entp1, ["e1_s", "Wentp1", "entpb1"], ["s31_s"])
_reg("ln31", p_ln31, ["hB0_s", "s31_s", "ln3g1", "ln3b1"], ["hA1_s"])
_reg("path1", p_path1, ["hA1_s", "Wpath10", "Wpath11"], ["s41_s"])
_reg("ln41", p_ln41, ["hA1_s", "s41_s", "ln4g1", "ln4b1"], ["hB1_s"])
_reg("lnout", p_lnout, ["x3_s", "hB1_s"], ["outT"])

ORDER = ["init", "l0in", "conv", "scan", "gate", "outproj", "ln0a",
         "ffnh", "ffn3", "ln0b", "qkv", "attn", "wout", "ln1a",
         "l1fa", "ent0", "entp0", "ln30", "path0", "ln40",
         "ent1", "entp1", "ln31", "path1", "ln41", "lnout"]
GROUPS = [list(ORDER)]


def build_group(phases, host_specs, keep_all=False):
    nc = bass.Bass("TRN2", target_bir_lowering=False, debug=False)
    used, defined = set(), set()
    for p in phases:
        _, u, d = PHASES[p]
        used.update(n for n in u if n not in defined)
        defined.update(d)
    later_used = set()
    idx = max(ORDER.index(p) for p in phases)
    for g_later in ORDER[idx + 1:]:
        later_used.update(PHASES[g_later][1])
    ext_in = {n for n in used if n in SCRATCH}
    ext_out = {n for n in defined
               if (n == "outT" or n in later_used or keep_all)}
    with ExitStack() as es:
        ctx = Ctx(nc, es, ext_in, ext_out, host_specs)
        ctx.cc = es.enter_context(
            nc.sbuf_tensor("ccpool", [128, len(CONSTS)], F32, side="right"))
        ctx.ps = [es.enter_context(nc.psum_tensor(f"psb{i}", [128, 512], F32))
                  for i in range(8)]
        for p in phases:
            with nc.named_scope(p):
                PHASES[p][0](ctx)
    return nc, [n for n in ctx.dram if n not in SCRATCH or n in ext_in], \
        sorted(n for n in ext_out if n in ctx.dram)


def host_specs_of(host):
    hs = {k: (v.shape, BF16 if v.dtype == BF else F32) for k, v in host.items()}
    hs["xT"] = ((2048, T), F32)
    return hs


def kernel(**inputs):
    host = prep_host(inputs)
    hs = host_specs_of(host)
    xTs = [np.ascontiguousarray(inputs["L0_attn_hidden_states"][i].T.astype(np.float32))
           for i in range(NCORE)]
    vals = [dict() for _ in range(NCORE)]
    for g in GROUPS:
        nc, ext_in, ext_out = build_group(g, hs)
        in_maps = []
        for i in range(NCORE):
            m = {}
            for n in ext_in:
                m[n] = xTs[i] if n == "xT" else host.get(n, vals[i].get(n))
            in_maps.append(m)
        res = run_bass_kernel_spmd(nc, in_maps, core_ids=list(range(NCORE)))
        for i in range(NCORE):
            for n in ext_out:
                vals[i][n] = np.asarray(res.results[i][n])
    out = np.stack([np.ascontiguousarray(vals[i]["outT"].T).astype(np.float32)
                    for i in range(NCORE)])
    return out
